# version 26
# baseline (speedup 1.0000x reference)
import contextlib
import numpy as np
import ml_dtypes
import concourse.bass as bass
import concourse.mybir as mybir
from concourse.bass_utils import run_bass_kernel_spmd

F32 = mybir.dt.float32
BF16 = mybir.dt.bfloat16
AF = mybir.ActivationFunctionType
ALU = mybir.AluOpType

D = 1024
S = 2048
NSEQ = 2
NCH = 8
QT = 512
NT = S // QT
NHP = 8
NKT = S // 128
PBS = 3
SCALE = 1.0 / 65536.0
EPS = 1e-5

WK, WV, WQ, WO, FR, FK, FV = range(7)


class Cnt:
    def __init__(self, name):
        self.name = name
        self.n = 0


def build_nc():
    nc = bass.Bass("TRN2", target_bir_lowering=False)

    xT = nc.dram_tensor("xT", [NSEQ, D, S], F32, kind="ExternalInput")
    wts = nc.dram_tensor("wts", [7, D, D], F32, kind="ExternalInput")
    mus = nc.dram_tensor("mus", [128, 18], F32, kind="ExternalInput")
    cst = nc.dram_tensor("cst", [128, 416], F32, kind="ExternalInput")
    outT = nc.dram_tensor("outT", [NSEQ, D, S], F32, kind="ExternalOutput")
    wsc = nc.dram_tensor("wsc", [7 * D, D], BF16, kind="Internal")

    plan = {k: [] for k in "tavsg"}
    sems = {}
    sem_handles = {}

    def em(eng, fn):
        plan[eng].append(fn)

    def C(name):
        c = Cnt(name)
        sems[name] = c
        return c

    c_cstb = C("c_cstb")
    c_wqr = C("c_wqr")
    c_cst = C("c_cst")
    c_wc0 = C("c_wc0")
    c_wc1 = C("c_wc1")
    c_ws0 = C("c_ws0")
    c_ws1 = C("c_ws1")
    c_w0 = C("c_w0")
    c_w1 = C("c_w1")
    c_xf = C("c_xf")
    c_xb = C("c_xb")
    c_out = C("c_out")
    c_sc = C("c_sc")
    c_p = C("c_p")
    c_kt = C("c_kt")
    c_rc = C("c_rc")
    c_g = C("c_g")
    c_vg = C("c_vg")
    c_ao = C("c_ao")
    c_rb = C("c_rb")
    c_tf = C("c_tf")
    c_va = C("c_va")
    c_ar = C("c_ar")

    def wait(eng, c, val=None):
        v = c.n if val is None else val
        if v <= 0:
            return
        em(eng, lambda e, _n=c.name, _v=v: e.wait_ge(sem_handles[_n], _v))

    def bump(c, by=1):
        c.n += by
        return c.n

    def dma(eng, dst, src, c):
        n = bump(c, 16)
        em(eng, lambda e, _d=dst, _s=src, _n=c.name:
           e.dma_start(_d, _s).then_inc(sem_handles[_n], 16))
        return n

    def mm(out, lhsT, rhs, start, stop, tp=None, inc=None):
        n = bump(inc) if inc is not None else None
        def f(e, _o=out, _l=lhsT, _r=rhs, _st=start, _sp=stop, _tp=tp, _inc=inc):
            i = e.matmul(_o, _l, _r, start=_st, stop=_sp, tile_position=_tp,
                         skip_group_check=True)
            if _inc is not None:
                i.then_inc(sem_handles[_inc.name], 1)
        em("t", f)
        return n

    def act(out, in_, func, scale=1.0, bias=0.0, inc=None):
        n = bump(inc) if inc is not None else None
        def f(e, _o=out, _i=in_, _f=func, _s=scale, _b=bias, _inc=inc):
            i = e.activation(_o, _i, _f, scale=_s, bias=_b)
            if _inc is not None:
                i.then_inc(sem_handles[_inc.name], 1)
        em("a", f)
        return n

    def vop(name, args, inc=None):
        n = bump(inc) if inc is not None else None
        def f(e, _nm=name, _a=args, _inc=inc):
            i = getattr(e, _nm)(*_a)
            if _inc is not None:
                i.then_inc(sem_handles[_inc.name], 1)
        em("v", f)
        return n

    def tnop_inc(c):
        n = bump(c)
        em("t", lambda e, _n=c.name: e.nop().then_inc(sem_handles[_n], 1))
        return n

    with contextlib.ExitStack() as ctx:
        sbt = lambda name, shape, dt: ctx.enter_context(nc.sbuf_tensor(name, shape, dt))
        pst = lambda name, shape: ctx.enter_context(nc.psum_tensor(name, shape, F32))

        WS = [sbt("ws0", [128, 8 * D], BF16), sbt("ws1", [128, 8 * D], BF16),
              sbt("ws2", [128, 8 * D], BF16)]
        KTB = sbt("ktb", [128, NHP * S], BF16)
        VTB = sbt("vtb", [128, NKT * D], BF16)
        QTB = sbt("qtb", [128, NCH * QT], BF16)
        PB = sbt("pb", [128, PBS * 1024], BF16)
        ABA = sbt("aba", [128, NCH * QT], BF16)
        XKA = sbt("xka", [128, NCH * QT], BF16)
        XRA = sbt("xra", [128, NCH * QT], BF16)
        RG = sbt("rg", [128, NCH * QT], BF16)
        OUTB = sbt("outb", [128, NCH * QT], F32)
        Y1 = sbt("y1", [128, NCH * 516], F32)
        STV = sbt("stv", [128, QT], F32)
        LTA = sbt("lta", [128, 1024], F32)
        RSB = sbt("rsb", [128, QT], F32)
        RECB = sbt("recb", [128, QT], F32)
        RCR = sbt("rcr", [64, QT], F32)
        SR = sbt("sr", [64, QT], F32)
        SR2 = sbt("sr2", [64, QT], F32)
        CSTF = sbt("cstf", [64, 384], F32)
        CSTB = sbt("cstb", [128, 32], BF16)
        MUS = sbt("musb", [128, 18], F32)

        SAB = pst("psab", [128, 2048])
        PV = pst("pspv", [128, 512])
        DEN = pst("psden", [128, 512])
        G0 = pst("psg0", [128, 512])
        G1 = pst("psg1", [128, 512])

        for name in sems:
            sem_handles[name] = ctx.enter_context(nc.semaphore(name))

        class Bank:
            def __init__(self, ap):
                self.ap = ap
                self.free = None
        bG = [Bank(G0), Bank(G1)]
        bPV = Bank(PV)
        bDEN = Bank(DEN)
        pool4 = [bG[0], bG[1], bPV, bDEN]

        def acquire(b):
            if b.free is not None:
                wait("t", b.free[0], b.free[1])
                b.free = None

        gidx = [0]
        hidx = [0]
        use4 = [False]

        def next_bank():
            if use4[0]:
                b = pool4[hidx[0] % 4]
                hidx[0] += 1
            else:
                b = bG[gidx[0] % 2]
                gidx[0] += 1
            acquire(b)
            return b

        def proj_group(ws_slot, oc, rhs_chunks, evac, w_ready=None):
            b = next_bank()
            if w_ready is not None:
                wait("t", w_ready[0], w_ready[1])
            for k in range(8):
                lhsT = ws_slot[:, k * D + oc * 128: k * D + (oc + 1) * 128]
                mm(b.ap[:, :], lhsT, rhs_chunks[k], start=(k == 0), stop=(k == 7),
                   inc=(c_g if k == 7 else None))
            evac(b)
            return c_g.n

        def proj_group_nb(ws_slot, oc, rhs_chunks, w_ready=None):
            b = next_bank()
            if w_ready is not None:
                wait("t", w_ready[0], w_ready[1])
            for k in range(8):
                lhsT = ws_slot[:, k * D + oc * 128: k * D + (oc + 1) * 128]
                mm(b.ap[:, :], lhsT, rhs_chunks[k], start=(k == 0), stop=(k == 7),
                   inc=(c_g if k == 7 else None))
            return b

        def vnop():
            em("v", lambda e: e.engine_nop())

        def layer_norm(YBF, YSQ, Ybuf, dst_y1, ycast_n, free_aba_after_stats,
                       ycast_sem=None):
            if gidx[0] % 2 == 1:
                gidx[0] += 1
            b0 = next_bank()
            wait("t", ycast_sem if ycast_sem is not None else c_vg, ycast_n)
            for k in range(8):
                mm(b0.ap[0:32, :], CSTB[:, 0:32], YBF[:, k * QT:(k + 1) * QT],
                   k == 0, k == 7, tp=(0, 0))
                mm(b0.ap[32:64, :], CSTB[:, 0:32], YSQ[:, k * QT:(k + 1) * QT],
                   k == 0, k == 7, tp=(0, 32), inc=(c_g if k == 7 else None))
            nstats = c_g.n
            wait("v", c_g, nstats)
            vop("tensor_scalar", (SR[0:32, :], b0.ap[0:32, :], 1.0 / 1024.0, None,
                                  ALU.mult))
            ntsv = vop("tensor_scalar", (SR[32:64, :], b0.ap[32:64, :], 1.0 / 1024.0,
                                         None, ALU.mult), inc=c_va)
            wait("v", c_va, ntsv)
            vop("tensor_tensor", (SR2[0:32, :], SR[0:32, :], SR[0:32, :], ALU.mult))
            nsr = vop("tensor_copy", (SR2[32:64, :], SR[32:64, :]), inc=c_va)
            b0.free = (c_va, nsr)
            b1 = next_bank()
            wait("t", c_va, nsr)
            nvar = mm(b1.ap[:, :], CSTF[0:64, 128:256], SR2[0:64, :], True, True,
                      inc=c_g)
            b0b = next_bank()
            nmean = mm(b0b.ap[:, :], CSTF[0:64, 0:128], SR[0:64, :], True, True,
                       inc=c_g)
            wait("a", c_g, nvar)
            achain_start()
            if last_nap[0]:
                wait("a", c_va, last_nap[0])
            nln = act(LTA[:, 0:512], b1.ap[:, :], AF.Ln, bias=MUS[:, 16:17],
                      inc=c_ar)
            wait("a", c_ar, nln)
            nrs = act(RSB[:, :], LTA[:, 0:512], AF.Exp, scale=-0.5, inc=c_ar)
            last_achain[0] = (c_ar, nrs)
            b1.free = (c_ar, nrs)
            wait("v", c_g, nmean)
            wait("v", c_ar, nrs)
            nap = 0
            for op in range(0, 8, 2):
                if op:
                    wait("v", c_va, c_va.n)
                for j, scr in ((op, STV), (op + 1, RECB)):
                    np1 = vop("scalar_tensor_tensor",
                              (scr[:, :], b0b.ap[:, :], -1.0,
                               Ybuf[:, j * QT:(j + 1) * QT],
                               ALU.mult, ALU.add),
                              inc=(c_va if j == op + 1 else None))
                wait("v", c_va, np1)
                for j, scr in ((op, STV), (op + 1, RECB)):
                    dst = (dst_y1[:, j * 516 + 4: j * 516 + 4 + QT]
                           if dst_y1 is not None else Ybuf[:, j * QT:(j + 1) * QT])
                    nap = vop("tensor_tensor", (dst, scr[:, :], RSB[:, :], ALU.mult),
                              inc=c_va)
            b0b.free = (c_va, nap)
            last_nap[0] = nap
            return nap, nstats

        n_y1_init = vop("memset", (Y1[:, :], 0.0), inc=c_va)
        dma("g", CSTB[:, :], cst[:, 0:32], c_cstb)
        n_wq_res = dma("g", bass.AP(WS[2], 0, [[8 * D, 128], [D, 8], [1, D]]),
                       bass.AP(wts, WQ * D * D, [[D, 128], [128 * D, 8], [1, D]]),
                       c_wqr)
        dma("s", CSTF[:, :], cst[0:64, 32:416], c_cst)
        dma("s", MUS[:, :], mus[:, :], c_cst)
        wait("t", c_cstb, 16)
        wait("t", c_cst, 32)
        wait("v", c_cst, 32)

        c_wc = [c_wc0, c_wc1]
        c_wsx = [c_ws0, c_ws1]
        c_w = [c_w0, c_w1]
        for m in range(7):
            slot = WS[m % 2]
            if m >= 2:
                wait("g", c_wsx[m % 2], c_wsx[m % 2].n)
            nld = dma("g", bass.AP(slot, 0, [[8 * D, 128], [D, 8], [1, D]]),
                      bass.AP(wts, m * D * D, [[D, 128], [128 * D, 8], [1, D]]),
                      c_wc[m % 2])
            wait("s", c_wc[m % 2], nld)
            dma("s", bass.AP(wsc, m * D * D, [[D, 128], [128 * D, 8], [1, D]]),
                bass.AP(slot, 0, [[8 * D, 128], [D, 8], [1, D]]), c_wsx[m % 2])

        wnext = [0]
        scratch_all = [(c_ws0, c_ws0.n), (c_ws1, c_ws1.n)]
        slot_free = [None, None]
        first_load = [True]

        def load_weight(mat):
            si = wnext[0]
            wnext[0] = 1 - si
            if first_load[0]:
                wait("s", scratch_all[0][0], scratch_all[0][1])
                wait("s", scratch_all[1][0], scratch_all[1][1])
                first_load[0] = False
            fa = slot_free[si]
            if fa is not None:
                wait("s", fa[0], fa[1])
            n = dma("s", bass.AP(WS[si], 0, [[8 * D, 128], [D, 8], [1, D]]),
                    bass.AP(wsc, mat * D * D, [[D, 128], [128 * D, 8], [1, D]]),
                    c_w[si])
            return si, n

        exp_hist = {}
        den_hist = {}
        last_achain = [None]
        last_nrb = [0]
        last_nap = [0]

        def achain_start():
            if last_achain[0] is not None:
                wait("a", last_achain[0][0], last_achain[0][1])
        gkt = [0]
        xb_free = [0]
        out_done = [0]
        prev = None
        last_ycast = [0]

        for seq in range(NSEQ):
            def emit_kv(kv_fillers):
                use4[0] = True
                wk_si, wk_n = load_weight(WK)
                wv_si, wv_n = load_weight(WV)
                wk_last = wv_last = 0
                wait("t", c_w[wk_si], wk_n)
                wait("t", c_w[wv_si], wv_n)
                for tb in (1, 2, 3, 0):
                    wait("g", c_g, xb_free[0])
                    nxb = dma("g",
                              bass.AP(ABA, 0, [[NCH * QT, 128], [QT, 8], [1, QT]]),
                              bass.AP(xT, seq * D * S + tb * QT,
                                      [[S, 128], [128 * S, 8], [1, QT]]), c_xb)
                    if tb == 0:
                        kv_nxb0[0] = nxb
                    wait("t", c_xb, nxb)
                    xbch = [ABA[:, k * QT:(k + 1) * QT] for k in range(8)]
                    for hp in range(NHP):
                        b = next_bank()
                        for k in range(8):
                            mm(b.ap[:, :],
                               WS[wk_si][:, k * D + hp * 128: k * D + (hp + 1) * 128],
                               xbch[k], start=(k == 0), stop=(k == 7),
                               inc=(c_g if k == 7 else None))
                        wk_last = c_g.n
                        wait("v", c_g)
                        nv = vop("tensor_copy",
                                 (KTB[:, hp * S + tb * QT: hp * S + (tb + 1) * QT],
                                  b.ap[:, :]), inc=c_vg)
                        b.free = (c_vg, nv)
                    if kv_fillers:
                        kv_fillers.pop(0)()
                    for tt in range(4):
                        for half in range(2):
                            b = next_bank()
                            for k in range(8):
                                mm(b.ap[:, :], xbch[k][:, tt * 128:(tt + 1) * 128],
                                   WS[wv_si][:, k * D + half * 512:
                                             k * D + half * 512 + 512],
                                   start=(k == 0), stop=(k == 7),
                                   inc=(c_g if k == 7 else None))
                            wv_last = c_g.n
                            kti = tb * 4 + tt
                            wait("v", c_g)
                            nv = vop("tensor_copy",
                                     (VTB[:, kti * D + half * 512:
                                          kti * D + half * 512 + 512],
                                      b.ap[:, :]), inc=c_vg)
                            b.free = (c_vg, nv)
                    if kv_fillers:
                        kv_fillers.pop(0)()
                    xb_free[0] = wv_last
                slot_free[wk_si] = (c_g, wk_last)
                slot_free[wv_si] = (c_g, wv_last)
                while kv_fillers:
                    kv_fillers.pop(0)()
                use4[0] = False

            kvf = []
            kv_nxb0 = [0]
            xra_nxb = [0]
            if prev is not None:
                emit_wo_pairs(prev)
                last_ycast[0] = prev["yadds"]
                xb_free[0] = prev["aba_free"]
                kvf = [lambda: emit_ln1_head(prev),
                       lambda: (emit_ln_bcast_apply(prev, Y1)[0], emit_mix(prev))]
            emit_kv(kvf)

            wait("v", c_va, c_va.n)
            for cch in range(NCH):
                vop("memset", (Y1[:, cch * 516 + 3: cch * 516 + 4], 0.0))

            def emit_wo_pairs(u):
                use4[0] = True
                aoch = [ABA[:, k * QT:(k + 1) * QT] for k in range(8)]
                wait("t", c_ao, u["ao_done"])
                wo_last = 0
                nv = 0
                for op in range(0, 8, 2):
                    be = proj_group_nb(WS[u["wo_si"]], op, aoch,
                                       (c_w[u["wo_si"]], u["wo_n"]) if op == 0
                                       else None)
                    bo = proj_group_nb(WS[u["wo_si"]], op + 1, aoch)
                    wo_last = c_g.n
                    wait("v", c_g, wo_last)
                    if op == 0:
                        wait("v", c_xf, u["nxf"])
                        if out_done[0]:
                            wait("v", c_out, out_done[0])
                    for j, b_ in ((op, be), (op + 1, bo)):
                        nv = vop("tensor_tensor",
                                 (OUTB[:, j * QT:(j + 1) * QT], b_.ap[:, :],
                                  Y1[:, j * 516 + 4: j * 516 + 4 + QT], ALU.add),
                                 inc=(c_vg if j == op + 1 else None))
                    be.free = bo.free = (c_vg, nv)
                slot_free[u["wo_si"]] = (c_g, wo_last)
                u["aba_free"] = wo_last
                u["yadds"] = nv
                wait("v", c_vg, c_vg.n)
                ycast = 0
                for op in range(0, 8, 2):
                    if op:
                        wait("v", c_va, c_va.n)
                    for j in (op, op + 1):
                        nv = vop("tensor_tensor",
                                 (RG[:, j * QT:(j + 1) * QT],
                                  OUTB[:, j * QT:(j + 1) * QT],
                                  OUTB[:, j * QT:(j + 1) * QT], ALU.max),
                                 inc=(c_va if j == op + 1 else None))
                    wait("v", c_va, nv)
                    for j in (op, op + 1):
                        nv = vop("tensor_tensor",
                                 (XKA[:, j * QT:(j + 1) * QT],
                                  RG[:, j * QT:(j + 1) * QT],
                                  RG[:, j * QT:(j + 1) * QT], ALU.mult),
                                 inc=(c_va if j == op + 1 else None))
                    ycast = nv
                u["ycast"] = ycast

            def emit_ln1_head(u):
                if gidx[0] % 2 == 1:
                    gidx[0] += 1
                b0 = next_bank()
                wait("t", c_va, u["ycast"])
                for k in range(8):
                    mm(b0.ap[0:32, :], CSTB[:, 0:32], RG[:, k * QT:(k + 1) * QT],
                       k == 0, k == 7, tp=(0, 0))
                    mm(b0.ap[32:64, :], CSTB[:, 0:32], XKA[:, k * QT:(k + 1) * QT],
                       k == 0, k == 7, tp=(0, 32), inc=(c_g if k == 7 else None))
                nstats = c_g.n
                wait("v", c_g, nstats)
                vop("tensor_scalar", (SR[0:32, :], b0.ap[0:32, :], 1.0 / 1024.0,
                                      None, ALU.mult))
                ntsv = vop("tensor_scalar", (SR[32:64, :], b0.ap[32:64, :],
                                             1.0 / 1024.0, None, ALU.mult),
                           inc=c_va)
                wait("v", c_va, ntsv)
                vop("tensor_tensor", (SR2[0:32, :], SR[0:32, :], SR[0:32, :],
                                      ALU.mult))
                nsr = vop("tensor_copy", (SR2[32:64, :], SR[32:64, :]), inc=c_va)
                b0.free = (c_va, nsr)
                u["nsr"] = nsr

            def emit_ln_bcast_head(u, dst_y1):
                b1 = next_bank()
                wait("t", c_va, u["nsr"])
                nvar = mm(b1.ap[:, :], CSTF[0:64, 128:256], SR2[0:64, :], True,
                          True, inc=c_g)
                b0b = next_bank()
                nmean = mm(b0b.ap[:, :], CSTF[0:64, 0:128], SR[0:64, :], True,
                           True, inc=c_g)
                wait("a", c_g, nvar)
                achain_start()
                if last_nap[0]:
                    wait("a", c_va, last_nap[0])
                nln = act(LTA[:, 0:512], b1.ap[:, :], AF.Ln, bias=MUS[:, 16:17],
                          inc=c_ar)
                wait("a", c_ar, nln)
                nrs = act(LTA[:, 512:1024], LTA[:, 0:512], AF.Exp, scale=-0.5,
                          inc=c_ar)
                last_achain[0] = (c_ar, nrs)
                b1.free = (c_ar, nrs)
                return {"b0b": b0b, "nrs": nrs, "nmean": nmean, "naps": [],
                        "dst": dst_y1}

            def emit_ln_apply_pair(st, op):
                Ybuf = OUTB
                dst_y1 = st["dst"]
                if op == 0:
                    wait("v", c_g, st["nmean"])
                    wait("v", c_ar, st["nrs"])
                wait("v", c_va, c_va.n)
                for j, scr in ((op, STV), (op + 1, RECB)):
                    np1 = vop("scalar_tensor_tensor",
                              (scr[:, :], st["b0b"].ap[:, :], -1.0,
                               Ybuf[:, j * QT:(j + 1) * QT],
                               ALU.mult, ALU.add),
                              inc=(c_va if j == op + 1 else None))
                wait("v", c_va, np1)
                nap = 0
                for j, scr in ((op, STV), (op + 1, RECB)):
                    dst = (dst_y1[:, j * 516 + 4: j * 516 + 4 + QT]
                           if dst_y1 is not None
                           else Ybuf[:, j * QT:(j + 1) * QT])
                    nap = vop("tensor_tensor",
                              (dst, scr[:, :], LTA[:, 512:1024], ALU.mult),
                              inc=c_va)
                st["naps"].append(nap)
                if op == 6:
                    st["b0b"].free = (c_va, nap)
                    last_nap[0] = nap
                return nap

            def emit_ln_bcast_apply(u, dst_y1):
                st = emit_ln_bcast_head(u, dst_y1)
                for op in range(0, 8, 2):
                    emit_ln_apply_pair(st, op)
                return st["naps"][-1], st["naps"]

            def emit_mix_pair(u, op):
                wait("v", c_va, c_va.n)
                for j, scr in ((op, STV), (op + 1, RECB)):
                    nd = vop("tensor_tensor",
                             (scr[:, :], Y1[:, j * 516 + 4: j * 516 + 4 + QT],
                              Y1[:, j * 516 + 3: j * 516 + 3 + QT],
                              ALU.subtract),
                             inc=(c_va if j == op + 1 else None))
                wait("v", c_va, nd)
                for j, scr in ((op, STV), (op + 1, RECB)):
                    vop("scalar_tensor_tensor",
                        (XRA[:, j * QT:(j + 1) * QT], scr[:, :],
                         MUS[:, j:j + 1],
                         Y1[:, j * 516 + 3: j * 516 + 3 + QT],
                         ALU.mult, ALU.add))
                for j, scr in ((op, STV), (op + 1, RECB)):
                    nk = vop("scalar_tensor_tensor",
                             (XKA[:, j * QT:(j + 1) * QT], scr[:, :],
                              MUS[:, 8 + j:9 + j],
                              Y1[:, j * 516 + 3: j * 516 + 3 + QT],
                              ALU.mult, ALU.add),
                             inc=(c_va if j == op + 1 else None))
                wait("v", c_va, nk)
                if op == 6:
                    u["mix_done"] = vop(
                        "tensor_copy",
                        (bass.AP(Y1, 3, [[NCH * 516, 128], [516, 8], [1, 1]]),
                         bass.AP(Y1, 515, [[NCH * 516, 128], [516, 8], [1, 1]])),
                        inc=c_va)

            def emit_mix(u):
                for op in range(0, 8, 2):
                    emit_mix_pair(u, op)

            def make_fillers(u, load_wo=True):
                fl = []

                def r_one(oc):
                    def f():
                        if oc == 0:
                            wait("t", c_va, u["mix_done"])
                        xrch = [XRA[:, k * QT:(k + 1) * QT] for k in range(8)]
                        def sigevac(b):
                            wait("a", c_g)
                            achain_start()
                            if last_nap[0]:
                                wait("a", c_va, last_nap[0])
                            nu = act(LTA[:, 0:512], b.ap[:, :], AF.Exp,
                                     scale=-1.0, inc=c_ar)
                            b.free = (c_ar, nu)
                            wait("a", c_ar, nu)
                            nv_ = act(LTA[:, 512:1024], LTA[:, 0:512], AF.Ln,
                                      bias=MUS[:, 17:18], inc=c_ar)
                            wait("a", c_ar, nv_)
                            nr = act(RG[:, oc * QT:(oc + 1) * QT],
                                     LTA[:, 512:1024], AF.Exp, scale=-1.0,
                                     inc=c_ar)
                            last_achain[0] = (c_ar, nr)
                        u["fr_last"] = proj_group(
                            WS[u["fr_si"]], oc, xrch, sigevac,
                            (c_w[u["fr_si"]], u["fr_n"]) if oc == 0 else None)
                        if oc == 7:
                            slot_free[u["fr_si"]] = (c_g, u["fr_last"])
                            u["fv_si"], u["fv_n"] = load_weight(FV)
                    return f

                def k_pair(op):
                    def f():
                        xkch = [XKA[:, k * QT:(k + 1) * QT] for k in range(8)]
                        be = proj_group_nb(WS[u["fk_si"]], op, xkch,
                                           (c_w[u["fk_si"]], u["fk_n"])
                                           if op == 0 else None)
                        bo = proj_group_nb(WS[u["fk_si"]], op + 1, xkch)
                        u["fk_last"] = c_g.n
                        wait("v", c_g, u["fk_last"])
                        wait("v", c_va, c_va.n)
                        for j, b_, scr in ((op, be, STV), (op + 1, bo, RECB)):
                            nv = vop("tensor_single_scalar",
                                     (scr[:, :], b_.ap[:, :], 0.0, ALU.max),
                                     inc=(c_vg if j == op + 1 else None))
                        be.free = bo.free = (c_vg, nv)
                        wait("v", c_vg, nv)
                        for j, b_, scr in ((op, be, STV), (op + 1, bo, RECB)):
                            nv = vop("tensor_tensor",
                                     (XRA[:, j * QT:(j + 1) * QT], scr[:, :],
                                      scr[:, :], ALU.mult),
                                     inc=(c_va if j == op + 1 else None))
                        u["ka_ready"] = nv
                        if op == 6:
                            slot_free[u["fk_si"]] = (c_g, u["fk_last"])
                            if load_wo:
                                u["wo2_si"], u["wo2_n"] = load_weight(WO)
                    return f

                def v_pair(op):
                    def f():
                        kach = [XRA[:, k * QT:(k + 1) * QT] for k in range(8)]
                        if op == 0:
                            wait("t", c_va, u["ka_ready"])
                        be = proj_group_nb(WS[u["fv_si"]], op, kach,
                                           (c_w[u["fv_si"]], u["fv_n"])
                                           if op == 0 else None)
                        bo = proj_group_nb(WS[u["fv_si"]], op + 1, kach)
                        u["fv_last"] = c_g.n
                        wait("v", c_g, u["fv_last"])
                        wait("v", c_ar)
                        wait("v", c_va, c_va.n)
                        for j, b_, scr in ((op, be, STV), (op + 1, bo, RECB)):
                            nv = vop("tensor_tensor",
                                     (scr[:, :], RG[:, j * QT:(j + 1) * QT],
                                      b_.ap[:, :], ALU.mult),
                                     inc=(c_vg if j == op + 1 else None))
                        be.free = bo.free = (c_vg, nv)
                        wait("v", c_vg, nv)
                        for j, scr in ((op, STV), (op + 1, RECB)):
                            nv = vop("tensor_tensor",
                                     (OUTB[:, j * QT:(j + 1) * QT], scr[:, :],
                                      Y1[:, j * 516 + 4: j * 516 + 4 + QT],
                                      ALU.add),
                                     inc=(c_va if j == op + 1 else None))
                        wait("v", c_va, nv)
                        for j in (op, op + 1):
                            nv = vop("tensor_copy",
                                     (RG[:, j * QT:(j + 1) * QT],
                                      OUTB[:, j * QT:(j + 1) * QT]),
                                     inc=(c_va if j == op + 1 else None))
                        wait("v", c_va, nv)
                        for j in (op, op + 1):
                            nv = vop("tensor_tensor",
                                     (XKA[:, j * QT:(j + 1) * QT],
                                      RG[:, j * QT:(j + 1) * QT],
                                      RG[:, j * QT:(j + 1) * QT], ALU.mult),
                                     inc=(c_va if j == op + 1 else None))
                        u["y2cast"] = nv
                        if op == 6:
                            slot_free[u["fv_si"]] = (c_g, u["fv_last"])
                    return f

                def ln2_stats():
                    if gidx[0] % 2 == 1:
                        gidx[0] += 1
                    b0 = next_bank()
                    wait("t", c_va, u["y2cast"])
                    for k in range(8):
                        mm(b0.ap[0:32, :], CSTB[:, 0:32],
                           RG[:, k * QT:(k + 1) * QT], k == 0, k == 7, tp=(0, 0))
                        mm(b0.ap[32:64, :], CSTB[:, 0:32],
                           XKA[:, k * QT:(k + 1) * QT], k == 0, k == 7,
                           tp=(0, 32), inc=(c_g if k == 7 else None))
                    u["ln2_stats"] = c_g.n
                    wait("v", c_g, u["ln2_stats"])
                    vop("tensor_scalar", (SR[0:32, :], b0.ap[0:32, :],
                                          1.0 / 1024.0, None, ALU.mult))
                    ntsv = vop("tensor_scalar", (SR[32:64, :], b0.ap[32:64, :],
                                                 1.0 / 1024.0, None, ALU.mult),
                               inc=c_va)
                    wait("v", c_va, ntsv)
                    vop("tensor_tensor", (SR2[0:32, :], SR[0:32, :],
                                          SR[0:32, :], ALU.mult))
                    nsr = vop("tensor_copy", (SR2[32:64, :], SR[32:64, :]),
                              inc=c_va)
                    b0.free = (c_va, nsr)
                    u["nsr"] = nsr

                ln2s = {}

                def ln2_head():
                    ln2s["st"] = emit_ln_bcast_head(u, None)

                def ln2_pair(op):
                    def f():
                        emit_ln_apply_pair(ln2s["st"], op)
                        if op == 6:
                            u["naps"] = ln2s["st"]["naps"]
                    return f

                for oc in range(8):
                    fl.append(r_one(oc))
                for op in range(0, 8, 2):
                    fl.append(k_pair(op))
                for op in range(0, 8, 2):
                    fl.append(v_pair(op))
                fl.append(ln2_stats)
                fl.append(ln2_head)
                for op in range(0, 8, 2):
                    fl.append(ln2_pair(op))
                return fl

            def emit_ln2_tail(u, seq_, t_, unext):
                naps = u["naps"]
                xb_free[0] = u["ln2_stats"]
                if unext is not None:
                    wait("s", c_va, u["y2cast"])
                    unext["nxf"] = dma(
                        "s",
                        bass.AP(Y1, 4, [[NCH * 516, 128], [516, 8], [1, QT]]),
                        bass.AP(xT, unext["seq"] * D * S + unext["t"] * QT,
                                [[S, 128], [128 * S, 8], [1, QT]]), c_xf)
                for i, op in enumerate(range(0, 8, 2)):
                    wait("s", c_va, naps[i])
                    out_done[0] = dma(
                        "s",
                        bass.AP(outT, seq_ * D * S + op * 128 * S + t_ * QT,
                                [[S, 128], [128 * S, 2], [1, QT]]),
                        bass.AP(OUTB, op * QT,
                                [[NCH * QT, 128], [QT, 2], [1, QT]]),
                        c_out)

            def q_group(t, oc, qev, u):
                xsrc = ABA if t == 0 else XRA
                xbch = [xsrc[:, k * QT:(k + 1) * QT] for k in range(8)]
                if oc == 0:
                    wait("t", c_xb, u["nxb"])
                    wait("t", c_wqr, n_wq_res)

                def qevac(b):
                    wait("v", c_g)
                    nv = vop("tensor_copy",
                             (QTB[:, oc * QT:(oc + 1) * QT], b.ap[:, :]),
                             inc=c_vg)
                    b.free = (c_vg, nv)
                qev[oc] = 0
                proj_group(WS[2], oc, xbch, qevac)
                qev[oc] = c_vg.n

            def emit_xb(t, u_prev, u):
                gate = u_prev["aba_free"] if u_prev is not None else xb_free[0]
                wait("g", c_g, gate)
                u["nxb"] = dma("g",
                               bass.AP(ABA, 0, [[NCH * QT, 128], [QT, 8], [1, QT]]),
                               bass.AP(xT, seq * D * S + t * QT,
                                       [[S, 128], [128 * S, 8], [1, QT]]), c_xb)

            def emit_attention(t, fillers, qev):
                att = {"den_q": [], "epi": None}
                use4[0] = False
                ao_base = c_ao.n
                pend = []
                step = [0]
                last_inj = [-2]

                def emit_scores_exp(hp, kt):
                    g = gkt[0]
                    so = (g % 2) * 1024
                    if kt == 0 and qev.get(hp):
                        wait("t", c_vg, qev[hp])
                    if g - 2 in exp_hist:
                        wait("t", c_p, exp_hist[g - 2])
                    qa = QTB[0:64, hp * QT:(hp + 1) * QT]
                    qb = QTB[64:128, hp * QT:(hp + 1) * QT]
                    ka = KTB[0:64, hp * S + kt * 128: hp * S + (kt + 1) * 128]
                    kb = KTB[64:128, hp * S + kt * 128: hp * S + (kt + 1) * 128]
                    mm(SAB[:, so:so + 512], ka, qa, True, True, tp=(0, 0))
                    nsc = mm(SAB[:, so + 512:so + 1024], kb, qb, True, True,
                             tp=(64, 0), inc=c_sc)
                    loc = g % (NHP * NKT)
                    if loc == 0 or loc == NHP * NKT - 1:
                        pbs = PB[:, (g % PBS) * 1024:(g % PBS) * 1024 + 1024]
                        wait("a", c_sc, nsc)
                        if g - PBS >= 0:
                            wait("a", c_kt, g - PBS + 1)
                        npp = act(pbs, SAB[:, so:so + 1024], AF.Exp,
                                  scale=SCALE, inc=c_p)
                        exp_hist[g] = npp
                    elif loc % 2 == 0:
                        s0 = ((g - 1) % PBS) * 1024
                        s1 = (g % PBS) * 1024
                        wait("a", c_sc, nsc)
                        if g - PBS >= 0:
                            wait("a", c_kt, g - PBS + 1)
                        npp = act(
                            bass.AP(PB, s1,
                                    [[PBS * 1024, 128], [s0 - s1, 2], [1, 1024]]),
                            SAB[:, 0:2048], AF.Exp, scale=SCALE, inc=c_p)
                        exp_hist[g - 1] = exp_hist[g] = npp
                    gkt[0] += 1
                    return g

                def emit_den(hp_, kt_, g_):
                    pbs_ = PB[:, (g_ % PBS) * 1024:(g_ % PBS) * 1024 + 1024]
                    mm(DEN[0:32, :], CSTB[:, 0:32], pbs_[:, 0:512], kt_ == 0,
                       kt_ == NKT - 1, tp=(0, 0))
                    mm(DEN[32:64, :], CSTB[:, 0:32], pbs_[:, 512:1024],
                       kt_ == 0, kt_ == NKT - 1, tp=(0, 32), inc=c_kt)

                def emit_epi():
                    nrc, nrb_res, b, hp_ = att["epi"]
                    att["epi"] = None
                    wait("t", c_rc, nrc)
                    nrb = mm(b.ap[:, :], CSTF[0:64, 256:384], RCR[0:64, :],
                             True, True, inc=c_rb)
                    assert nrb == nrb_res
                    wait("a", c_rb, nrb_res)
                    ncp = act(RSB[:, :], b.ap[:, :], AF.Copy, inc=c_ar)
                    b.free = (c_ar, ncp)
                    wait("v", c_ar, ncp)
                    nao = vop("tensor_tensor",
                              (ABA[:, hp_ * QT:(hp_ + 1) * QT], PV[:, :],
                               RSB[:, :], ALU.mult), inc=c_ao)
                    bPV.free = (c_ao, nao)

                def emit_pv(hp_, kt_, g_):
                    wait("t", c_p, exp_hist[g_])
                    if kt_ == 0:
                        acquire(bPV)
                    pbs_ = PB[:, (g_ % PBS) * 1024:(g_ % PBS) * 1024 + 1024]
                    va = VTB[:, kt_ * D + (2 * hp_) * 64:
                             kt_ * D + (2 * hp_) * 64 + 64]
                    vb = VTB[:, kt_ * D + (2 * hp_ + 1) * 64:
                             kt_ * D + (2 * hp_ + 1) * 64 + 64]
                    mm(PV[0:64, :], va, pbs_[:, 0:512], kt_ == 0, kt_ == NKT - 1,
                       tp=(0, 0))
                    mm(PV[64:128, :], vb, pbs_[:, 512:1024], kt_ == 0,
                       kt_ == NKT - 1, tp=(0, 64))

                def flush_pv():
                    hp, kt, g = pend.pop(0)
                    if att["epi"] is not None:
                        if att["epi_cd"] == 0:
                            emit_epi()
                        else:
                            att["epi_cd"] -= 1
                    if kt < 2:
                        att["den_q"].append((hp, kt, g))
                        return
                    if kt == 2:
                        dq = att["den_q"]
                        if dq:
                            for hp_, kt_, g_ in dq:
                                emit_pv(hp_, kt_, g_)
                            acquire(bDEN)
                            while dq:
                                emit_den(*dq.pop(0))
                    emit_pv(hp, kt, g)
                    emit_den(hp, kt, g)
                    if kt == NKT - 1:
                        wait("a", c_kt, c_kt.n)
                        achain_start()
                        if last_nrb[0]:
                            wait("a", c_rb, last_nrb[0])
                        nln = act(LTA[0:64, 0:512], DEN[0:64, :], AF.Ln,
                                  inc=c_ar)
                        wait("a", c_ar, nln)
                        nrc = act(RCR[0:64, :], LTA[0:64, 0:512], AF.Exp,
                                  scale=-1.0, inc=c_rc)
                        last_achain[0] = (c_rc, nrc)
                        bDEN.free = (c_ar, nln)
                        b = next_bank()
                        nrb_res = c_rb.n + 1
                        last_nrb[0] = nrb_res
                        att["epi"] = (nrc, nrb_res, b, hp)
                        att["epi_cd"] = 1

                for hp in range(NHP):
                    for kt in range(NKT):
                        g = emit_scores_exp(hp, kt)
                        pend.append((hp, kt, g))
                        if len(pend) >= 3:
                            flush_pv()
                        step[0] += 1
                        if (fillers and fillers[0][0] <= step[0]
                                and step[0] - last_inj[0] >= 2
                                and att["epi"] is None):
                            use4[0] = False
                            fillers.pop(0)[1]()
                            last_inj[0] = step[0]
                while pend:
                    flush_pv()
                if att["epi"] is not None:
                    emit_epi()
                while fillers:
                    fillers.pop(0)[1]()
                return ao_base + 8

            for t in range(NT):
                u = {"wo_si": None, "t": t, "seq": seq}
                fillers = []
                qev = {}

                def mk_q(oc, _t=t, _u=u, _qev=qev):
                    def f():
                        q_group(_t, oc, _qev, _u)
                    return f

                if prev is None:
                    wait("s", c_va, n_y1_init)
                    u["nxf"] = dma(
                        "s",
                        bass.AP(Y1, 4, [[NCH * 516, 128], [516, 8], [1, QT]]),
                        bass.AP(xT, seq * D * S + t * QT,
                                [[S, 128], [128 * S, 8], [1, QT]]), c_xf)
                if prev is not None and t > 0:
                    emit_wo_pairs(prev)
                u["nxb"] = kv_nxb0[0] if t == 0 else xra_nxb[0]
                use4[0] = True
                q_group(t, 0, qev, u)
                for oc in range(1, 8):
                    fillers.append((2 * oc - 1, mk_q(oc)))

                if prev is not None:
                    def u_loads(p=prev):
                        p["fr_si"], p["fr_n"] = load_weight(FR)
                        p["fk_si"], p["fk_n"] = load_weight(FK)
                    fillers.append((15, u_loads))
                    if t > 0:
                        def u_stats(p=prev):
                            emit_ln1_head(p)
                        lnst = {}
                        def u_bhead(p=prev, s=lnst):
                            s["st"] = emit_ln_bcast_head(p, Y1)
                        def mk_ap(op, s=lnst):
                            return lambda: emit_ln_apply_pair(s["st"], op)
                        def mk_mx(op, p=prev):
                            return lambda: emit_mix_pair(p, op)
                        fillers.append((17, u_stats))
                        fillers.append((19, u_bhead))
                        for i, op in enumerate(range(0, 8, 2)):
                            fillers.append((21 + 4 * i, mk_ap(op)))
                            fillers.append((23 + 4 * i, mk_mx(op)))
                        steps = [50 + 3 * i for i in range(17)]
                        steps += [100 + 2 * i for i in range(5)]
                    else:
                        steps = [24 + 3 * i for i in range(17)]
                        steps += [75 + 2 * i for i in range(5)]
                    for st, f in zip(steps, make_fillers(prev, load_wo=True)):
                        fillers.append((st, f))

                u["ao_done"] = emit_attention(t, fillers, qev)
                if prev is not None:
                    u["wo_si"], u["wo_n"] = prev["wo2_si"], prev["wo2_n"]
                    emit_ln2_tail(prev, prev["seq"], prev["t"], u)
                else:
                    wo_si0, wo_n0 = load_weight(WO)
                    u["wo_si"], u["wo_n"] = wo_si0, wo_n0
                if t < NT - 1:
                    wait("g", c_xb, c_xb.n)
                    if prev is not None:
                        wait("g", c_g, prev["fv_last"])
                    xra_nxb[0] = dma(
                        "g",
                        bass.AP(XRA, 0, [[NCH * QT, 128], [QT, 8], [1, QT]]),
                        bass.AP(xT, seq * D * S + (t + 1) * QT,
                                [[S, 128], [128 * S, 8], [1, QT]]), c_xb)
                prev = u

        emit_wo_pairs(prev)
        last_ycast[0] = prev["yadds"]
        prev["fr_si"], prev["fr_n"] = load_weight(FR)
        prev["fk_si"], prev["fk_n"] = load_weight(FK)
        emit_ln1_head(prev)
        emit_ln_bcast_apply(prev, Y1)
        emit_mix(prev)
        for f in make_fillers(prev, load_wo=False):
            f()
        emit_ln2_tail(prev, prev["seq"], prev["t"], None)

        wait("s", c_out, c_out.n)

        with nc.Block() as block:
            def runner(key):
                def _run(e):
                    for f in plan[key]:
                        f(e)
                return _run
            block.tensor(runner("t"))
            block.scalar(runner("a"))
            block.vector(runner("v"))
            block.sync(runner("s"))
            block.gpsimd(runner("g"))

    return nc


_NC_CACHE = [None]


def kernel(L0_attn_x, L0_attn_kernel, L0_attn_WO, L0_ffn_W_r, L0_ffn_W_k,
           L0_ffn_W_v, L0_ffn_mu_r, L0_ffn_mu_k):
    x = np.asarray(L0_attn_x, dtype=np.float32)
    ker = np.asarray(L0_attn_kernel, dtype=np.float32)
    wo = np.asarray(L0_attn_WO, dtype=np.float32)[0]
    wr = np.asarray(L0_ffn_W_r, dtype=np.float32)
    wk = np.asarray(L0_ffn_W_k, dtype=np.float32)
    wv = np.asarray(L0_ffn_W_v, dtype=np.float32)
    mur = np.asarray(L0_ffn_mu_r, dtype=np.float32)
    muk = np.asarray(L0_ffn_mu_k, dtype=np.float32)

    B = x.shape[0]
    ncore = 8
    wq_l = np.ascontiguousarray(ker[0].transpose(1, 0, 2).reshape(D, D))
    wk_l = np.ascontiguousarray(ker[1].transpose(1, 0, 2).reshape(D, D))
    wv_l = np.ascontiguousarray(ker[2].transpose(1, 0, 2).reshape(D, D))
    wo_l = np.ascontiguousarray(wo.reshape(64, 16, D).transpose(1, 0, 2).reshape(D, D))
    wts = np.stack([wk_l, wv_l, wq_l, wo_l,
                    np.ascontiguousarray(wr.T),
                    np.ascontiguousarray(wk.T),
                    np.ascontiguousarray(wv.T)], axis=0)

    mus = np.zeros((128, 18), np.float32)
    for c in range(8):
        mus[:, c] = mur[c * 128:(c + 1) * 128]
        mus[:, 8 + c] = muk[c * 128:(c + 1) * 128]
    mus[:, 16] = EPS
    mus[:, 17] = 1.0

    cst = np.zeros((128, 416), np.float32)
    cst[:, 0:32] = 1.0
    cst[0, 32:160] = 1.0
    cst[0, 160:288] = -1.0
    cst[32, 160:288] = 1.0
    cst[0, 288:352] = 1.0
    cst[32, 352:416] = 1.0

    xT = np.ascontiguousarray(x.transpose(0, 2, 1))

    if _NC_CACHE[0] is None:
        _NC_CACHE[0] = build_nc()
    nc = _NC_CACHE[0]

    in_maps = [{"xT": np.ascontiguousarray(xT[2 * c:2 * c + 2]),
                "wts": wts, "mus": mus, "cst": cst} for c in range(ncore)]
    res = run_bass_kernel_spmd(nc, in_maps, core_ids=list(range(ncore)))
    out = np.empty((B, S, D), np.float32)
    for c in range(ncore):
        o = res.results[c]["outT"]
        out[2 * c] = o[0].T
        out[2 * c + 1] = o[1].T
    return out


# version 27
# speedup vs baseline: 1.3270x; 1.3270x over previous
import contextlib
import numpy as np
import ml_dtypes
import concourse.bass as bass
import concourse.mybir as mybir
from concourse.bass_utils import run_bass_kernel_spmd

F32 = mybir.dt.float32
BF16 = mybir.dt.bfloat16
AF = mybir.ActivationFunctionType
ALU = mybir.AluOpType

D = 1024
S = 2048
NSEQ = 2
NCH = 8
QT = 512
NT = S // QT
NHP = 8
NKT = S // 128
PBS = 3
SCALE = 1.0 / 65536.0
EPS = 1e-5

WK, WV, WQ, WO, FR, FK, FV = range(7)


class Cnt:
    def __init__(self, name):
        self.name = name
        self.n = 0


def build_nc():
    nc = bass.Bass("TRN2", target_bir_lowering=False)

    xT = nc.dram_tensor("xT", [NSEQ, D, S], F32, kind="ExternalInput")
    wts = nc.dram_tensor("wts", [7, D, D], F32, kind="ExternalInput")
    mus = nc.dram_tensor("mus", [128, 18], F32, kind="ExternalInput")
    cst = nc.dram_tensor("cst", [128, 416], F32, kind="ExternalInput")
    outT = nc.dram_tensor("outT", [NSEQ, D, S], F32, kind="ExternalOutput")
    wsc = nc.dram_tensor("wsc", [7 * D, D], BF16, kind="Internal")

    plan = {k: [] for k in "tavsg"}
    sems = {}
    sem_handles = {}

    def em(eng, fn):
        plan[eng].append(fn)

    def C(name):
        c = Cnt(name)
        sems[name] = c
        return c

    c_cstb = C("c_cstb")
    c_wqr = C("c_wqr")
    c_cst = C("c_cst")
    c_wc0 = C("c_wc0")
    c_wc1 = C("c_wc1")
    c_ws0 = C("c_ws0")
    c_ws1 = C("c_ws1")
    c_w0 = C("c_w0")
    c_w1 = C("c_w1")
    c_xf = C("c_xf")
    c_xb = C("c_xb")
    c_out = C("c_out")
    c_sc = C("c_sc")
    c_p = C("c_p")
    c_kt = C("c_kt")
    c_rc = C("c_rc")
    c_g = C("c_g")
    c_vg = C("c_vg")
    c_ao = C("c_ao")
    c_rb = C("c_rb")
    c_tf = C("c_tf")
    c_va = C("c_va")
    c_ar = C("c_ar")

    def wait(eng, c, val=None):
        v = c.n if val is None else val
        if v <= 0:
            return
        em(eng, lambda e, _n=c.name, _v=v: e.wait_ge(sem_handles[_n], _v))

    def bump(c, by=1):
        c.n += by
        return c.n

    def dma(eng, dst, src, c):
        n = bump(c, 16)
        em(eng, lambda e, _d=dst, _s=src, _n=c.name:
           e.dma_start(_d, _s).then_inc(sem_handles[_n], 16))
        return n

    def mm(out, lhsT, rhs, start, stop, tp=None, inc=None):
        n = bump(inc) if inc is not None else None
        def f(e, _o=out, _l=lhsT, _r=rhs, _st=start, _sp=stop, _tp=tp, _inc=inc):
            i = e.matmul(_o, _l, _r, start=_st, stop=_sp, tile_position=_tp,
                         skip_group_check=True)
            if _inc is not None:
                i.then_inc(sem_handles[_inc.name], 1)
        em("t", f)
        return n

    def act(out, in_, func, scale=1.0, bias=0.0, inc=None):
        n = bump(inc) if inc is not None else None
        def f(e, _o=out, _i=in_, _f=func, _s=scale, _b=bias, _inc=inc):
            i = e.activation(_o, _i, _f, scale=_s, bias=_b)
            if _inc is not None:
                i.then_inc(sem_handles[_inc.name], 1)
        em("a", f)
        return n

    def vop(name, args, inc=None):
        n = bump(inc) if inc is not None else None
        def f(e, _nm=name, _a=args, _inc=inc):
            i = getattr(e, _nm)(*_a)
            if _inc is not None:
                i.then_inc(sem_handles[_inc.name], 1)
        em("v", f)
        return n

    def tnop_inc(c):
        n = bump(c)
        em("t", lambda e, _n=c.name: e.nop().then_inc(sem_handles[_n], 1))
        return n

    with contextlib.ExitStack() as ctx:
        sbt = lambda name, shape, dt: ctx.enter_context(nc.sbuf_tensor(name, shape, dt))
        pst = lambda name, shape: ctx.enter_context(nc.psum_tensor(name, shape, F32))

        WS = [sbt("ws0", [128, 8 * D], BF16), sbt("ws1", [128, 8 * D], BF16),
              sbt("ws2", [128, 8 * D], BF16)]
        KTB = sbt("ktb", [128, NHP * S], BF16)
        VTB = sbt("vtb", [128, NKT * D], BF16)
        QTB = sbt("qtb", [128, NCH * QT], BF16)
        PB = sbt("pb", [128, PBS * 1024], BF16)
        ABA = sbt("aba", [128, NCH * QT], BF16)
        XKA = sbt("xka", [128, NCH * QT], BF16)
        XRA = sbt("xra", [128, NCH * QT], BF16)
        RG = sbt("rg", [128, NCH * QT], BF16)
        OUTB = sbt("outb", [128, NCH * QT], F32)
        Y1 = sbt("y1", [128, NCH * 516], F32)
        STV = sbt("stv", [128, QT], F32)
        LTA = sbt("lta", [128, 1024], F32)
        RSB = sbt("rsb", [128, QT], F32)
        RECB = sbt("recb", [128, QT], F32)
        RCR = sbt("rcr", [64, QT], F32)
        SR = sbt("sr", [64, QT], F32)
        SR2 = sbt("sr2", [64, QT], F32)
        CSTF = sbt("cstf", [64, 384], F32)
        CSTB = sbt("cstb", [128, 32], BF16)
        MUS = sbt("musb", [128, 18], F32)

        SA = pst("psa", [128, 1024])
        SB2 = pst("psb", [128, 1024])
        PV = pst("pspv", [128, 512])
        DEN = pst("psden", [128, 512])
        G0 = pst("psg0", [128, 512])
        G1 = pst("psg1", [128, 512])

        for name in sems:
            sem_handles[name] = ctx.enter_context(nc.semaphore(name))

        class Bank:
            def __init__(self, ap):
                self.ap = ap
                self.free = None
        bG = [Bank(G0), Bank(G1)]
        bPV = Bank(PV)
        bDEN = Bank(DEN)
        pool4 = [bG[0], bG[1], bPV, bDEN]

        def acquire(b):
            if b.free is not None:
                wait("t", b.free[0], b.free[1])
                b.free = None

        gidx = [0]
        hidx = [0]
        use4 = [False]

        def next_bank():
            if use4[0]:
                b = pool4[hidx[0] % 4]
                hidx[0] += 1
            else:
                b = bG[gidx[0] % 2]
                gidx[0] += 1
            acquire(b)
            return b

        def proj_group(ws_slot, oc, rhs_chunks, evac, w_ready=None):
            b = next_bank()
            if w_ready is not None:
                wait("t", w_ready[0], w_ready[1])
            for k in range(8):
                lhsT = ws_slot[:, k * D + oc * 128: k * D + (oc + 1) * 128]
                mm(b.ap[:, :], lhsT, rhs_chunks[k], start=(k == 0), stop=(k == 7),
                   inc=(c_g if k == 7 else None))
            evac(b)
            return c_g.n

        def proj_group_nb(ws_slot, oc, rhs_chunks, w_ready=None):
            b = next_bank()
            if w_ready is not None:
                wait("t", w_ready[0], w_ready[1])
            for k in range(8):
                lhsT = ws_slot[:, k * D + oc * 128: k * D + (oc + 1) * 128]
                mm(b.ap[:, :], lhsT, rhs_chunks[k], start=(k == 0), stop=(k == 7),
                   inc=(c_g if k == 7 else None))
            return b

        def vnop():
            em("v", lambda e: e.engine_nop())

        def layer_norm(YBF, YSQ, Ybuf, dst_y1, ycast_n, free_aba_after_stats,
                       ycast_sem=None):
            if gidx[0] % 2 == 1:
                gidx[0] += 1
            b0 = next_bank()
            wait("t", ycast_sem if ycast_sem is not None else c_vg, ycast_n)
            for k in range(8):
                mm(b0.ap[0:32, :], CSTB[:, 0:32], YBF[:, k * QT:(k + 1) * QT],
                   k == 0, k == 7, tp=(0, 0))
                mm(b0.ap[32:64, :], CSTB[:, 0:32], YSQ[:, k * QT:(k + 1) * QT],
                   k == 0, k == 7, tp=(0, 32), inc=(c_g if k == 7 else None))
            nstats = c_g.n
            wait("v", c_g, nstats)
            vop("tensor_scalar", (SR[0:32, :], b0.ap[0:32, :], 1.0 / 1024.0, None,
                                  ALU.mult))
            ntsv = vop("tensor_scalar", (SR[32:64, :], b0.ap[32:64, :], 1.0 / 1024.0,
                                         None, ALU.mult), inc=c_va)
            wait("v", c_va, ntsv)
            vop("tensor_tensor", (SR2[0:32, :], SR[0:32, :], SR[0:32, :], ALU.mult))
            nsr = vop("tensor_copy", (SR2[32:64, :], SR[32:64, :]), inc=c_va)
            b0.free = (c_va, nsr)
            b1 = next_bank()
            wait("t", c_va, nsr)
            nvar = mm(b1.ap[:, :], CSTF[0:64, 128:256], SR2[0:64, :], True, True,
                      inc=c_g)
            b0b = next_bank()
            nmean = mm(b0b.ap[:, :], CSTF[0:64, 0:128], SR[0:64, :], True, True,
                       inc=c_g)
            wait("a", c_g, nvar)
            achain_start()
            if last_nap[0]:
                wait("a", c_va, last_nap[0])
            nln = act(LTA[:, 0:512], b1.ap[:, :], AF.Ln, bias=MUS[:, 16:17],
                      inc=c_ar)
            wait("a", c_ar, nln)
            nrs = act(RSB[:, :], LTA[:, 0:512], AF.Exp, scale=-0.5, inc=c_ar)
            last_achain[0] = (c_ar, nrs)
            b1.free = (c_ar, nrs)
            wait("v", c_g, nmean)
            wait("v", c_ar, nrs)
            nap = 0
            for op in range(0, 8, 2):
                if op:
                    wait("v", c_va, c_va.n)
                for j, scr in ((op, STV), (op + 1, RECB)):
                    np1 = vop("scalar_tensor_tensor",
                              (scr[:, :], b0b.ap[:, :], -1.0,
                               Ybuf[:, j * QT:(j + 1) * QT],
                               ALU.mult, ALU.add),
                              inc=(c_va if j == op + 1 else None))
                wait("v", c_va, np1)
                for j, scr in ((op, STV), (op + 1, RECB)):
                    dst = (dst_y1[:, j * 516 + 4: j * 516 + 4 + QT]
                           if dst_y1 is not None else Ybuf[:, j * QT:(j + 1) * QT])
                    nap = vop("tensor_tensor", (dst, scr[:, :], RSB[:, :], ALU.mult),
                              inc=c_va)
            b0b.free = (c_va, nap)
            last_nap[0] = nap
            return nap, nstats

        n_y1_init = vop("memset", (Y1[:, :], 0.0), inc=c_va)
        dma("g", CSTB[:, :], cst[:, 0:32], c_cstb)
        n_wq_res = dma("g", bass.AP(WS[2], 0, [[8 * D, 128], [D, 8], [1, D]]),
                       bass.AP(wts, WQ * D * D, [[D, 128], [128 * D, 8], [1, D]]),
                       c_wqr)
        dma("s", CSTF[:, :], cst[0:64, 32:416], c_cst)
        dma("s", MUS[:, :], mus[:, :], c_cst)
        wait("t", c_cstb, 16)
        wait("t", c_cst, 32)
        wait("v", c_cst, 32)

        c_wc = [c_wc0, c_wc1]
        c_wsx = [c_ws0, c_ws1]
        c_w = [c_w0, c_w1]
        for m in range(7):
            slot = WS[m % 2]
            if m >= 2:
                wait("g", c_wsx[m % 2], c_wsx[m % 2].n)
            nld = dma("g", bass.AP(slot, 0, [[8 * D, 128], [D, 8], [1, D]]),
                      bass.AP(wts, m * D * D, [[D, 128], [128 * D, 8], [1, D]]),
                      c_wc[m % 2])
            wait("s", c_wc[m % 2], nld)
            dma("s", bass.AP(wsc, m * D * D, [[D, 128], [128 * D, 8], [1, D]]),
                bass.AP(slot, 0, [[8 * D, 128], [D, 8], [1, D]]), c_wsx[m % 2])

        wnext = [0]
        scratch_all = [(c_ws0, c_ws0.n), (c_ws1, c_ws1.n)]
        slot_free = [None, None]
        first_load = [True]

        def load_weight(mat):
            si = wnext[0]
            wnext[0] = 1 - si
            if first_load[0]:
                wait("s", scratch_all[0][0], scratch_all[0][1])
                wait("s", scratch_all[1][0], scratch_all[1][1])
                first_load[0] = False
            fa = slot_free[si]
            if fa is not None:
                wait("s", fa[0], fa[1])
            n = dma("s", bass.AP(WS[si], 0, [[8 * D, 128], [D, 8], [1, D]]),
                    bass.AP(wsc, mat * D * D, [[D, 128], [128 * D, 8], [1, D]]),
                    c_w[si])
            return si, n

        exp_hist = {}
        den_hist = {}
        last_achain = [None]
        last_nrb = [0]
        last_nap = [0]

        def achain_start():
            if last_achain[0] is not None:
                wait("a", last_achain[0][0], last_achain[0][1])
        gkt = [0]
        xb_free = [0]
        out_done = [0]
        prev = None
        last_ycast = [0]

        for seq in range(NSEQ):
            def emit_kv(kv_fillers):
                use4[0] = True
                wk_si, wk_n = load_weight(WK)
                wv_si, wv_n = load_weight(WV)
                wk_last = wv_last = 0
                wait("t", c_w[wk_si], wk_n)
                wait("t", c_w[wv_si], wv_n)
                for tb in (1, 2, 3, 0):
                    wait("g", c_g, xb_free[0])
                    nxb = dma("g",
                              bass.AP(ABA, 0, [[NCH * QT, 128], [QT, 8], [1, QT]]),
                              bass.AP(xT, seq * D * S + tb * QT,
                                      [[S, 128], [128 * S, 8], [1, QT]]), c_xb)
                    if tb == 0:
                        kv_nxb0[0] = nxb
                    wait("t", c_xb, nxb)
                    xbch = [ABA[:, k * QT:(k + 1) * QT] for k in range(8)]
                    for hp in range(NHP):
                        b = next_bank()
                        for k in range(8):
                            mm(b.ap[:, :],
                               WS[wk_si][:, k * D + hp * 128: k * D + (hp + 1) * 128],
                               xbch[k], start=(k == 0), stop=(k == 7),
                               inc=(c_g if k == 7 else None))
                        wk_last = c_g.n
                        wait("v", c_g)
                        nv = vop("tensor_copy",
                                 (KTB[:, hp * S + tb * QT: hp * S + (tb + 1) * QT],
                                  b.ap[:, :]), inc=c_vg)
                        b.free = (c_vg, nv)
                    if kv_fillers:
                        kv_fillers.pop(0)()
                    for tt in range(4):
                        for half in range(2):
                            b = next_bank()
                            for k in range(8):
                                mm(b.ap[:, :], xbch[k][:, tt * 128:(tt + 1) * 128],
                                   WS[wv_si][:, k * D + half * 512:
                                             k * D + half * 512 + 512],
                                   start=(k == 0), stop=(k == 7),
                                   inc=(c_g if k == 7 else None))
                            wv_last = c_g.n
                            kti = tb * 4 + tt
                            wait("v", c_g)
                            nv = vop("tensor_copy",
                                     (VTB[:, kti * D + half * 512:
                                          kti * D + half * 512 + 512],
                                      b.ap[:, :]), inc=c_vg)
                            b.free = (c_vg, nv)
                    if kv_fillers:
                        kv_fillers.pop(0)()
                    xb_free[0] = wv_last
                slot_free[wk_si] = (c_g, wk_last)
                slot_free[wv_si] = (c_g, wv_last)
                while kv_fillers:
                    kv_fillers.pop(0)()
                use4[0] = False

            kvf = []
            kv_nxb0 = [0]
            xra_nxb = [0]
            if prev is not None:
                emit_wo_pairs(prev)
                last_ycast[0] = prev["yadds"]
                xb_free[0] = prev["aba_free"]
                kvf = [lambda: emit_ln1_head(prev),
                       lambda: (emit_ln_bcast_apply(prev, Y1)[0], emit_mix(prev))]
            emit_kv(kvf)

            wait("v", c_va, c_va.n)
            for cch in range(NCH):
                vop("memset", (Y1[:, cch * 516 + 3: cch * 516 + 4], 0.0))

            def emit_wo_pairs(u):
                use4[0] = True
                aoch = [ABA[:, k * QT:(k + 1) * QT] for k in range(8)]
                wait("t", c_ao, u["ao_done"])
                wo_last = 0
                nv = 0
                for op in range(0, 8, 2):
                    be = proj_group_nb(WS[u["wo_si"]], op, aoch,
                                       (c_w[u["wo_si"]], u["wo_n"]) if op == 0
                                       else None)
                    bo = proj_group_nb(WS[u["wo_si"]], op + 1, aoch)
                    wo_last = c_g.n
                    wait("v", c_g, wo_last)
                    if op == 0:
                        wait("v", c_xf, u["nxf"])
                        if out_done[0]:
                            wait("v", c_out, out_done[0])
                    for j, b_ in ((op, be), (op + 1, bo)):
                        nv = vop("tensor_tensor",
                                 (OUTB[:, j * QT:(j + 1) * QT], b_.ap[:, :],
                                  Y1[:, j * 516 + 4: j * 516 + 4 + QT], ALU.add),
                                 inc=(c_vg if j == op + 1 else None))
                    be.free = bo.free = (c_vg, nv)
                slot_free[u["wo_si"]] = (c_g, wo_last)
                u["aba_free"] = wo_last
                u["yadds"] = nv
                wait("v", c_vg, c_vg.n)
                ycast = 0
                for op in range(0, 8, 2):
                    if op:
                        wait("v", c_va, c_va.n)
                    for j in (op, op + 1):
                        nv = vop("tensor_tensor",
                                 (RG[:, j * QT:(j + 1) * QT],
                                  OUTB[:, j * QT:(j + 1) * QT],
                                  OUTB[:, j * QT:(j + 1) * QT], ALU.max),
                                 inc=(c_va if j == op + 1 else None))
                    wait("v", c_va, nv)
                    for j in (op, op + 1):
                        nv = vop("tensor_tensor",
                                 (XKA[:, j * QT:(j + 1) * QT],
                                  RG[:, j * QT:(j + 1) * QT],
                                  RG[:, j * QT:(j + 1) * QT], ALU.mult),
                                 inc=(c_va if j == op + 1 else None))
                    ycast = nv
                u["ycast"] = ycast

            def emit_ln1_head(u):
                if gidx[0] % 2 == 1:
                    gidx[0] += 1
                b0 = next_bank()
                wait("t", c_va, u["ycast"])
                for k in range(8):
                    mm(b0.ap[0:32, :], CSTB[:, 0:32], RG[:, k * QT:(k + 1) * QT],
                       k == 0, k == 7, tp=(0, 0))
                    mm(b0.ap[32:64, :], CSTB[:, 0:32], XKA[:, k * QT:(k + 1) * QT],
                       k == 0, k == 7, tp=(0, 32), inc=(c_g if k == 7 else None))
                nstats = c_g.n
                wait("v", c_g, nstats)
                vop("tensor_scalar", (SR[0:32, :], b0.ap[0:32, :], 1.0 / 1024.0,
                                      None, ALU.mult))
                ntsv = vop("tensor_scalar", (SR[32:64, :], b0.ap[32:64, :],
                                             1.0 / 1024.0, None, ALU.mult),
                           inc=c_va)
                wait("v", c_va, ntsv)
                vop("tensor_tensor", (SR2[0:32, :], SR[0:32, :], SR[0:32, :],
                                      ALU.mult))
                nsr = vop("tensor_copy", (SR2[32:64, :], SR[32:64, :]), inc=c_va)
                b0.free = (c_va, nsr)
                u["nsr"] = nsr

            def emit_ln_bcast_head(u, dst_y1):
                b1 = next_bank()
                wait("t", c_va, u["nsr"])
                nvar = mm(b1.ap[:, :], CSTF[0:64, 128:256], SR2[0:64, :], True,
                          True, inc=c_g)
                b0b = next_bank()
                nmean = mm(b0b.ap[:, :], CSTF[0:64, 0:128], SR[0:64, :], True,
                           True, inc=c_g)
                wait("a", c_g, nvar)
                achain_start()
                if last_nap[0]:
                    wait("a", c_va, last_nap[0])
                nln = act(LTA[:, 0:512], b1.ap[:, :], AF.Ln, bias=MUS[:, 16:17],
                          inc=c_ar)
                wait("a", c_ar, nln)
                nrs = act(LTA[:, 512:1024], LTA[:, 0:512], AF.Exp, scale=-0.5,
                          inc=c_ar)
                last_achain[0] = (c_ar, nrs)
                b1.free = (c_ar, nrs)
                return {"b0b": b0b, "nrs": nrs, "nmean": nmean, "naps": [],
                        "dst": dst_y1}

            def emit_ln_apply_pair(st, op):
                Ybuf = OUTB
                dst_y1 = st["dst"]
                if op == 0:
                    wait("v", c_g, st["nmean"])
                    wait("v", c_ar, st["nrs"])
                wait("v", c_va, c_va.n)
                for j, scr in ((op, STV), (op + 1, RECB)):
                    np1 = vop("scalar_tensor_tensor",
                              (scr[:, :], st["b0b"].ap[:, :], -1.0,
                               Ybuf[:, j * QT:(j + 1) * QT],
                               ALU.mult, ALU.add),
                              inc=(c_va if j == op + 1 else None))
                wait("v", c_va, np1)
                nap = 0
                for j, scr in ((op, STV), (op + 1, RECB)):
                    dst = (dst_y1[:, j * 516 + 4: j * 516 + 4 + QT]
                           if dst_y1 is not None
                           else Ybuf[:, j * QT:(j + 1) * QT])
                    nap = vop("tensor_tensor",
                              (dst, scr[:, :], LTA[:, 512:1024], ALU.mult),
                              inc=c_va)
                st["naps"].append(nap)
                if op == 6:
                    st["b0b"].free = (c_va, nap)
                    last_nap[0] = nap
                return nap

            def emit_ln_bcast_apply(u, dst_y1):
                st = emit_ln_bcast_head(u, dst_y1)
                for op in range(0, 8, 2):
                    emit_ln_apply_pair(st, op)
                return st["naps"][-1], st["naps"]

            def emit_mix_pair(u, op):
                wait("v", c_va, c_va.n)
                for j, scr in ((op, STV), (op + 1, RECB)):
                    nd = vop("tensor_tensor",
                             (scr[:, :], Y1[:, j * 516 + 4: j * 516 + 4 + QT],
                              Y1[:, j * 516 + 3: j * 516 + 3 + QT],
                              ALU.subtract),
                             inc=(c_va if j == op + 1 else None))
                wait("v", c_va, nd)
                for j, scr in ((op, STV), (op + 1, RECB)):
                    vop("scalar_tensor_tensor",
                        (XRA[:, j * QT:(j + 1) * QT], scr[:, :],
                         MUS[:, j:j + 1],
                         Y1[:, j * 516 + 3: j * 516 + 3 + QT],
                         ALU.mult, ALU.add))
                for j, scr in ((op, STV), (op + 1, RECB)):
                    nk = vop("scalar_tensor_tensor",
                             (XKA[:, j * QT:(j + 1) * QT], scr[:, :],
                              MUS[:, 8 + j:9 + j],
                              Y1[:, j * 516 + 3: j * 516 + 3 + QT],
                              ALU.mult, ALU.add),
                             inc=(c_va if j == op + 1 else None))
                wait("v", c_va, nk)
                if op == 6:
                    u["mix_done"] = vop(
                        "tensor_copy",
                        (bass.AP(Y1, 3, [[NCH * 516, 128], [516, 8], [1, 1]]),
                         bass.AP(Y1, 515, [[NCH * 516, 128], [516, 8], [1, 1]])),
                        inc=c_va)

            def emit_mix(u):
                for op in range(0, 8, 2):
                    emit_mix_pair(u, op)

            def make_fillers(u, load_wo=True):
                fl = []

                def r_one(oc):
                    def f():
                        if oc == 0:
                            wait("t", c_va, u["mix_done"])
                        xrch = [XRA[:, k * QT:(k + 1) * QT] for k in range(8)]
                        def sigevac(b):
                            wait("a", c_g)
                            achain_start()
                            if last_nap[0]:
                                wait("a", c_va, last_nap[0])
                            nu = act(LTA[:, 0:512], b.ap[:, :], AF.Exp,
                                     scale=-1.0, inc=c_ar)
                            b.free = (c_ar, nu)
                            wait("a", c_ar, nu)
                            nv_ = act(LTA[:, 512:1024], LTA[:, 0:512], AF.Ln,
                                      bias=MUS[:, 17:18], inc=c_ar)
                            wait("a", c_ar, nv_)
                            nr = act(RG[:, oc * QT:(oc + 1) * QT],
                                     LTA[:, 512:1024], AF.Exp, scale=-1.0,
                                     inc=c_ar)
                            last_achain[0] = (c_ar, nr)
                        u["fr_last"] = proj_group(
                            WS[u["fr_si"]], oc, xrch, sigevac,
                            (c_w[u["fr_si"]], u["fr_n"]) if oc == 0 else None)
                        if oc == 7:
                            slot_free[u["fr_si"]] = (c_g, u["fr_last"])
                            u["fv_si"], u["fv_n"] = load_weight(FV)
                    return f

                def k_pair(op):
                    def f():
                        xkch = [XKA[:, k * QT:(k + 1) * QT] for k in range(8)]
                        be = proj_group_nb(WS[u["fk_si"]], op, xkch,
                                           (c_w[u["fk_si"]], u["fk_n"])
                                           if op == 0 else None)
                        bo = proj_group_nb(WS[u["fk_si"]], op + 1, xkch)
                        u["fk_last"] = c_g.n
                        wait("v", c_g, u["fk_last"])
                        wait("v", c_va, c_va.n)
                        for j, b_, scr in ((op, be, STV), (op + 1, bo, RECB)):
                            nv = vop("tensor_single_scalar",
                                     (scr[:, :], b_.ap[:, :], 0.0, ALU.max),
                                     inc=(c_vg if j == op + 1 else None))
                        be.free = bo.free = (c_vg, nv)
                        wait("v", c_vg, nv)
                        for j, b_, scr in ((op, be, STV), (op + 1, bo, RECB)):
                            nv = vop("tensor_tensor",
                                     (XRA[:, j * QT:(j + 1) * QT], scr[:, :],
                                      scr[:, :], ALU.mult),
                                     inc=(c_va if j == op + 1 else None))
                        u["ka_ready"] = nv
                        if op == 6:
                            slot_free[u["fk_si"]] = (c_g, u["fk_last"])
                            if load_wo:
                                u["wo2_si"], u["wo2_n"] = load_weight(WO)
                    return f

                def v_pair(op):
                    def f():
                        kach = [XRA[:, k * QT:(k + 1) * QT] for k in range(8)]
                        if op == 0:
                            wait("t", c_va, u["ka_ready"])
                        be = proj_group_nb(WS[u["fv_si"]], op, kach,
                                           (c_w[u["fv_si"]], u["fv_n"])
                                           if op == 0 else None)
                        bo = proj_group_nb(WS[u["fv_si"]], op + 1, kach)
                        u["fv_last"] = c_g.n
                        wait("v", c_g, u["fv_last"])
                        wait("v", c_ar)
                        wait("v", c_va, c_va.n)
                        for j, b_, scr in ((op, be, STV), (op + 1, bo, RECB)):
                            nv = vop("tensor_tensor",
                                     (scr[:, :], RG[:, j * QT:(j + 1) * QT],
                                      b_.ap[:, :], ALU.mult),
                                     inc=(c_vg if j == op + 1 else None))
                        be.free = bo.free = (c_vg, nv)
                        wait("v", c_vg, nv)
                        for j, scr in ((op, STV), (op + 1, RECB)):
                            nv = vop("tensor_tensor",
                                     (OUTB[:, j * QT:(j + 1) * QT], scr[:, :],
                                      Y1[:, j * 516 + 4: j * 516 + 4 + QT],
                                      ALU.add),
                                     inc=(c_va if j == op + 1 else None))
                        wait("v", c_va, nv)
                        for j in (op, op + 1):
                            nv = vop("tensor_copy",
                                     (RG[:, j * QT:(j + 1) * QT],
                                      OUTB[:, j * QT:(j + 1) * QT]),
                                     inc=(c_va if j == op + 1 else None))
                        wait("v", c_va, nv)
                        for j in (op, op + 1):
                            nv = vop("tensor_tensor",
                                     (XKA[:, j * QT:(j + 1) * QT],
                                      RG[:, j * QT:(j + 1) * QT],
                                      RG[:, j * QT:(j + 1) * QT], ALU.mult),
                                     inc=(c_va if j == op + 1 else None))
                        u["y2cast"] = nv
                        if op == 6:
                            slot_free[u["fv_si"]] = (c_g, u["fv_last"])
                    return f

                def ln2_stats():
                    if gidx[0] % 2 == 1:
                        gidx[0] += 1
                    b0 = next_bank()
                    wait("t", c_va, u["y2cast"])
                    for k in range(8):
                        mm(b0.ap[0:32, :], CSTB[:, 0:32],
                           RG[:, k * QT:(k + 1) * QT], k == 0, k == 7, tp=(0, 0))
                        mm(b0.ap[32:64, :], CSTB[:, 0:32],
                           XKA[:, k * QT:(k + 1) * QT], k == 0, k == 7,
                           tp=(0, 32), inc=(c_g if k == 7 else None))
                    u["ln2_stats"] = c_g.n
                    wait("v", c_g, u["ln2_stats"])
                    vop("tensor_scalar", (SR[0:32, :], b0.ap[0:32, :],
                                          1.0 / 1024.0, None, ALU.mult))
                    ntsv = vop("tensor_scalar", (SR[32:64, :], b0.ap[32:64, :],
                                                 1.0 / 1024.0, None, ALU.mult),
                               inc=c_va)
                    wait("v", c_va, ntsv)
                    vop("tensor_tensor", (SR2[0:32, :], SR[0:32, :],
                                          SR[0:32, :], ALU.mult))
                    nsr = vop("tensor_copy", (SR2[32:64, :], SR[32:64, :]),
                              inc=c_va)
                    b0.free = (c_va, nsr)
                    u["nsr"] = nsr

                ln2s = {}

                def ln2_head():
                    ln2s["st"] = emit_ln_bcast_head(u, None)

                def ln2_pair(op):
                    def f():
                        emit_ln_apply_pair(ln2s["st"], op)
                        if op == 6:
                            u["naps"] = ln2s["st"]["naps"]
                    return f

                for oc in range(8):
                    fl.append(r_one(oc))
                for op in range(0, 8, 2):
                    fl.append(k_pair(op))
                for op in range(0, 8, 2):
                    fl.append(v_pair(op))
                fl.append(ln2_stats)
                fl.append(ln2_head)
                for op in range(0, 8, 2):
                    fl.append(ln2_pair(op))
                return fl

            def emit_ln2_tail(u, seq_, t_, unext):
                naps = u["naps"]
                xb_free[0] = u["ln2_stats"]
                if unext is not None:
                    wait("s", c_va, u["y2cast"])
                    unext["nxf"] = dma(
                        "s",
                        bass.AP(Y1, 4, [[NCH * 516, 128], [516, 8], [1, QT]]),
                        bass.AP(xT, unext["seq"] * D * S + unext["t"] * QT,
                                [[S, 128], [128 * S, 8], [1, QT]]), c_xf)
                for i, op in enumerate(range(0, 8, 2)):
                    wait("a", c_va, naps[i])
                    out_done[0] = dma(
                        "a",
                        bass.AP(outT, seq_ * D * S + op * 128 * S + t_ * QT,
                                [[S, 128], [128 * S, 2], [1, QT]]),
                        bass.AP(OUTB, op * QT,
                                [[NCH * QT, 128], [QT, 2], [1, QT]]),
                        c_out)

            def q_group(t, oc, qev, u):
                xsrc = ABA if t == 0 else XRA
                xbch = [xsrc[:, k * QT:(k + 1) * QT] for k in range(8)]
                if oc == 0:
                    wait("t", c_xb, u["nxb"])
                    wait("t", c_wqr, n_wq_res)

                def qevac(b):
                    wait("v", c_g)
                    nv = vop("tensor_copy",
                             (QTB[:, oc * QT:(oc + 1) * QT], b.ap[:, :]),
                             inc=c_vg)
                    b.free = (c_vg, nv)
                qev[oc] = 0
                proj_group(WS[2], oc, xbch, qevac)
                qev[oc] = c_vg.n

            def emit_xb(t, u_prev, u):
                gate = u_prev["aba_free"] if u_prev is not None else xb_free[0]
                wait("g", c_g, gate)
                u["nxb"] = dma("g",
                               bass.AP(ABA, 0, [[NCH * QT, 128], [QT, 8], [1, QT]]),
                               bass.AP(xT, seq * D * S + t * QT,
                                       [[S, 128], [128 * S, 8], [1, QT]]), c_xb)

            def emit_attention(t, fillers, qev):
                att = {"den_q": [], "epi": None}
                use4[0] = False
                ao_base = c_ao.n
                pend = []
                step = [0]
                last_inj = [-2]

                def emit_scores_exp(hp, kt):
                    g = gkt[0]
                    sps = SA if (g % 2 == 0) else SB2
                    if kt == 0 and qev.get(hp):
                        wait("t", c_vg, qev[hp])
                    if g - 2 in exp_hist:
                        wait("t", c_p, exp_hist[g - 2])
                    qa = QTB[0:64, hp * QT:(hp + 1) * QT]
                    qb = QTB[64:128, hp * QT:(hp + 1) * QT]
                    ka = KTB[0:64, hp * S + kt * 128: hp * S + (kt + 1) * 128]
                    kb = KTB[64:128, hp * S + kt * 128: hp * S + (kt + 1) * 128]
                    mm(sps[:, 0:512], ka, qa, True, True, tp=(0, 0))
                    nsc = mm(sps[:, 512:1024], kb, qb, True, True, tp=(64, 0),
                             inc=c_sc)
                    pbs = PB[:, (g % PBS) * 1024:(g % PBS) * 1024 + 1024]
                    wait("a", c_sc, nsc)
                    if g - PBS >= 0:
                        wait("a", c_kt, g - PBS + 1)
                    npp = act(pbs, sps[:, 0:1024], AF.Exp, scale=SCALE, inc=c_p)
                    exp_hist[g] = npp
                    gkt[0] += 1
                    return g

                def emit_den(hp_, kt_, g_):
                    pbs_ = PB[:, (g_ % PBS) * 1024:(g_ % PBS) * 1024 + 1024]
                    mm(DEN[0:32, :], CSTB[:, 0:32], pbs_[:, 0:512], kt_ == 0,
                       kt_ == NKT - 1, tp=(0, 0))
                    mm(DEN[32:64, :], CSTB[:, 0:32], pbs_[:, 512:1024],
                       kt_ == 0, kt_ == NKT - 1, tp=(0, 32), inc=c_kt)

                def emit_epi():
                    nrc, nrb_res, b = att["epi"]
                    att["epi"] = None
                    wait("t", c_rc, nrc)
                    nrb = mm(b.ap[:, :], CSTF[0:64, 256:384], RCR[0:64, :],
                             True, True, inc=c_rb)
                    assert nrb == nrb_res

                def emit_pv(hp_, kt_, g_):
                    wait("t", c_p, exp_hist[g_])
                    if kt_ == 0:
                        acquire(bPV)
                    pbs_ = PB[:, (g_ % PBS) * 1024:(g_ % PBS) * 1024 + 1024]
                    va = VTB[:, kt_ * D + (2 * hp_) * 64:
                             kt_ * D + (2 * hp_) * 64 + 64]
                    vb = VTB[:, kt_ * D + (2 * hp_ + 1) * 64:
                             kt_ * D + (2 * hp_ + 1) * 64 + 64]
                    mm(PV[0:64, :], va, pbs_[:, 0:512], kt_ == 0, kt_ == NKT - 1,
                       tp=(0, 0))
                    mm(PV[64:128, :], vb, pbs_[:, 512:1024], kt_ == 0,
                       kt_ == NKT - 1, tp=(0, 64))

                def flush_pv():
                    hp, kt, g = pend.pop(0)
                    if att["epi"] is not None:
                        if att["epi_cd"] == 0:
                            emit_epi()
                        else:
                            att["epi_cd"] -= 1
                    if kt < 2:
                        att["den_q"].append((hp, kt, g))
                        return
                    if kt == 2:
                        dq = att["den_q"]
                        if dq:
                            for hp_, kt_, g_ in dq:
                                emit_pv(hp_, kt_, g_)
                            acquire(bDEN)
                            while dq:
                                emit_den(*dq.pop(0))
                    emit_pv(hp, kt, g)
                    emit_den(hp, kt, g)
                    if kt == NKT - 1:
                        wait("a", c_kt, c_kt.n)
                        achain_start()
                        if last_nrb[0]:
                            wait("a", c_rb, last_nrb[0])
                        nln = act(LTA[0:64, 0:512], DEN[0:64, :], AF.Ln,
                                  inc=c_ar)
                        wait("a", c_ar, nln)
                        nrc = act(RCR[0:64, :], LTA[0:64, 0:512], AF.Exp,
                                  scale=-1.0, inc=c_rc)
                        last_achain[0] = (c_rc, nrc)
                        bDEN.free = (c_ar, nln)
                        b = next_bank()
                        nrb_res = c_rb.n + 1
                        last_nrb[0] = nrb_res
                        att["epi"] = (nrc, nrb_res, b)
                        att["epi_cd"] = 1
                        wait("a", c_rb, nrb_res)
                        ncp = act(RSB[:, :], b.ap[:, :], AF.Copy, inc=c_ar)
                        b.free = (c_ar, ncp)
                        wait("v", c_ar, ncp)
                        nao = vop("tensor_tensor",
                                  (ABA[:, hp * QT:(hp + 1) * QT], PV[:, :],
                                   RSB[:, :], ALU.mult), inc=c_ao)
                        bPV.free = (c_ao, nao)

                for hp in range(NHP):
                    for kt in range(NKT):
                        g = emit_scores_exp(hp, kt)
                        pend.append((hp, kt, g))
                        if len(pend) >= 3:
                            flush_pv()
                        step[0] += 1
                        if (fillers and fillers[0][0] <= step[0]
                                and step[0] - last_inj[0] >= 2
                                and att["epi"] is None):
                            use4[0] = False
                            fillers.pop(0)[1]()
                            last_inj[0] = step[0]
                while pend:
                    flush_pv()
                if att["epi"] is not None:
                    emit_epi()
                while fillers:
                    fillers.pop(0)[1]()
                return ao_base + 8

            for t in range(NT):
                u = {"wo_si": None, "t": t, "seq": seq}
                fillers = []
                qev = {}

                def mk_q(oc, _t=t, _u=u, _qev=qev):
                    def f():
                        q_group(_t, oc, _qev, _u)
                    return f

                if prev is None:
                    wait("s", c_va, n_y1_init)
                    u["nxf"] = dma(
                        "s",
                        bass.AP(Y1, 4, [[NCH * 516, 128], [516, 8], [1, QT]]),
                        bass.AP(xT, seq * D * S + t * QT,
                                [[S, 128], [128 * S, 8], [1, QT]]), c_xf)
                if prev is not None and t > 0:
                    emit_wo_pairs(prev)
                u["nxb"] = kv_nxb0[0] if t == 0 else xra_nxb[0]
                use4[0] = True
                q_group(t, 0, qev, u)
                for oc in range(1, 8):
                    fillers.append((2 * oc - 1, mk_q(oc)))

                if prev is not None:
                    def u_loads(p=prev):
                        p["fr_si"], p["fr_n"] = load_weight(FR)
                        p["fk_si"], p["fk_n"] = load_weight(FK)
                    fillers.append((15, u_loads))
                    if t > 0:
                        def u_stats(p=prev):
                            emit_ln1_head(p)
                        lnst = {}
                        def u_bhead(p=prev, s=lnst):
                            s["st"] = emit_ln_bcast_head(p, Y1)
                        def mk_ap(op, s=lnst):
                            return lambda: emit_ln_apply_pair(s["st"], op)
                        def mk_mx(op, p=prev):
                            return lambda: emit_mix_pair(p, op)
                        fillers.append((17, u_stats))
                        fillers.append((19, u_bhead))
                        for i, op in enumerate(range(0, 8, 2)):
                            fillers.append((21 + 4 * i, mk_ap(op)))
                            fillers.append((23 + 4 * i, mk_mx(op)))
                        steps = [50 + 3 * i for i in range(17)]
                        steps += [100 + 2 * i for i in range(5)]
                    else:
                        steps = [24 + 3 * i for i in range(17)]
                        steps += [75 + 2 * i for i in range(5)]
                    for st, f in zip(steps, make_fillers(prev, load_wo=True)):
                        fillers.append((st, f))

                u["ao_done"] = emit_attention(t, fillers, qev)
                if prev is not None:
                    u["wo_si"], u["wo_n"] = prev["wo2_si"], prev["wo2_n"]
                    emit_ln2_tail(prev, prev["seq"], prev["t"], u)
                else:
                    wo_si0, wo_n0 = load_weight(WO)
                    u["wo_si"], u["wo_n"] = wo_si0, wo_n0
                if t < NT - 1:
                    wait("g", c_xb, c_xb.n)
                    if prev is not None:
                        wait("g", c_g, prev["fv_last"])
                    xra_nxb[0] = dma(
                        "g",
                        bass.AP(XRA, 0, [[NCH * QT, 128], [QT, 8], [1, QT]]),
                        bass.AP(xT, seq * D * S + (t + 1) * QT,
                                [[S, 128], [128 * S, 8], [1, QT]]), c_xb)
                prev = u

        emit_wo_pairs(prev)
        last_ycast[0] = prev["yadds"]
        prev["fr_si"], prev["fr_n"] = load_weight(FR)
        prev["fk_si"], prev["fk_n"] = load_weight(FK)
        emit_ln1_head(prev)
        emit_ln_bcast_apply(prev, Y1)
        emit_mix(prev)
        for f in make_fillers(prev, load_wo=False):
            f()
        emit_ln2_tail(prev, prev["seq"], prev["t"], None)

        wait("s", c_out, c_out.n)

        with nc.Block() as block:
            def runner(key):
                def _run(e):
                    for f in plan[key]:
                        f(e)
                return _run
            block.tensor(runner("t"))
            block.scalar(runner("a"))
            block.vector(runner("v"))
            block.sync(runner("s"))
            block.gpsimd(runner("g"))

    return nc


_NC_CACHE = [None]


def kernel(L0_attn_x, L0_attn_kernel, L0_attn_WO, L0_ffn_W_r, L0_ffn_W_k,
           L0_ffn_W_v, L0_ffn_mu_r, L0_ffn_mu_k):
    x = np.asarray(L0_attn_x, dtype=np.float32)
    ker = np.asarray(L0_attn_kernel, dtype=np.float32)
    wo = np.asarray(L0_attn_WO, dtype=np.float32)[0]
    wr = np.asarray(L0_ffn_W_r, dtype=np.float32)
    wk = np.asarray(L0_ffn_W_k, dtype=np.float32)
    wv = np.asarray(L0_ffn_W_v, dtype=np.float32)
    mur = np.asarray(L0_ffn_mu_r, dtype=np.float32)
    muk = np.asarray(L0_ffn_mu_k, dtype=np.float32)

    B = x.shape[0]
    ncore = 8
    wq_l = np.ascontiguousarray(ker[0].transpose(1, 0, 2).reshape(D, D))
    wk_l = np.ascontiguousarray(ker[1].transpose(1, 0, 2).reshape(D, D))
    wv_l = np.ascontiguousarray(ker[2].transpose(1, 0, 2).reshape(D, D))
    wo_l = np.ascontiguousarray(wo.reshape(64, 16, D).transpose(1, 0, 2).reshape(D, D))
    wts = np.stack([wk_l, wv_l, wq_l, wo_l,
                    np.ascontiguousarray(wr.T),
                    np.ascontiguousarray(wk.T),
                    np.ascontiguousarray(wv.T)], axis=0)

    mus = np.zeros((128, 18), np.float32)
    for c in range(8):
        mus[:, c] = mur[c * 128:(c + 1) * 128]
        mus[:, 8 + c] = muk[c * 128:(c + 1) * 128]
    mus[:, 16] = EPS
    mus[:, 17] = 1.0

    cst = np.zeros((128, 416), np.float32)
    cst[:, 0:32] = 1.0
    cst[0, 32:160] = 1.0
    cst[0, 160:288] = -1.0
    cst[32, 160:288] = 1.0
    cst[0, 288:352] = 1.0
    cst[32, 352:416] = 1.0

    xT = np.ascontiguousarray(x.transpose(0, 2, 1))

    if _NC_CACHE[0] is None:
        _NC_CACHE[0] = build_nc()
    nc = _NC_CACHE[0]

    in_maps = [{"xT": np.ascontiguousarray(xT[2 * c:2 * c + 2]),
                "wts": wts, "mus": mus, "cst": cst} for c in range(ncore)]
    res = run_bass_kernel_spmd(nc, in_maps, core_ids=list(range(ncore)))
    out = np.empty((B, S, D), np.float32)
    for c in range(ncore):
        o = res.results[c]["outT"]
        out[2 * c] = o[0].T
        out[2 * c + 1] = o[1].T
    return out


# version 28
# speedup vs baseline: 1.3342x; 1.0054x over previous
import contextlib
import numpy as np
import ml_dtypes
import concourse.bass as bass
import concourse.mybir as mybir
from concourse.bass_utils import run_bass_kernel_spmd

F32 = mybir.dt.float32
BF16 = mybir.dt.bfloat16
AF = mybir.ActivationFunctionType
ALU = mybir.AluOpType

D = 1024
S = 2048
NSEQ = 2
NCH = 8
QT = 512
NT = S // QT
NHP = 8
NKT = S // 128
PBS = 3
SCALE = 1.0 / 65536.0
EPS = 1e-5

WK, WV, WQ, WO, FR, FK, FV = range(7)


class Cnt:
    def __init__(self, name):
        self.name = name
        self.n = 0


def build_nc():
    nc = bass.Bass("TRN2", target_bir_lowering=False)

    xT = nc.dram_tensor("xT", [NSEQ, D, S], F32, kind="ExternalInput")
    wts = nc.dram_tensor("wts", [7, D, D], F32, kind="ExternalInput")
    mus = nc.dram_tensor("mus", [128, 18], F32, kind="ExternalInput")
    cst = nc.dram_tensor("cst", [128, 416], F32, kind="ExternalInput")
    outT = nc.dram_tensor("outT", [NSEQ, D, S], F32, kind="ExternalOutput")
    wsc = nc.dram_tensor("wsc", [7 * D, D], BF16, kind="Internal")

    plan = {k: [] for k in "tavsg"}
    sems = {}
    sem_handles = {}

    def em(eng, fn):
        plan[eng].append(fn)

    def C(name):
        c = Cnt(name)
        sems[name] = c
        return c

    c_cstb = C("c_cstb")
    c_wqr = C("c_wqr")
    c_cst = C("c_cst")
    c_wc0 = C("c_wc0")
    c_wc1 = C("c_wc1")
    c_ws0 = C("c_ws0")
    c_ws1 = C("c_ws1")
    c_w0 = C("c_w0")
    c_w1 = C("c_w1")
    c_xf = C("c_xf")
    c_xb = C("c_xb")
    c_out = C("c_out")
    c_sc = C("c_sc")
    c_p = C("c_p")
    c_kt = C("c_kt")
    c_rc = C("c_rc")
    c_g = C("c_g")
    c_vg = C("c_vg")
    c_ao = C("c_ao")
    c_rb = C("c_rb")
    c_tf = C("c_tf")
    c_va = C("c_va")
    c_ar = C("c_ar")

    def wait(eng, c, val=None):
        v = c.n if val is None else val
        if v <= 0:
            return
        em(eng, lambda e, _n=c.name, _v=v: e.wait_ge(sem_handles[_n], _v))

    def bump(c, by=1):
        c.n += by
        return c.n

    def dma(eng, dst, src, c):
        n = bump(c, 16)
        em(eng, lambda e, _d=dst, _s=src, _n=c.name:
           e.dma_start(_d, _s).then_inc(sem_handles[_n], 16))
        return n

    def mm(out, lhsT, rhs, start, stop, tp=None, inc=None):
        n = bump(inc) if inc is not None else None
        def f(e, _o=out, _l=lhsT, _r=rhs, _st=start, _sp=stop, _tp=tp, _inc=inc):
            i = e.matmul(_o, _l, _r, start=_st, stop=_sp, tile_position=_tp,
                         skip_group_check=True)
            if _inc is not None:
                i.then_inc(sem_handles[_inc.name], 1)
        em("t", f)
        return n

    def act(out, in_, func, scale=1.0, bias=0.0, inc=None):
        n = bump(inc) if inc is not None else None
        def f(e, _o=out, _i=in_, _f=func, _s=scale, _b=bias, _inc=inc):
            i = e.activation(_o, _i, _f, scale=_s, bias=_b)
            if _inc is not None:
                i.then_inc(sem_handles[_inc.name], 1)
        em("a", f)
        return n

    def vop(name, args, inc=None):
        n = bump(inc) if inc is not None else None
        def f(e, _nm=name, _a=args, _inc=inc):
            i = getattr(e, _nm)(*_a)
            if _inc is not None:
                i.then_inc(sem_handles[_inc.name], 1)
        em("v", f)
        return n

    def tnop_inc(c):
        n = bump(c)
        em("t", lambda e, _n=c.name: e.nop().then_inc(sem_handles[_n], 1))
        return n

    with contextlib.ExitStack() as ctx:
        sbt = lambda name, shape, dt: ctx.enter_context(nc.sbuf_tensor(name, shape, dt))
        pst = lambda name, shape: ctx.enter_context(nc.psum_tensor(name, shape, F32))

        WS = [sbt("ws0", [128, 8 * D], BF16), sbt("ws1", [128, 8 * D], BF16),
              sbt("ws2", [128, 8 * D], BF16)]
        KTB = sbt("ktb", [128, NHP * S], BF16)
        VTB = sbt("vtb", [128, NKT * D], BF16)
        QTB = sbt("qtb", [128, NCH * QT], BF16)
        PB = sbt("pb", [128, PBS * 1024], BF16)
        ABA = sbt("aba", [128, NCH * QT], BF16)
        XKA = sbt("xka", [128, NCH * QT], BF16)
        XRA = sbt("xra", [128, NCH * QT], BF16)
        RG = sbt("rg", [128, NCH * QT], BF16)
        OUTB = sbt("outb", [128, NCH * QT], F32)
        Y1 = sbt("y1", [128, NCH * 516], F32)
        STV = sbt("stv", [128, QT], F32)
        LTA = sbt("lta", [128, 1024], F32)
        RSB = sbt("rsb", [128, QT], F32)
        RECB = sbt("recb", [128, QT], F32)
        RCR = sbt("rcr", [64, QT], F32)
        SR = sbt("sr", [64, QT], F32)
        SR2 = sbt("sr2", [64, QT], F32)
        CSTF = sbt("cstf", [64, 384], F32)
        CSTB = sbt("cstb", [128, 32], BF16)
        MUS = sbt("musb", [128, 18], F32)

        SA = pst("psa", [128, 1024])
        SB2 = pst("psb", [128, 1024])
        PV = pst("pspv", [128, 512])
        DEN = pst("psden", [128, 512])
        G0 = pst("psg0", [128, 512])
        G1 = pst("psg1", [128, 512])

        for name in sems:
            sem_handles[name] = ctx.enter_context(nc.semaphore(name))

        class Bank:
            def __init__(self, ap):
                self.ap = ap
                self.free = None
        bG = [Bank(G0), Bank(G1)]
        bPV = Bank(PV)
        bDEN = Bank(DEN)
        pool4 = [bG[0], bG[1], bPV, bDEN]

        def acquire(b):
            if b.free is not None:
                wait("t", b.free[0], b.free[1])
                b.free = None

        gidx = [0]
        hidx = [0]
        use4 = [False]

        def next_bank():
            if use4[0]:
                b = pool4[hidx[0] % 4]
                hidx[0] += 1
            else:
                b = bG[gidx[0] % 2]
                gidx[0] += 1
            acquire(b)
            return b

        def proj_group(ws_slot, oc, rhs_chunks, evac, w_ready=None):
            b = next_bank()
            if w_ready is not None:
                wait("t", w_ready[0], w_ready[1])
            for k in range(8):
                lhsT = ws_slot[:, k * D + oc * 128: k * D + (oc + 1) * 128]
                mm(b.ap[:, :], lhsT, rhs_chunks[k], start=(k == 0), stop=(k == 7),
                   inc=(c_g if k == 7 else None))
            evac(b)
            return c_g.n

        def proj_group_nb(ws_slot, oc, rhs_chunks, w_ready=None):
            b = next_bank()
            if w_ready is not None:
                wait("t", w_ready[0], w_ready[1])
            for k in range(8):
                lhsT = ws_slot[:, k * D + oc * 128: k * D + (oc + 1) * 128]
                mm(b.ap[:, :], lhsT, rhs_chunks[k], start=(k == 0), stop=(k == 7),
                   inc=(c_g if k == 7 else None))
            return b

        def vnop():
            em("v", lambda e: e.engine_nop())

        def layer_norm(YBF, YSQ, Ybuf, dst_y1, ycast_n, free_aba_after_stats,
                       ycast_sem=None):
            if gidx[0] % 2 == 1:
                gidx[0] += 1
            b0 = next_bank()
            wait("t", ycast_sem if ycast_sem is not None else c_vg, ycast_n)
            for k in range(8):
                mm(b0.ap[0:32, :], CSTB[:, 0:32], YBF[:, k * QT:(k + 1) * QT],
                   k == 0, k == 7, tp=(0, 0))
                mm(b0.ap[32:64, :], CSTB[:, 0:32], YSQ[:, k * QT:(k + 1) * QT],
                   k == 0, k == 7, tp=(0, 32), inc=(c_g if k == 7 else None))
            nstats = c_g.n
            wait("v", c_g, nstats)
            vop("tensor_scalar", (SR[0:32, :], b0.ap[0:32, :], 1.0 / 1024.0, None,
                                  ALU.mult))
            ntsv = vop("tensor_scalar", (SR[32:64, :], b0.ap[32:64, :], 1.0 / 1024.0,
                                         None, ALU.mult), inc=c_va)
            wait("v", c_va, ntsv)
            vop("tensor_tensor", (SR2[0:32, :], SR[0:32, :], SR[0:32, :], ALU.mult))
            nsr = vop("tensor_copy", (SR2[32:64, :], SR[32:64, :]), inc=c_va)
            b0.free = (c_va, nsr)
            b1 = next_bank()
            wait("t", c_va, nsr)
            nvar = mm(b1.ap[:, :], CSTF[0:64, 128:256], SR2[0:64, :], True, True,
                      inc=c_g)
            b0b = next_bank()
            nmean = mm(b0b.ap[:, :], CSTF[0:64, 0:128], SR[0:64, :], True, True,
                       inc=c_g)
            wait("a", c_g, nvar)
            achain_start()
            if last_nap[0]:
                wait("a", c_va, last_nap[0])
            nln = act(LTA[:, 0:512], b1.ap[:, :], AF.Ln, bias=MUS[:, 16:17],
                      inc=c_ar)
            wait("a", c_ar, nln)
            nrs = act(RSB[:, :], LTA[:, 0:512], AF.Exp, scale=-0.5, inc=c_ar)
            last_achain[0] = (c_ar, nrs)
            b1.free = (c_ar, nrs)
            wait("v", c_g, nmean)
            wait("v", c_ar, nrs)
            nap = 0
            for op in range(0, 8, 2):
                if op:
                    wait("v", c_va, c_va.n)
                for j, scr in ((op, STV), (op + 1, RECB)):
                    np1 = vop("scalar_tensor_tensor",
                              (scr[:, :], b0b.ap[:, :], -1.0,
                               Ybuf[:, j * QT:(j + 1) * QT],
                               ALU.mult, ALU.add),
                              inc=(c_va if j == op + 1 else None))
                wait("v", c_va, np1)
                for j, scr in ((op, STV), (op + 1, RECB)):
                    dst = (dst_y1[:, j * 516 + 4: j * 516 + 4 + QT]
                           if dst_y1 is not None else Ybuf[:, j * QT:(j + 1) * QT])
                    nap = vop("tensor_tensor", (dst, scr[:, :], RSB[:, :], ALU.mult),
                              inc=c_va)
            b0b.free = (c_va, nap)
            last_nap[0] = nap
            return nap, nstats

        n_y1_init = vop("memset", (Y1[:, :], 0.0), inc=c_va)
        dma("g", CSTB[:, :], cst[:, 0:32], c_cstb)
        n_wq_res = dma("g", bass.AP(WS[2], 0, [[8 * D, 128], [D, 8], [1, D]]),
                       bass.AP(wts, WQ * D * D, [[D, 128], [128 * D, 8], [1, D]]),
                       c_wqr)
        dma("s", CSTF[:, :], cst[0:64, 32:416], c_cst)
        dma("s", MUS[:, :], mus[:, :], c_cst)
        wait("t", c_cstb, 16)
        wait("t", c_cst, 32)
        wait("v", c_cst, 32)

        c_wc = [c_wc0, c_wc1]
        c_wsx = [c_ws0, c_ws1]
        c_w = [c_w0, c_w1]
        for m in range(7):
            slot = WS[m % 2]
            if m >= 2:
                wait("g", c_wsx[m % 2], c_wsx[m % 2].n)
            nld = dma("g", bass.AP(slot, 0, [[8 * D, 128], [D, 8], [1, D]]),
                      bass.AP(wts, m * D * D, [[D, 128], [128 * D, 8], [1, D]]),
                      c_wc[m % 2])
            wait("s", c_wc[m % 2], nld)
            dma("s", bass.AP(wsc, m * D * D, [[D, 128], [128 * D, 8], [1, D]]),
                bass.AP(slot, 0, [[8 * D, 128], [D, 8], [1, D]]), c_wsx[m % 2])

        wnext = [0]
        scratch_all = [(c_ws0, c_ws0.n), (c_ws1, c_ws1.n)]
        slot_free = [None, None]
        first_load = [True]

        def load_weight(mat):
            si = wnext[0]
            wnext[0] = 1 - si
            if first_load[0]:
                wait("s", scratch_all[0][0], scratch_all[0][1])
                wait("s", scratch_all[1][0], scratch_all[1][1])
                first_load[0] = False
            fa = slot_free[si]
            if fa is not None:
                wait("s", fa[0], fa[1])
            n = dma("s", bass.AP(WS[si], 0, [[8 * D, 128], [D, 8], [1, D]]),
                    bass.AP(wsc, mat * D * D, [[D, 128], [128 * D, 8], [1, D]]),
                    c_w[si])
            return si, n

        exp_hist = {}
        den_hist = {}
        last_achain = [None]
        last_nrb = [0]
        last_nap = [0]

        def achain_start():
            if last_achain[0] is not None:
                wait("a", last_achain[0][0], last_achain[0][1])
        gkt = [0]
        xb_free = [0]
        out_done = [0]
        prev = None
        last_ycast = [0]

        for seq in range(NSEQ):
            def emit_kv(kv_fillers):
                use4[0] = True
                wk_si, wk_n = load_weight(WK)
                wv_si, wv_n = load_weight(WV)
                wk_last = wv_last = 0
                wait("t", c_w[wk_si], wk_n)
                wait("t", c_w[wv_si], wv_n)
                for tb in (1, 2, 3, 0):
                    wait("g", c_g, xb_free[0])
                    nxb = dma("g",
                              bass.AP(ABA, 0, [[NCH * QT, 128], [QT, 8], [1, QT]]),
                              bass.AP(xT, seq * D * S + tb * QT,
                                      [[S, 128], [128 * S, 8], [1, QT]]), c_xb)
                    if tb == 0:
                        kv_nxb0[0] = nxb
                    wait("t", c_xb, nxb)
                    xbch = [ABA[:, k * QT:(k + 1) * QT] for k in range(8)]
                    for hp in range(NHP):
                        b = next_bank()
                        for k in range(8):
                            mm(b.ap[:, :],
                               WS[wk_si][:, k * D + hp * 128: k * D + (hp + 1) * 128],
                               xbch[k], start=(k == 0), stop=(k == 7),
                               inc=(c_g if k == 7 else None))
                        wk_last = c_g.n
                        wait("v", c_g)
                        nv = vop("tensor_copy",
                                 (KTB[:, hp * S + tb * QT: hp * S + (tb + 1) * QT],
                                  b.ap[:, :]), inc=c_vg)
                        b.free = (c_vg, nv)
                    if kv_fillers:
                        kv_fillers.pop(0)()
                    for tt in range(4):
                        for half in range(2):
                            b = next_bank()
                            for k in range(8):
                                mm(b.ap[:, :], xbch[k][:, tt * 128:(tt + 1) * 128],
                                   WS[wv_si][:, k * D + half * 512:
                                             k * D + half * 512 + 512],
                                   start=(k == 0), stop=(k == 7),
                                   inc=(c_g if k == 7 else None))
                            wv_last = c_g.n
                            kti = tb * 4 + tt
                            wait("v", c_g)
                            nv = vop("tensor_copy",
                                     (VTB[:, kti * D + half * 512:
                                          kti * D + half * 512 + 512],
                                      b.ap[:, :]), inc=c_vg)
                            b.free = (c_vg, nv)
                    if kv_fillers:
                        kv_fillers.pop(0)()
                    xb_free[0] = wv_last
                slot_free[wk_si] = (c_g, wk_last)
                slot_free[wv_si] = (c_g, wv_last)
                while kv_fillers:
                    kv_fillers.pop(0)()
                use4[0] = False

            kvf = []
            kv_nxb0 = [0]
            xra_nxb = [0]
            if prev is not None:
                emit_wo_pairs(prev)
                last_ycast[0] = prev["yadds"]
                xb_free[0] = prev["aba_free"]
                kvf = [lambda: emit_ln1_head(prev),
                       lambda: (emit_ln_bcast_apply(prev, Y1)[0], emit_mix(prev))]
            emit_kv(kvf)

            wait("v", c_va, c_va.n)
            for cch in range(NCH):
                vop("memset", (Y1[:, cch * 516 + 3: cch * 516 + 4], 0.0))

            def emit_wo_pairs(u):
                use4[0] = True
                aoch = [ABA[:, k * QT:(k + 1) * QT] for k in range(8)]
                wait("t", c_ao, u["ao_done"])
                wo_last = 0
                nv = 0
                for op in range(0, 8, 2):
                    be = proj_group_nb(WS[u["wo_si"]], op, aoch,
                                       (c_w[u["wo_si"]], u["wo_n"]) if op == 0
                                       else None)
                    bo = proj_group_nb(WS[u["wo_si"]], op + 1, aoch)
                    wo_last = c_g.n
                    wait("v", c_g, wo_last)
                    if op == 0:
                        wait("v", c_xf, u["nxf"])
                        if out_done[0]:
                            wait("v", c_out, out_done[0])
                    for j, b_ in ((op, be), (op + 1, bo)):
                        nv = vop("tensor_tensor",
                                 (OUTB[:, j * QT:(j + 1) * QT], b_.ap[:, :],
                                  Y1[:, j * 516 + 4: j * 516 + 4 + QT], ALU.add),
                                 inc=(c_vg if j == op + 1 else None))
                    be.free = bo.free = (c_vg, nv)
                slot_free[u["wo_si"]] = (c_g, wo_last)
                u["aba_free"] = wo_last
                u["yadds"] = nv
                wait("v", c_vg, c_vg.n)
                ycast = 0
                for op in range(0, 8, 2):
                    if op:
                        wait("v", c_va, c_va.n)
                    for j in (op, op + 1):
                        nv = vop("tensor_tensor",
                                 (RG[:, j * QT:(j + 1) * QT],
                                  OUTB[:, j * QT:(j + 1) * QT],
                                  OUTB[:, j * QT:(j + 1) * QT], ALU.max),
                                 inc=(c_va if j == op + 1 else None))
                    wait("v", c_va, nv)
                    for j in (op, op + 1):
                        nv = vop("tensor_tensor",
                                 (XKA[:, j * QT:(j + 1) * QT],
                                  RG[:, j * QT:(j + 1) * QT],
                                  RG[:, j * QT:(j + 1) * QT], ALU.mult),
                                 inc=(c_va if j == op + 1 else None))
                    ycast = nv
                u["ycast"] = ycast

            def emit_ln1_head(u):
                if gidx[0] % 2 == 1:
                    gidx[0] += 1
                b0 = next_bank()
                wait("t", c_va, u["ycast"])
                for k in range(8):
                    mm(b0.ap[0:32, :], CSTB[:, 0:32], RG[:, k * QT:(k + 1) * QT],
                       k == 0, k == 7, tp=(0, 0))
                    mm(b0.ap[32:64, :], CSTB[:, 0:32], XKA[:, k * QT:(k + 1) * QT],
                       k == 0, k == 7, tp=(0, 32), inc=(c_g if k == 7 else None))
                nstats = c_g.n
                wait("v", c_g, nstats)
                vop("tensor_scalar", (SR[0:32, :], b0.ap[0:32, :], 1.0 / 1024.0,
                                      None, ALU.mult))
                ntsv = vop("tensor_scalar", (SR[32:64, :], b0.ap[32:64, :],
                                             1.0 / 1024.0, None, ALU.mult),
                           inc=c_va)
                wait("v", c_va, ntsv)
                vop("tensor_tensor", (SR2[0:32, :], SR[0:32, :], SR[0:32, :],
                                      ALU.mult))
                nsr = vop("tensor_copy", (SR2[32:64, :], SR[32:64, :]), inc=c_va)
                b0.free = (c_va, nsr)
                u["nsr"] = nsr

            def emit_ln_bcast_head(u, dst_y1):
                b1 = next_bank()
                wait("t", c_va, u["nsr"])
                nvar = mm(b1.ap[:, :], CSTF[0:64, 128:256], SR2[0:64, :], True,
                          True, inc=c_g)
                b0b = next_bank()
                nmean = mm(b0b.ap[:, :], CSTF[0:64, 0:128], SR[0:64, :], True,
                           True, inc=c_g)
                wait("a", c_g, nvar)
                achain_start()
                if last_nap[0]:
                    wait("a", c_va, last_nap[0])
                nln = act(LTA[:, 0:512], b1.ap[:, :], AF.Ln, bias=MUS[:, 16:17],
                          inc=c_ar)
                wait("a", c_ar, nln)
                nrs = act(LTA[:, 512:1024], LTA[:, 0:512], AF.Exp, scale=-0.5,
                          inc=c_ar)
                last_achain[0] = (c_ar, nrs)
                b1.free = (c_ar, nrs)
                return {"b0b": b0b, "nrs": nrs, "nmean": nmean, "naps": [],
                        "dst": dst_y1}

            def emit_ln_apply_pair(st, op):
                Ybuf = OUTB
                dst_y1 = st["dst"]
                if op == 0:
                    wait("v", c_g, st["nmean"])
                    wait("v", c_ar, st["nrs"])
                wait("v", c_va, c_va.n)
                for j, scr in ((op, STV), (op + 1, RECB)):
                    np1 = vop("scalar_tensor_tensor",
                              (scr[:, :], st["b0b"].ap[:, :], -1.0,
                               Ybuf[:, j * QT:(j + 1) * QT],
                               ALU.mult, ALU.add),
                              inc=(c_va if j == op + 1 else None))
                wait("v", c_va, np1)
                nap = 0
                for j, scr in ((op, STV), (op + 1, RECB)):
                    dst = (dst_y1[:, j * 516 + 4: j * 516 + 4 + QT]
                           if dst_y1 is not None
                           else Ybuf[:, j * QT:(j + 1) * QT])
                    nap = vop("tensor_tensor",
                              (dst, scr[:, :], LTA[:, 512:1024], ALU.mult),
                              inc=c_va)
                st["naps"].append(nap)
                if op == 6:
                    st["b0b"].free = (c_va, nap)
                    last_nap[0] = nap
                return nap

            def emit_ln_bcast_apply(u, dst_y1):
                st = emit_ln_bcast_head(u, dst_y1)
                for op in range(0, 8, 2):
                    emit_ln_apply_pair(st, op)
                return st["naps"][-1], st["naps"]

            def emit_mix_pair(u, op):
                wait("v", c_va, c_va.n)
                for j, scr in ((op, STV), (op + 1, RECB)):
                    nd = vop("tensor_tensor",
                             (scr[:, :], Y1[:, j * 516 + 4: j * 516 + 4 + QT],
                              Y1[:, j * 516 + 3: j * 516 + 3 + QT],
                              ALU.subtract),
                             inc=(c_va if j == op + 1 else None))
                wait("v", c_va, nd)
                for j, scr in ((op, STV), (op + 1, RECB)):
                    vop("scalar_tensor_tensor",
                        (XRA[:, j * QT:(j + 1) * QT], scr[:, :],
                         MUS[:, j:j + 1],
                         Y1[:, j * 516 + 3: j * 516 + 3 + QT],
                         ALU.mult, ALU.add))
                for j, scr in ((op, STV), (op + 1, RECB)):
                    nk = vop("scalar_tensor_tensor",
                             (XKA[:, j * QT:(j + 1) * QT], scr[:, :],
                              MUS[:, 8 + j:9 + j],
                              Y1[:, j * 516 + 3: j * 516 + 3 + QT],
                              ALU.mult, ALU.add),
                             inc=(c_va if j == op + 1 else None))
                wait("v", c_va, nk)
                if op == 6:
                    u["mix_done"] = vop(
                        "tensor_copy",
                        (bass.AP(Y1, 3, [[NCH * 516, 128], [516, 8], [1, 1]]),
                         bass.AP(Y1, 515, [[NCH * 516, 128], [516, 8], [1, 1]])),
                        inc=c_va)

            def emit_mix(u):
                for op in range(0, 8, 2):
                    emit_mix_pair(u, op)

            def make_fillers(u, load_wo=True):
                fl = []

                def r_one(oc):
                    def f():
                        if oc == 0:
                            wait("t", c_va, u["mix_done"])
                        xrch = [XRA[:, k * QT:(k + 1) * QT] for k in range(8)]
                        def sigevac(b):
                            wait("a", c_g)
                            achain_start()
                            if last_nap[0]:
                                wait("a", c_va, last_nap[0])
                            nu = act(LTA[:, 0:512], b.ap[:, :], AF.Exp,
                                     scale=-1.0, inc=c_ar)
                            b.free = (c_ar, nu)
                            wait("a", c_ar, nu)
                            nv_ = act(LTA[:, 512:1024], LTA[:, 0:512], AF.Ln,
                                      bias=MUS[:, 17:18], inc=c_ar)
                            wait("a", c_ar, nv_)
                            nr = act(RG[:, oc * QT:(oc + 1) * QT],
                                     LTA[:, 512:1024], AF.Exp, scale=-1.0,
                                     inc=c_ar)
                            last_achain[0] = (c_ar, nr)
                        u["fr_last"] = proj_group(
                            WS[u["fr_si"]], oc, xrch, sigevac,
                            (c_w[u["fr_si"]], u["fr_n"]) if oc == 0 else None)
                        if oc == 7:
                            slot_free[u["fr_si"]] = (c_g, u["fr_last"])
                            u["fv_si"], u["fv_n"] = load_weight(FV)
                    return f

                def k_pair(op):
                    def f():
                        xkch = [XKA[:, k * QT:(k + 1) * QT] for k in range(8)]
                        be = proj_group_nb(WS[u["fk_si"]], op, xkch,
                                           (c_w[u["fk_si"]], u["fk_n"])
                                           if op == 0 else None)
                        bo = proj_group_nb(WS[u["fk_si"]], op + 1, xkch)
                        u["fk_last"] = c_g.n
                        wait("v", c_g, u["fk_last"])
                        wait("v", c_va, c_va.n)
                        for j, b_, scr in ((op, be, STV), (op + 1, bo, RECB)):
                            nv = vop("tensor_single_scalar",
                                     (scr[:, :], b_.ap[:, :], 0.0, ALU.max),
                                     inc=(c_vg if j == op + 1 else None))
                        be.free = bo.free = (c_vg, nv)
                        wait("v", c_vg, nv)
                        for j, b_, scr in ((op, be, STV), (op + 1, bo, RECB)):
                            nv = vop("tensor_tensor",
                                     (XRA[:, j * QT:(j + 1) * QT], scr[:, :],
                                      scr[:, :], ALU.mult),
                                     inc=(c_va if j == op + 1 else None))
                        u["ka_ready"] = nv
                        if op == 6:
                            slot_free[u["fk_si"]] = (c_g, u["fk_last"])
                            if load_wo:
                                u["wo2_si"], u["wo2_n"] = load_weight(WO)
                    return f

                def v_pair(op):
                    def f():
                        kach = [XRA[:, k * QT:(k + 1) * QT] for k in range(8)]
                        if op == 0:
                            wait("t", c_va, u["ka_ready"])
                        be = proj_group_nb(WS[u["fv_si"]], op, kach,
                                           (c_w[u["fv_si"]], u["fv_n"])
                                           if op == 0 else None)
                        bo = proj_group_nb(WS[u["fv_si"]], op + 1, kach)
                        u["fv_last"] = c_g.n
                        wait("v", c_g, u["fv_last"])
                        wait("v", c_ar)
                        wait("v", c_va, c_va.n)
                        for j, b_, scr in ((op, be, STV), (op + 1, bo, RECB)):
                            nv = vop("tensor_tensor",
                                     (scr[:, :], RG[:, j * QT:(j + 1) * QT],
                                      b_.ap[:, :], ALU.mult),
                                     inc=(c_vg if j == op + 1 else None))
                        be.free = bo.free = (c_vg, nv)
                        wait("v", c_vg, nv)
                        for j, scr in ((op, STV), (op + 1, RECB)):
                            nv = vop("tensor_tensor",
                                     (OUTB[:, j * QT:(j + 1) * QT], scr[:, :],
                                      Y1[:, j * 516 + 4: j * 516 + 4 + QT],
                                      ALU.add),
                                     inc=(c_va if j == op + 1 else None))
                        wait("v", c_va, nv)
                        for j in (op, op + 1):
                            nv = vop("tensor_copy",
                                     (RG[:, j * QT:(j + 1) * QT],
                                      OUTB[:, j * QT:(j + 1) * QT]),
                                     inc=(c_va if j == op + 1 else None))
                        wait("v", c_va, nv)
                        for j in (op, op + 1):
                            nv = vop("tensor_tensor",
                                     (XKA[:, j * QT:(j + 1) * QT],
                                      RG[:, j * QT:(j + 1) * QT],
                                      RG[:, j * QT:(j + 1) * QT], ALU.mult),
                                     inc=(c_va if j == op + 1 else None))
                        u["y2cast"] = nv
                        if op == 6:
                            slot_free[u["fv_si"]] = (c_g, u["fv_last"])
                    return f

                def ln2_stats():
                    if gidx[0] % 2 == 1:
                        gidx[0] += 1
                    b0 = next_bank()
                    wait("t", c_va, u["y2cast"])
                    for k in range(8):
                        mm(b0.ap[0:32, :], CSTB[:, 0:32],
                           RG[:, k * QT:(k + 1) * QT], k == 0, k == 7, tp=(0, 0))
                        mm(b0.ap[32:64, :], CSTB[:, 0:32],
                           XKA[:, k * QT:(k + 1) * QT], k == 0, k == 7,
                           tp=(0, 32), inc=(c_g if k == 7 else None))
                    u["ln2_stats"] = c_g.n
                    wait("v", c_g, u["ln2_stats"])
                    vop("tensor_scalar", (SR[0:32, :], b0.ap[0:32, :],
                                          1.0 / 1024.0, None, ALU.mult))
                    ntsv = vop("tensor_scalar", (SR[32:64, :], b0.ap[32:64, :],
                                                 1.0 / 1024.0, None, ALU.mult),
                               inc=c_va)
                    wait("v", c_va, ntsv)
                    vop("tensor_tensor", (SR2[0:32, :], SR[0:32, :],
                                          SR[0:32, :], ALU.mult))
                    nsr = vop("tensor_copy", (SR2[32:64, :], SR[32:64, :]),
                              inc=c_va)
                    b0.free = (c_va, nsr)
                    u["nsr"] = nsr

                ln2s = {}

                def ln2_head():
                    ln2s["st"] = emit_ln_bcast_head(u, None)

                def ln2_pair(op):
                    def f():
                        emit_ln_apply_pair(ln2s["st"], op)
                        if op == 6:
                            u["naps"] = ln2s["st"]["naps"]
                    return f

                for oc in range(8):
                    fl.append(r_one(oc))
                for op in range(0, 8, 2):
                    fl.append(k_pair(op))
                for op in range(0, 8, 2):
                    fl.append(v_pair(op))
                fl.append(ln2_stats)
                fl.append(ln2_head)
                for op in range(0, 8, 2):
                    fl.append(ln2_pair(op))
                return fl

            def emit_ln2_tail(u, seq_, t_, unext):
                naps = u["naps"]
                xb_free[0] = u["ln2_stats"]
                if unext is not None:
                    wait("s", c_va, u["y2cast"])
                    unext["nxf"] = dma(
                        "s",
                        bass.AP(Y1, 4, [[NCH * 516, 128], [516, 8], [1, QT]]),
                        bass.AP(xT, unext["seq"] * D * S + unext["t"] * QT,
                                [[S, 128], [128 * S, 8], [1, QT]]), c_xf)
                for i, op in enumerate(range(0, 8, 2)):
                    wait("a", c_va, naps[i])
                    out_done[0] = dma(
                        "a",
                        bass.AP(outT, seq_ * D * S + op * 128 * S + t_ * QT,
                                [[S, 128], [128 * S, 2], [1, QT]]),
                        bass.AP(OUTB, op * QT,
                                [[NCH * QT, 128], [QT, 2], [1, QT]]),
                        c_out)

            def q_group(t, oc, qev, u):
                xsrc = ABA if t == 0 else XRA
                xbch = [xsrc[:, k * QT:(k + 1) * QT] for k in range(8)]
                if oc == 0:
                    wait("t", c_xb, u["nxb"])
                    wait("t", c_wqr, n_wq_res)

                def qevac(b):
                    wait("v", c_g)
                    nv = vop("tensor_copy",
                             (QTB[:, oc * QT:(oc + 1) * QT], b.ap[:, :]),
                             inc=c_vg)
                    b.free = (c_vg, nv)
                qev[oc] = 0
                proj_group(WS[2], oc, xbch, qevac)
                qev[oc] = c_vg.n

            def emit_xb(t, u_prev, u):
                gate = u_prev["aba_free"] if u_prev is not None else xb_free[0]
                wait("g", c_g, gate)
                u["nxb"] = dma("g",
                               bass.AP(ABA, 0, [[NCH * QT, 128], [QT, 8], [1, QT]]),
                               bass.AP(xT, seq * D * S + t * QT,
                                       [[S, 128], [128 * S, 8], [1, QT]]), c_xb)

            def emit_attention(t, fillers, qev):
                att = {"den_q": [], "epi": None}
                use4[0] = False
                ao_base = c_ao.n
                pend = []
                step = [0]
                last_inj = [-2]

                def emit_scores_exp(hp, kt):
                    g = gkt[0]
                    sps = SA if (g % 2 == 0) else SB2
                    if kt == 0 and qev.get(hp):
                        wait("t", c_vg, qev[hp])
                    if g - 2 in exp_hist:
                        wait("t", c_p, exp_hist[g - 2])
                    qa = QTB[0:64, hp * QT:(hp + 1) * QT]
                    qb = QTB[64:128, hp * QT:(hp + 1) * QT]
                    ka = KTB[0:64, hp * S + kt * 128: hp * S + (kt + 1) * 128]
                    kb = KTB[64:128, hp * S + kt * 128: hp * S + (kt + 1) * 128]
                    mm(sps[:, 0:512], ka, qa, True, True, tp=(0, 0))
                    nsc = mm(sps[:, 512:1024], kb, qb, True, True, tp=(64, 0),
                             inc=c_sc)
                    pbs = PB[:, (g % PBS) * 1024:(g % PBS) * 1024 + 1024]
                    wait("a", c_sc, nsc)
                    if g - PBS >= 0:
                        wait("a", c_kt, g - PBS + 1)
                    npp = act(pbs, sps[:, 0:1024], AF.Exp, scale=SCALE, inc=c_p)
                    exp_hist[g] = npp
                    gkt[0] += 1
                    return g

                def emit_den(hp_, kt_, g_):
                    pbs_ = PB[:, (g_ % PBS) * 1024:(g_ % PBS) * 1024 + 1024]
                    mm(DEN[0:32, :], CSTB[:, 0:32], pbs_[:, 0:512], kt_ == 0,
                       kt_ == NKT - 1, tp=(0, 0))
                    mm(DEN[32:64, :], CSTB[:, 0:32], pbs_[:, 512:1024],
                       kt_ == 0, kt_ == NKT - 1, tp=(0, 32), inc=c_kt)

                def emit_epi():
                    nrc, nrb_res, b = att["epi"]
                    att["epi"] = None
                    wait("t", c_rc, nrc)
                    nrb = mm(b.ap[:, :], CSTF[0:64, 256:384], RCR[0:64, :],
                             True, True, inc=c_rb)
                    assert nrb == nrb_res

                def emit_pv(hp_, kt_, g_):
                    wait("t", c_p, exp_hist[g_])
                    if kt_ == 0:
                        acquire(bPV)
                    pbs_ = PB[:, (g_ % PBS) * 1024:(g_ % PBS) * 1024 + 1024]
                    va = VTB[:, kt_ * D + (2 * hp_) * 64:
                             kt_ * D + (2 * hp_) * 64 + 64]
                    vb = VTB[:, kt_ * D + (2 * hp_ + 1) * 64:
                             kt_ * D + (2 * hp_ + 1) * 64 + 64]
                    mm(PV[0:64, :], va, pbs_[:, 0:512], kt_ == 0, kt_ == NKT - 1,
                       tp=(0, 0))
                    mm(PV[64:128, :], vb, pbs_[:, 512:1024], kt_ == 0,
                       kt_ == NKT - 1, tp=(0, 64))

                def flush_pv():
                    hp, kt, g = pend.pop(0)
                    if att["epi"] is not None:
                        if att["epi_cd"] == 0:
                            emit_epi()
                        else:
                            att["epi_cd"] -= 1
                    if kt < 2:
                        att["den_q"].append((hp, kt, g))
                        return
                    if kt == 2:
                        dq = att["den_q"]
                        if dq:
                            for hp_, kt_, g_ in dq:
                                emit_pv(hp_, kt_, g_)
                            acquire(bDEN)
                            while dq:
                                emit_den(*dq.pop(0))
                    emit_pv(hp, kt, g)
                    emit_den(hp, kt, g)
                    if kt == NKT - 1:
                        wait("a", c_kt, c_kt.n)
                        achain_start()
                        if last_nrb[0]:
                            wait("a", c_rb, last_nrb[0])
                        nln = act(LTA[0:64, 0:512], DEN[0:64, :], AF.Ln,
                                  inc=c_ar)
                        wait("a", c_ar, nln)
                        nrc = act(RCR[0:64, :], LTA[0:64, 0:512], AF.Exp,
                                  scale=-1.0, inc=c_rc)
                        last_achain[0] = (c_rc, nrc)
                        bDEN.free = (c_ar, nln)
                        b = next_bank()
                        nrb_res = c_rb.n + 1
                        last_nrb[0] = nrb_res
                        att["epi"] = (nrc, nrb_res, b)
                        att["epi_cd"] = 1
                        wait("a", c_rb, nrb_res)
                        ncp = act(RSB[:, :], b.ap[:, :], AF.Copy, inc=c_ar)
                        b.free = (c_ar, ncp)
                        wait("v", c_ar, ncp)
                        nao = vop("tensor_tensor",
                                  (ABA[:, hp * QT:(hp + 1) * QT], PV[:, :],
                                   RSB[:, :], ALU.mult), inc=c_ao)
                        bPV.free = (c_ao, nao)

                for hp in range(NHP):
                    for kt in range(NKT):
                        g = emit_scores_exp(hp, kt)
                        pend.append((hp, kt, g))
                        if len(pend) >= 3:
                            flush_pv()
                        step[0] += 1
                        if (fillers and fillers[0][0] <= step[0]
                                and step[0] - last_inj[0] >= 2
                                and (att["epi"] is None
                                     or (len(fillers[0]) > 2 and fillers[0][2]))):
                            use4[0] = False
                            fillers.pop(0)[1]()
                            last_inj[0] = step[0]
                while pend:
                    flush_pv()
                if att["epi"] is not None:
                    emit_epi()
                while fillers:
                    fillers.pop(0)[1]()
                return ao_base + 8

            for t in range(NT):
                u = {"wo_si": None, "t": t, "seq": seq}
                fillers = []
                qev = {}

                def mk_q(oc, _t=t, _u=u, _qev=qev):
                    def f():
                        q_group(_t, oc, _qev, _u)
                    return f

                if prev is None:
                    wait("s", c_va, n_y1_init)
                    u["nxf"] = dma(
                        "s",
                        bass.AP(Y1, 4, [[NCH * 516, 128], [516, 8], [1, QT]]),
                        bass.AP(xT, seq * D * S + t * QT,
                                [[S, 128], [128 * S, 8], [1, QT]]), c_xf)
                if prev is not None and t > 0:
                    emit_wo_pairs(prev)
                u["nxb"] = kv_nxb0[0] if t == 0 else xra_nxb[0]
                use4[0] = True
                q_group(t, 0, qev, u)
                for oc in range(1, 8):
                    fillers.append((2 * oc - 1, mk_q(oc)))

                if prev is not None:
                    def u_loads(p=prev):
                        p["fr_si"], p["fr_n"] = load_weight(FR)
                        p["fk_si"], p["fk_n"] = load_weight(FK)
                    fillers.append((15, u_loads, True))
                    if t > 0:
                        def u_stats(p=prev):
                            emit_ln1_head(p)
                        lnst = {}
                        def u_bhead(p=prev, s=lnst):
                            s["st"] = emit_ln_bcast_head(p, Y1)
                        def mk_ap(op, s=lnst):
                            return lambda: emit_ln_apply_pair(s["st"], op)
                        def mk_mx(op, p=prev):
                            return lambda: emit_mix_pair(p, op)
                        fillers.append((17, u_stats))
                        fillers.append((19, u_bhead))
                        for i, op in enumerate(range(0, 8, 2)):
                            fillers.append((21 + 4 * i, mk_ap(op), True))
                            fillers.append((23 + 4 * i, mk_mx(op), True))
                        steps = [50 + 3 * i for i in range(17)]
                        steps += [100 + 2 * i for i in range(5)]
                    else:
                        steps = [24 + 3 * i for i in range(17)]
                        steps += [75 + 2 * i for i in range(5)]
                    mkfl = make_fillers(prev, load_wo=True)
                    for i, (st, f) in enumerate(zip(steps, mkfl)):
                        fillers.append((st, f, i >= len(mkfl) - 4))

                u["ao_done"] = emit_attention(t, fillers, qev)
                if prev is not None:
                    u["wo_si"], u["wo_n"] = prev["wo2_si"], prev["wo2_n"]
                    emit_ln2_tail(prev, prev["seq"], prev["t"], u)
                else:
                    wo_si0, wo_n0 = load_weight(WO)
                    u["wo_si"], u["wo_n"] = wo_si0, wo_n0
                if t < NT - 1:
                    wait("g", c_xb, c_xb.n)
                    if prev is not None:
                        wait("g", c_g, prev["fv_last"])
                    xra_nxb[0] = dma(
                        "g",
                        bass.AP(XRA, 0, [[NCH * QT, 128], [QT, 8], [1, QT]]),
                        bass.AP(xT, seq * D * S + (t + 1) * QT,
                                [[S, 128], [128 * S, 8], [1, QT]]), c_xb)
                prev = u

        emit_wo_pairs(prev)
        last_ycast[0] = prev["yadds"]
        prev["fr_si"], prev["fr_n"] = load_weight(FR)
        prev["fk_si"], prev["fk_n"] = load_weight(FK)
        emit_ln1_head(prev)
        emit_ln_bcast_apply(prev, Y1)
        emit_mix(prev)
        for f in make_fillers(prev, load_wo=False):
            f()
        emit_ln2_tail(prev, prev["seq"], prev["t"], None)

        wait("s", c_out, c_out.n)

        with nc.Block() as block:
            def runner(key):
                def _run(e):
                    for f in plan[key]:
                        f(e)
                return _run
            block.tensor(runner("t"))
            block.scalar(runner("a"))
            block.vector(runner("v"))
            block.sync(runner("s"))
            block.gpsimd(runner("g"))

    return nc


_NC_CACHE = [None]


def kernel(L0_attn_x, L0_attn_kernel, L0_attn_WO, L0_ffn_W_r, L0_ffn_W_k,
           L0_ffn_W_v, L0_ffn_mu_r, L0_ffn_mu_k):
    x = np.asarray(L0_attn_x, dtype=np.float32)
    ker = np.asarray(L0_attn_kernel, dtype=np.float32)
    wo = np.asarray(L0_attn_WO, dtype=np.float32)[0]
    wr = np.asarray(L0_ffn_W_r, dtype=np.float32)
    wk = np.asarray(L0_ffn_W_k, dtype=np.float32)
    wv = np.asarray(L0_ffn_W_v, dtype=np.float32)
    mur = np.asarray(L0_ffn_mu_r, dtype=np.float32)
    muk = np.asarray(L0_ffn_mu_k, dtype=np.float32)

    B = x.shape[0]
    ncore = 8
    wq_l = np.ascontiguousarray(ker[0].transpose(1, 0, 2).reshape(D, D))
    wk_l = np.ascontiguousarray(ker[1].transpose(1, 0, 2).reshape(D, D))
    wv_l = np.ascontiguousarray(ker[2].transpose(1, 0, 2).reshape(D, D))
    wo_l = np.ascontiguousarray(wo.reshape(64, 16, D).transpose(1, 0, 2).reshape(D, D))
    wts = np.stack([wk_l, wv_l, wq_l, wo_l,
                    np.ascontiguousarray(wr.T),
                    np.ascontiguousarray(wk.T),
                    np.ascontiguousarray(wv.T)], axis=0)

    mus = np.zeros((128, 18), np.float32)
    for c in range(8):
        mus[:, c] = mur[c * 128:(c + 1) * 128]
        mus[:, 8 + c] = muk[c * 128:(c + 1) * 128]
    mus[:, 16] = EPS
    mus[:, 17] = 1.0

    cst = np.zeros((128, 416), np.float32)
    cst[:, 0:32] = 1.0
    cst[0, 32:160] = 1.0
    cst[0, 160:288] = -1.0
    cst[32, 160:288] = 1.0
    cst[0, 288:352] = 1.0
    cst[32, 352:416] = 1.0

    xT = np.ascontiguousarray(x.transpose(0, 2, 1))

    if _NC_CACHE[0] is None:
        _NC_CACHE[0] = build_nc()
    nc = _NC_CACHE[0]

    in_maps = [{"xT": np.ascontiguousarray(xT[2 * c:2 * c + 2]),
                "wts": wts, "mus": mus, "cst": cst} for c in range(ncore)]
    res = run_bass_kernel_spmd(nc, in_maps, core_ids=list(range(ncore)))
    out = np.empty((B, S, D), np.float32)
    for c in range(ncore):
        o = res.results[c]["outT"]
        out[2 * c] = o[0].T
        out[2 * c + 1] = o[1].T
    return out


# version 29
# speedup vs baseline: 1.3603x; 1.0196x over previous
import contextlib
import numpy as np
import ml_dtypes
import concourse.bass as bass
import concourse.mybir as mybir
from concourse.bass_utils import run_bass_kernel_spmd

F32 = mybir.dt.float32
BF16 = mybir.dt.bfloat16
AF = mybir.ActivationFunctionType
ALU = mybir.AluOpType

D = 1024
S = 2048
NSEQ = 2
NCH = 8
QT = 512
NT = S // QT
NHP = 8
NKT = S // 128
PBS = 3
SCALE = 1.0 / 65536.0
EPS = 1e-5

WK, WV, WQ, WO, FR, FK, FV = range(7)


class Cnt:
    def __init__(self, name):
        self.name = name
        self.n = 0


def build_nc():
    nc = bass.Bass("TRN2", target_bir_lowering=False)

    xT = nc.dram_tensor("xT", [NSEQ, D, S], F32, kind="ExternalInput")
    wts = nc.dram_tensor("wts", [7, D, D], F32, kind="ExternalInput")
    mus = nc.dram_tensor("mus", [128, 18], F32, kind="ExternalInput")
    cst = nc.dram_tensor("cst", [128, 416], F32, kind="ExternalInput")
    outT = nc.dram_tensor("outT", [NSEQ, D, S], F32, kind="ExternalOutput")
    wsc = nc.dram_tensor("wsc", [7 * D, D], BF16, kind="Internal")

    plan = {k: [] for k in "tavsg"}
    sems = {}
    sem_handles = {}

    def em(eng, fn):
        plan[eng].append(fn)

    def C(name):
        c = Cnt(name)
        sems[name] = c
        return c

    c_cstb = C("c_cstb")
    c_wqr = C("c_wqr")
    c_cst = C("c_cst")
    c_wc0 = C("c_wc0")
    c_wc1 = C("c_wc1")
    c_ws0 = C("c_ws0")
    c_ws1 = C("c_ws1")
    c_w0 = C("c_w0")
    c_w1 = C("c_w1")
    c_xf = C("c_xf")
    c_xb = C("c_xb")
    c_out = C("c_out")
    c_sc = C("c_sc")
    c_p = C("c_p")
    c_kt = C("c_kt")
    c_rc = C("c_rc")
    c_g = C("c_g")
    c_vg = C("c_vg")
    c_ao = C("c_ao")
    c_rb = C("c_rb")
    c_tf = C("c_tf")
    c_va = C("c_va")
    c_ar = C("c_ar")

    def wait(eng, c, val=None):
        v = c.n if val is None else val
        if v <= 0:
            return
        em(eng, lambda e, _n=c.name, _v=v: e.wait_ge(sem_handles[_n], _v))

    def bump(c, by=1):
        c.n += by
        return c.n

    def dma(eng, dst, src, c):
        n = bump(c, 16)
        em(eng, lambda e, _d=dst, _s=src, _n=c.name:
           e.dma_start(_d, _s).then_inc(sem_handles[_n], 16))
        return n

    def mm(out, lhsT, rhs, start, stop, tp=None, inc=None):
        n = bump(inc) if inc is not None else None
        def f(e, _o=out, _l=lhsT, _r=rhs, _st=start, _sp=stop, _tp=tp, _inc=inc):
            i = e.matmul(_o, _l, _r, start=_st, stop=_sp, tile_position=_tp,
                         skip_group_check=True)
            if _inc is not None:
                i.then_inc(sem_handles[_inc.name], 1)
        em("t", f)
        return n

    def act(out, in_, func, scale=1.0, bias=0.0, inc=None):
        n = bump(inc) if inc is not None else None
        def f(e, _o=out, _i=in_, _f=func, _s=scale, _b=bias, _inc=inc):
            i = e.activation(_o, _i, _f, scale=_s, bias=_b)
            if _inc is not None:
                i.then_inc(sem_handles[_inc.name], 1)
        em("a", f)
        return n

    def vop(name, args, inc=None):
        n = bump(inc) if inc is not None else None
        def f(e, _nm=name, _a=args, _inc=inc):
            i = getattr(e, _nm)(*_a)
            if _inc is not None:
                i.then_inc(sem_handles[_inc.name], 1)
        em("v", f)
        return n

    def tnop_inc(c):
        n = bump(c)
        em("t", lambda e, _n=c.name: e.nop().then_inc(sem_handles[_n], 1))
        return n

    with contextlib.ExitStack() as ctx:
        sbt = lambda name, shape, dt: ctx.enter_context(nc.sbuf_tensor(name, shape, dt))
        pst = lambda name, shape: ctx.enter_context(nc.psum_tensor(name, shape, F32))

        WS = [sbt("ws0", [128, 8 * D], BF16), sbt("ws1", [128, 8 * D], BF16),
              sbt("ws2", [128, 8 * D], BF16)]
        KTB = sbt("ktb", [128, NHP * S], BF16)
        VTB = sbt("vtb", [128, NKT * D], BF16)
        QTB = sbt("qtb", [128, NCH * QT], BF16)
        PB = sbt("pb", [128, PBS * 1024], BF16)
        ABA = sbt("aba", [128, NCH * QT], BF16)
        XKA = sbt("xka", [128, NCH * QT], BF16)
        XRA = sbt("xra", [128, NCH * QT], BF16)
        RG = sbt("rg", [128, NCH * QT], BF16)
        OUTB = sbt("outb", [128, NCH * QT], F32)
        Y1 = sbt("y1", [128, NCH * 516], F32)
        STV = sbt("stv", [128, QT], F32)
        LTA = sbt("lta", [128, 1024], F32)
        RSB = sbt("rsb", [128, QT], F32)
        RECB = sbt("recb", [128, QT], F32)
        RCR = sbt("rcr", [64, QT], F32)
        SR = sbt("sr", [64, QT], F32)
        SR2 = sbt("sr2", [64, QT], F32)
        CSTF = sbt("cstf", [64, 384], F32)
        CSTB = sbt("cstb", [128, 32], BF16)
        MUS = sbt("musb", [128, 18], F32)

        SA = pst("psa", [128, 1024])
        SB2 = pst("psb", [128, 1024])
        PV = pst("pspv", [128, 512])
        DEN = pst("psden", [128, 512])
        G0 = pst("psg0", [128, 512])
        G1 = pst("psg1", [128, 512])

        for name in sems:
            sem_handles[name] = ctx.enter_context(nc.semaphore(name))

        class Bank:
            def __init__(self, ap):
                self.ap = ap
                self.free = None
        bG = [Bank(G0), Bank(G1)]
        bPV = Bank(PV)
        bDEN = Bank(DEN)
        pool4 = [bG[0], bG[1], bPV, bDEN]

        def acquire(b):
            if b.free is not None:
                wait("t", b.free[0], b.free[1])
                b.free = None

        gidx = [0]
        hidx = [0]
        use4 = [False]

        def next_bank():
            if use4[0]:
                b = pool4[hidx[0] % 4]
                hidx[0] += 1
            else:
                b = bG[gidx[0] % 2]
                gidx[0] += 1
            acquire(b)
            return b

        def proj_group(ws_slot, oc, rhs_chunks, evac, w_ready=None):
            b = next_bank()
            if w_ready is not None:
                wait("t", w_ready[0], w_ready[1])
            for k in range(8):
                lhsT = ws_slot[:, k * D + oc * 128: k * D + (oc + 1) * 128]
                mm(b.ap[:, :], lhsT, rhs_chunks[k], start=(k == 0), stop=(k == 7),
                   inc=(c_g if k == 7 else None))
            evac(b)
            return c_g.n

        def proj_group_nb(ws_slot, oc, rhs_chunks, w_ready=None):
            b = next_bank()
            if w_ready is not None:
                wait("t", w_ready[0], w_ready[1])
            for k in range(8):
                lhsT = ws_slot[:, k * D + oc * 128: k * D + (oc + 1) * 128]
                mm(b.ap[:, :], lhsT, rhs_chunks[k], start=(k == 0), stop=(k == 7),
                   inc=(c_g if k == 7 else None))
            return b

        def vnop():
            em("v", lambda e: e.engine_nop())

        def layer_norm(YBF, YSQ, Ybuf, dst_y1, ycast_n, free_aba_after_stats,
                       ycast_sem=None):
            if gidx[0] % 2 == 1:
                gidx[0] += 1
            b0 = next_bank()
            wait("t", ycast_sem if ycast_sem is not None else c_vg, ycast_n)
            for k in range(8):
                mm(b0.ap[0:32, :], CSTB[:, 0:32], YBF[:, k * QT:(k + 1) * QT],
                   k == 0, k == 7, tp=(0, 0))
                mm(b0.ap[32:64, :], CSTB[:, 0:32], YSQ[:, k * QT:(k + 1) * QT],
                   k == 0, k == 7, tp=(0, 32), inc=(c_g if k == 7 else None))
            nstats = c_g.n
            wait("v", c_g, nstats)
            vop("tensor_scalar", (SR[0:32, :], b0.ap[0:32, :], 1.0 / 1024.0, None,
                                  ALU.mult))
            ntsv = vop("tensor_scalar", (SR[32:64, :], b0.ap[32:64, :], 1.0 / 1024.0,
                                         None, ALU.mult), inc=c_va)
            wait("v", c_va, ntsv)
            vop("tensor_tensor", (SR2[0:32, :], SR[0:32, :], SR[0:32, :], ALU.mult))
            nsr = vop("tensor_copy", (SR2[32:64, :], SR[32:64, :]), inc=c_va)
            b0.free = (c_va, nsr)
            b1 = next_bank()
            wait("t", c_va, nsr)
            nvar = mm(b1.ap[:, :], CSTF[0:64, 128:256], SR2[0:64, :], True, True,
                      inc=c_g)
            b0b = next_bank()
            nmean = mm(b0b.ap[:, :], CSTF[0:64, 0:128], SR[0:64, :], True, True,
                       inc=c_g)
            wait("a", c_g, nvar)
            achain_start()
            if last_nap[0]:
                wait("a", c_va, last_nap[0])
            nln = act(LTA[:, 0:512], b1.ap[:, :], AF.Ln, bias=MUS[:, 16:17],
                      inc=c_ar)
            wait("a", c_ar, nln)
            nrs = act(RSB[:, :], LTA[:, 0:512], AF.Exp, scale=-0.5, inc=c_ar)
            last_achain[0] = (c_ar, nrs)
            b1.free = (c_ar, nrs)
            wait("v", c_g, nmean)
            wait("v", c_ar, nrs)
            nap = 0
            for op in range(0, 8, 2):
                if op:
                    wait("v", c_va, c_va.n)
                for j, scr in ((op, STV), (op + 1, RECB)):
                    np1 = vop("scalar_tensor_tensor",
                              (scr[:, :], b0b.ap[:, :], -1.0,
                               Ybuf[:, j * QT:(j + 1) * QT],
                               ALU.mult, ALU.add),
                              inc=(c_va if j == op + 1 else None))
                wait("v", c_va, np1)
                for j, scr in ((op, STV), (op + 1, RECB)):
                    dst = (dst_y1[:, j * 516 + 4: j * 516 + 4 + QT]
                           if dst_y1 is not None else Ybuf[:, j * QT:(j + 1) * QT])
                    nap = vop("tensor_tensor", (dst, scr[:, :], RSB[:, :], ALU.mult),
                              inc=c_va)
            b0b.free = (c_va, nap)
            last_nap[0] = nap
            return nap, nstats

        n_y1_init = vop("memset", (Y1[:, :], 0.0), inc=c_va)
        dma("g", CSTB[:, :], cst[:, 0:32], c_cstb)
        n_wq_res = dma("g", bass.AP(WS[2], 0, [[8 * D, 128], [D, 8], [1, D]]),
                       bass.AP(wts, WQ * D * D, [[D, 128], [128 * D, 8], [1, D]]),
                       c_wqr)
        dma("s", CSTF[:, :], cst[0:64, 32:416], c_cst)
        dma("s", MUS[:, :], mus[:, :], c_cst)
        wait("t", c_cstb, 16)
        wait("t", c_cst, 32)
        wait("v", c_cst, 32)

        c_wc = [c_wc0, c_wc1]
        c_wsx = [c_ws0, c_ws1]
        c_w = [c_w0, c_w1]
        for m in range(7):
            slot = WS[m % 2]
            if m >= 2:
                wait("g", c_wsx[m % 2], c_wsx[m % 2].n)
            nld = dma("g", bass.AP(slot, 0, [[8 * D, 128], [D, 8], [1, D]]),
                      bass.AP(wts, m * D * D, [[D, 128], [128 * D, 8], [1, D]]),
                      c_wc[m % 2])
            wait("s", c_wc[m % 2], nld)
            dma("s", bass.AP(wsc, m * D * D, [[D, 128], [128 * D, 8], [1, D]]),
                bass.AP(slot, 0, [[8 * D, 128], [D, 8], [1, D]]), c_wsx[m % 2])

        wnext = [0]
        scratch_all = [(c_ws0, c_ws0.n), (c_ws1, c_ws1.n)]
        slot_free = [None, None]
        first_load = [True]

        def load_weight(mat):
            si = wnext[0]
            wnext[0] = 1 - si
            if first_load[0]:
                wait("s", scratch_all[0][0], scratch_all[0][1])
                wait("s", scratch_all[1][0], scratch_all[1][1])
                first_load[0] = False
            fa = slot_free[si]
            if fa is not None:
                wait("s", fa[0], fa[1])
            n = dma("s", bass.AP(WS[si], 0, [[8 * D, 128], [D, 8], [1, D]]),
                    bass.AP(wsc, mat * D * D, [[D, 128], [128 * D, 8], [1, D]]),
                    c_w[si])
            return si, n

        exp_hist = {}
        den_hist = {}
        last_achain = [None]
        last_nrb = [0]
        last_nap = [0]

        def achain_start():
            if last_achain[0] is not None:
                wait("a", last_achain[0][0], last_achain[0][1])
        gkt = [0]
        xb_free = [0]
        out_done = [0]
        prev = None
        last_ycast = [0]

        for seq in range(NSEQ):
            def emit_kv(kv_fillers):
                use4[0] = True
                wk_si, wk_n = load_weight(WK)
                wv_si, wv_n = load_weight(WV)
                wk_last = wv_last = 0
                wait("t", c_w[wk_si], wk_n)
                wait("t", c_w[wv_si], wv_n)
                order = (1, 2, 3, 0)
                dsts = {1: ABA, 2: RG, 3: ABA, 0: RG}
                buf_free = {id(ABA): (c_g, xb_free[0]), id(RG): None}
                loads = {}

                def emit_load(tb):
                    buf = dsts[tb]
                    wait("g", c_xb, c_xb.n)
                    fa = buf_free[id(buf)]
                    if fa is not None:
                        wait("g", fa[0], fa[1])
                    loads[tb] = dma(
                        "g",
                        bass.AP(buf, 0, [[NCH * QT, 128], [QT, 8], [1, QT]]),
                        bass.AP(xT, seq * D * S + tb * QT,
                                [[S, 128], [128 * S, 8], [1, QT]]), c_xb)

                emit_load(1)
                for idx, tb in enumerate(order):
                    buf = dsts[tb]
                    nxb = loads[tb]
                    if tb == 0:
                        kv_nxb0[0] = nxb
                    wait("t", c_xb, nxb)
                    xbch = [buf[:, k * QT:(k + 1) * QT] for k in range(8)]
                    for hp in range(NHP):
                        b = next_bank()
                        for k in range(8):
                            mm(b.ap[:, :],
                               WS[wk_si][:, k * D + hp * 128: k * D + (hp + 1) * 128],
                               xbch[k], start=(k == 0), stop=(k == 7),
                               inc=(c_g if k == 7 else None))
                        wk_last = c_g.n
                        wait("v", c_g)
                        nv = vop("tensor_copy",
                                 (KTB[:, hp * S + tb * QT: hp * S + (tb + 1) * QT],
                                  b.ap[:, :]), inc=c_vg)
                        b.free = (c_vg, nv)
                    if kv_fillers:
                        kv_fillers.pop(0)()
                    if idx + 1 < len(order):
                        if buf_free[id(RG)] is None:
                            buf_free[id(RG)] = (c_g, c_g.n)
                        emit_load(order[idx + 1])
                    for tt in range(4):
                        for half in range(2):
                            b = next_bank()
                            for k in range(8):
                                mm(b.ap[:, :], xbch[k][:, tt * 128:(tt + 1) * 128],
                                   WS[wv_si][:, k * D + half * 512:
                                             k * D + half * 512 + 512],
                                   start=(k == 0), stop=(k == 7),
                                   inc=(c_g if k == 7 else None))
                            wv_last = c_g.n
                            kti = tb * 4 + tt
                            wait("v", c_g)
                            nv = vop("tensor_copy",
                                     (VTB[:, kti * D + half * 512:
                                          kti * D + half * 512 + 512],
                                      b.ap[:, :]), inc=c_vg)
                            b.free = (c_vg, nv)
                    if kv_fillers:
                        kv_fillers.pop(0)()
                    buf_free[id(buf)] = (c_g, wv_last)
                    xb_free[0] = wv_last
                slot_free[wk_si] = (c_g, wk_last)
                slot_free[wv_si] = (c_g, wv_last)
                while kv_fillers:
                    kv_fillers.pop(0)()
                use4[0] = False

            kvf = []
            kv_nxb0 = [0]
            xra_nxb = [0]
            if prev is not None:
                emit_wo_pairs(prev)
                last_ycast[0] = prev["yadds"]
                xb_free[0] = prev["aba_free"]
                kvf = [lambda: emit_ln1_head(prev),
                       lambda: (emit_ln_bcast_apply(prev, Y1)[0], emit_mix(prev))]
            emit_kv(kvf)

            wait("v", c_va, c_va.n)
            for cch in range(NCH):
                vop("memset", (Y1[:, cch * 516 + 3: cch * 516 + 4], 0.0))

            def emit_wo_pairs(u):
                use4[0] = True
                aoch = [ABA[:, k * QT:(k + 1) * QT] for k in range(8)]
                wait("t", c_ao, u["ao_done"])
                wo_last = 0
                nv = 0
                for op in range(0, 8, 2):
                    be = proj_group_nb(WS[u["wo_si"]], op, aoch,
                                       (c_w[u["wo_si"]], u["wo_n"]) if op == 0
                                       else None)
                    bo = proj_group_nb(WS[u["wo_si"]], op + 1, aoch)
                    wo_last = c_g.n
                    wait("v", c_g, wo_last)
                    if op == 0:
                        wait("v", c_xf, u["nxf"])
                        if out_done[0]:
                            wait("v", c_out, out_done[0])
                    for j, b_ in ((op, be), (op + 1, bo)):
                        nv = vop("tensor_tensor",
                                 (OUTB[:, j * QT:(j + 1) * QT], b_.ap[:, :],
                                  Y1[:, j * 516 + 4: j * 516 + 4 + QT], ALU.add),
                                 inc=(c_vg if j == op + 1 else None))
                    be.free = bo.free = (c_vg, nv)
                slot_free[u["wo_si"]] = (c_g, wo_last)
                u["aba_free"] = wo_last
                u["yadds"] = nv
                wait("v", c_vg, c_vg.n)
                ycast = 0
                for op in range(0, 8, 2):
                    if op:
                        wait("v", c_va, c_va.n)
                    for j in (op, op + 1):
                        nv = vop("tensor_tensor",
                                 (RG[:, j * QT:(j + 1) * QT],
                                  OUTB[:, j * QT:(j + 1) * QT],
                                  OUTB[:, j * QT:(j + 1) * QT], ALU.max),
                                 inc=(c_va if j == op + 1 else None))
                    wait("v", c_va, nv)
                    for j in (op, op + 1):
                        nv = vop("tensor_tensor",
                                 (XKA[:, j * QT:(j + 1) * QT],
                                  RG[:, j * QT:(j + 1) * QT],
                                  RG[:, j * QT:(j + 1) * QT], ALU.mult),
                                 inc=(c_va if j == op + 1 else None))
                    ycast = nv
                u["ycast"] = ycast

            def emit_ln1_head(u):
                if gidx[0] % 2 == 1:
                    gidx[0] += 1
                b0 = next_bank()
                wait("t", c_va, u["ycast"])
                for k in range(8):
                    mm(b0.ap[0:32, :], CSTB[:, 0:32], RG[:, k * QT:(k + 1) * QT],
                       k == 0, k == 7, tp=(0, 0))
                    mm(b0.ap[32:64, :], CSTB[:, 0:32], XKA[:, k * QT:(k + 1) * QT],
                       k == 0, k == 7, tp=(0, 32), inc=(c_g if k == 7 else None))
                nstats = c_g.n
                wait("v", c_g, nstats)
                vop("tensor_scalar", (SR[0:32, :], b0.ap[0:32, :], 1.0 / 1024.0,
                                      None, ALU.mult))
                ntsv = vop("tensor_scalar", (SR[32:64, :], b0.ap[32:64, :],
                                             1.0 / 1024.0, None, ALU.mult),
                           inc=c_va)
                wait("v", c_va, ntsv)
                vop("tensor_tensor", (SR2[0:32, :], SR[0:32, :], SR[0:32, :],
                                      ALU.mult))
                nsr = vop("tensor_copy", (SR2[32:64, :], SR[32:64, :]), inc=c_va)
                b0.free = (c_va, nsr)
                u["nsr"] = nsr

            def emit_ln_bcast_head(u, dst_y1):
                b1 = next_bank()
                wait("t", c_va, u["nsr"])
                nvar = mm(b1.ap[:, :], CSTF[0:64, 128:256], SR2[0:64, :], True,
                          True, inc=c_g)
                b0b = next_bank()
                nmean = mm(b0b.ap[:, :], CSTF[0:64, 0:128], SR[0:64, :], True,
                           True, inc=c_g)
                wait("a", c_g, nvar)
                achain_start()
                if last_nap[0]:
                    wait("a", c_va, last_nap[0])
                nln = act(LTA[:, 0:512], b1.ap[:, :], AF.Ln, bias=MUS[:, 16:17],
                          inc=c_ar)
                wait("a", c_ar, nln)
                nrs = act(LTA[:, 512:1024], LTA[:, 0:512], AF.Exp, scale=-0.5,
                          inc=c_ar)
                last_achain[0] = (c_ar, nrs)
                b1.free = (c_ar, nrs)
                return {"b0b": b0b, "nrs": nrs, "nmean": nmean, "naps": [],
                        "dst": dst_y1}

            def emit_ln_apply_pair(st, op):
                Ybuf = OUTB
                dst_y1 = st["dst"]
                if op == 0:
                    wait("v", c_g, st["nmean"])
                    wait("v", c_ar, st["nrs"])
                wait("v", c_va, c_va.n)
                for j, scr in ((op, STV), (op + 1, RECB)):
                    np1 = vop("scalar_tensor_tensor",
                              (scr[:, :], st["b0b"].ap[:, :], -1.0,
                               Ybuf[:, j * QT:(j + 1) * QT],
                               ALU.mult, ALU.add),
                              inc=(c_va if j == op + 1 else None))
                wait("v", c_va, np1)
                nap = 0
                for j, scr in ((op, STV), (op + 1, RECB)):
                    dst = (dst_y1[:, j * 516 + 4: j * 516 + 4 + QT]
                           if dst_y1 is not None
                           else Ybuf[:, j * QT:(j + 1) * QT])
                    nap = vop("tensor_tensor",
                              (dst, scr[:, :], LTA[:, 512:1024], ALU.mult),
                              inc=c_va)
                st["naps"].append(nap)
                if op == 6:
                    st["b0b"].free = (c_va, nap)
                    last_nap[0] = nap
                return nap

            def emit_ln_bcast_apply(u, dst_y1):
                st = emit_ln_bcast_head(u, dst_y1)
                for op in range(0, 8, 2):
                    emit_ln_apply_pair(st, op)
                return st["naps"][-1], st["naps"]

            def emit_mix_pair(u, op):
                wait("v", c_va, c_va.n)
                for j, scr in ((op, STV), (op + 1, RECB)):
                    nd = vop("tensor_tensor",
                             (scr[:, :], Y1[:, j * 516 + 4: j * 516 + 4 + QT],
                              Y1[:, j * 516 + 3: j * 516 + 3 + QT],
                              ALU.subtract),
                             inc=(c_va if j == op + 1 else None))
                wait("v", c_va, nd)
                for j, scr in ((op, STV), (op + 1, RECB)):
                    vop("scalar_tensor_tensor",
                        (XRA[:, j * QT:(j + 1) * QT], scr[:, :],
                         MUS[:, j:j + 1],
                         Y1[:, j * 516 + 3: j * 516 + 3 + QT],
                         ALU.mult, ALU.add))
                for j, scr in ((op, STV), (op + 1, RECB)):
                    nk = vop("scalar_tensor_tensor",
                             (XKA[:, j * QT:(j + 1) * QT], scr[:, :],
                              MUS[:, 8 + j:9 + j],
                              Y1[:, j * 516 + 3: j * 516 + 3 + QT],
                              ALU.mult, ALU.add),
                             inc=(c_va if j == op + 1 else None))
                wait("v", c_va, nk)
                if op == 6:
                    u["mix_done"] = vop(
                        "tensor_copy",
                        (bass.AP(Y1, 3, [[NCH * 516, 128], [516, 8], [1, 1]]),
                         bass.AP(Y1, 515, [[NCH * 516, 128], [516, 8], [1, 1]])),
                        inc=c_va)

            def emit_mix(u):
                for op in range(0, 8, 2):
                    emit_mix_pair(u, op)

            def make_fillers(u, load_wo=True):
                fl = []

                def r_one(oc):
                    def f():
                        if oc == 0:
                            wait("t", c_va, u["mix_done"])
                        xrch = [XRA[:, k * QT:(k + 1) * QT] for k in range(8)]
                        def sigevac(b):
                            wait("a", c_g)
                            achain_start()
                            if last_nap[0]:
                                wait("a", c_va, last_nap[0])
                            nu = act(LTA[:, 0:512], b.ap[:, :], AF.Exp,
                                     scale=-1.0, inc=c_ar)
                            b.free = (c_ar, nu)
                            wait("a", c_ar, nu)
                            nv_ = act(LTA[:, 512:1024], LTA[:, 0:512], AF.Ln,
                                      bias=MUS[:, 17:18], inc=c_ar)
                            wait("a", c_ar, nv_)
                            nr = act(RG[:, oc * QT:(oc + 1) * QT],
                                     LTA[:, 512:1024], AF.Exp, scale=-1.0,
                                     inc=c_ar)
                            last_achain[0] = (c_ar, nr)
                        u["fr_last"] = proj_group(
                            WS[u["fr_si"]], oc, xrch, sigevac,
                            (c_w[u["fr_si"]], u["fr_n"]) if oc == 0 else None)
                        if oc == 7:
                            slot_free[u["fr_si"]] = (c_g, u["fr_last"])
                            u["fv_si"], u["fv_n"] = load_weight(FV)
                    return f

                def k_pair(op):
                    def f():
                        xkch = [XKA[:, k * QT:(k + 1) * QT] for k in range(8)]
                        be = proj_group_nb(WS[u["fk_si"]], op, xkch,
                                           (c_w[u["fk_si"]], u["fk_n"])
                                           if op == 0 else None)
                        bo = proj_group_nb(WS[u["fk_si"]], op + 1, xkch)
                        u["fk_last"] = c_g.n
                        wait("v", c_g, u["fk_last"])
                        wait("v", c_va, c_va.n)
                        for j, b_, scr in ((op, be, STV), (op + 1, bo, RECB)):
                            nv = vop("tensor_single_scalar",
                                     (scr[:, :], b_.ap[:, :], 0.0, ALU.max),
                                     inc=(c_vg if j == op + 1 else None))
                        be.free = bo.free = (c_vg, nv)
                        wait("v", c_vg, nv)
                        for j, b_, scr in ((op, be, STV), (op + 1, bo, RECB)):
                            nv = vop("tensor_tensor",
                                     (XRA[:, j * QT:(j + 1) * QT], scr[:, :],
                                      scr[:, :], ALU.mult),
                                     inc=(c_va if j == op + 1 else None))
                        u["ka_ready"] = nv
                        if op == 6:
                            slot_free[u["fk_si"]] = (c_g, u["fk_last"])
                            if load_wo:
                                u["wo2_si"], u["wo2_n"] = load_weight(WO)
                    return f

                def v_pair(op):
                    def f():
                        kach = [XRA[:, k * QT:(k + 1) * QT] for k in range(8)]
                        if op == 0:
                            wait("t", c_va, u["ka_ready"])
                        be = proj_group_nb(WS[u["fv_si"]], op, kach,
                                           (c_w[u["fv_si"]], u["fv_n"])
                                           if op == 0 else None)
                        bo = proj_group_nb(WS[u["fv_si"]], op + 1, kach)
                        u["fv_last"] = c_g.n
                        wait("v", c_g, u["fv_last"])
                        wait("v", c_ar)
                        wait("v", c_va, c_va.n)
                        for j, b_, scr in ((op, be, STV), (op + 1, bo, RECB)):
                            nv = vop("tensor_tensor",
                                     (scr[:, :], RG[:, j * QT:(j + 1) * QT],
                                      b_.ap[:, :], ALU.mult),
                                     inc=(c_vg if j == op + 1 else None))
                        be.free = bo.free = (c_vg, nv)
                        wait("v", c_vg, nv)
                        for j, scr in ((op, STV), (op + 1, RECB)):
                            nv = vop("tensor_tensor",
                                     (OUTB[:, j * QT:(j + 1) * QT], scr[:, :],
                                      Y1[:, j * 516 + 4: j * 516 + 4 + QT],
                                      ALU.add),
                                     inc=(c_va if j == op + 1 else None))
                        wait("v", c_va, nv)
                        for j in (op, op + 1):
                            nv = vop("tensor_copy",
                                     (RG[:, j * QT:(j + 1) * QT],
                                      OUTB[:, j * QT:(j + 1) * QT]),
                                     inc=(c_va if j == op + 1 else None))
                        wait("v", c_va, nv)
                        for j in (op, op + 1):
                            nv = vop("tensor_tensor",
                                     (XKA[:, j * QT:(j + 1) * QT],
                                      RG[:, j * QT:(j + 1) * QT],
                                      RG[:, j * QT:(j + 1) * QT], ALU.mult),
                                     inc=(c_va if j == op + 1 else None))
                        u["y2cast"] = nv
                        if op == 6:
                            slot_free[u["fv_si"]] = (c_g, u["fv_last"])
                    return f

                def ln2_stats():
                    if gidx[0] % 2 == 1:
                        gidx[0] += 1
                    b0 = next_bank()
                    wait("t", c_va, u["y2cast"])
                    for k in range(8):
                        mm(b0.ap[0:32, :], CSTB[:, 0:32],
                           RG[:, k * QT:(k + 1) * QT], k == 0, k == 7, tp=(0, 0))
                        mm(b0.ap[32:64, :], CSTB[:, 0:32],
                           XKA[:, k * QT:(k + 1) * QT], k == 0, k == 7,
                           tp=(0, 32), inc=(c_g if k == 7 else None))
                    u["ln2_stats"] = c_g.n
                    wait("v", c_g, u["ln2_stats"])
                    vop("tensor_scalar", (SR[0:32, :], b0.ap[0:32, :],
                                          1.0 / 1024.0, None, ALU.mult))
                    ntsv = vop("tensor_scalar", (SR[32:64, :], b0.ap[32:64, :],
                                                 1.0 / 1024.0, None, ALU.mult),
                               inc=c_va)
                    wait("v", c_va, ntsv)
                    vop("tensor_tensor", (SR2[0:32, :], SR[0:32, :],
                                          SR[0:32, :], ALU.mult))
                    nsr = vop("tensor_copy", (SR2[32:64, :], SR[32:64, :]),
                              inc=c_va)
                    b0.free = (c_va, nsr)
                    u["nsr"] = nsr

                ln2s = {}

                def ln2_head():
                    ln2s["st"] = emit_ln_bcast_head(u, None)

                def ln2_pair(op):
                    def f():
                        emit_ln_apply_pair(ln2s["st"], op)
                        if op == 6:
                            u["naps"] = ln2s["st"]["naps"]
                    return f

                for oc in range(8):
                    fl.append(r_one(oc))
                for op in range(0, 8, 2):
                    fl.append(k_pair(op))
                for op in range(0, 8, 2):
                    fl.append(v_pair(op))
                fl.append(ln2_stats)
                fl.append(ln2_head)
                for op in range(0, 8, 2):
                    fl.append(ln2_pair(op))
                return fl

            def emit_ln2_tail(u, seq_, t_, unext):
                naps = u["naps"]
                xb_free[0] = u["ln2_stats"]
                if unext is not None:
                    wait("s", c_va, u["y2cast"])
                    unext["nxf"] = dma(
                        "s",
                        bass.AP(Y1, 4, [[NCH * 516, 128], [516, 8], [1, QT]]),
                        bass.AP(xT, unext["seq"] * D * S + unext["t"] * QT,
                                [[S, 128], [128 * S, 8], [1, QT]]), c_xf)
                for i, op in enumerate(range(0, 8, 2)):
                    wait("a", c_va, naps[i])
                    out_done[0] = dma(
                        "a",
                        bass.AP(outT, seq_ * D * S + op * 128 * S + t_ * QT,
                                [[S, 128], [128 * S, 2], [1, QT]]),
                        bass.AP(OUTB, op * QT,
                                [[NCH * QT, 128], [QT, 2], [1, QT]]),
                        c_out)

            def q_group(t, oc, qev, u):
                xsrc = RG if t == 0 else XRA
                xbch = [xsrc[:, k * QT:(k + 1) * QT] for k in range(8)]
                if oc == 0:
                    wait("t", c_xb, u["nxb"])
                    wait("t", c_wqr, n_wq_res)

                def qevac(b):
                    wait("v", c_g)
                    nv = vop("tensor_copy",
                             (QTB[:, oc * QT:(oc + 1) * QT], b.ap[:, :]),
                             inc=c_vg)
                    b.free = (c_vg, nv)
                qev[oc] = 0
                proj_group(WS[2], oc, xbch, qevac)
                qev[oc] = c_vg.n

            def emit_xb(t, u_prev, u):
                gate = u_prev["aba_free"] if u_prev is not None else xb_free[0]
                wait("g", c_g, gate)
                u["nxb"] = dma("g",
                               bass.AP(ABA, 0, [[NCH * QT, 128], [QT, 8], [1, QT]]),
                               bass.AP(xT, seq * D * S + t * QT,
                                       [[S, 128], [128 * S, 8], [1, QT]]), c_xb)

            def emit_attention(t, fillers, qev):
                att = {"den_q": [], "epi": None}
                use4[0] = False
                ao_base = c_ao.n
                pend = []
                step = [0]
                last_inj = [-2]

                def emit_scores_exp(hp, kt):
                    g = gkt[0]
                    sps = SA if (g % 2 == 0) else SB2
                    if kt == 0 and qev.get(hp):
                        wait("t", c_vg, qev[hp])
                    if g - 2 in exp_hist:
                        wait("t", c_p, exp_hist[g - 2])
                    qa = QTB[0:64, hp * QT:(hp + 1) * QT]
                    qb = QTB[64:128, hp * QT:(hp + 1) * QT]
                    ka = KTB[0:64, hp * S + kt * 128: hp * S + (kt + 1) * 128]
                    kb = KTB[64:128, hp * S + kt * 128: hp * S + (kt + 1) * 128]
                    mm(sps[:, 0:512], ka, qa, True, True, tp=(0, 0))
                    nsc = mm(sps[:, 512:1024], kb, qb, True, True, tp=(64, 0),
                             inc=c_sc)
                    pbs = PB[:, (g % PBS) * 1024:(g % PBS) * 1024 + 1024]
                    wait("a", c_sc, nsc)
                    if g - PBS >= 0:
                        wait("a", c_kt, g - PBS + 1)
                    npp = act(pbs, sps[:, 0:1024], AF.Exp, scale=SCALE, inc=c_p)
                    exp_hist[g] = npp
                    gkt[0] += 1
                    return g

                def emit_den(hp_, kt_, g_):
                    pbs_ = PB[:, (g_ % PBS) * 1024:(g_ % PBS) * 1024 + 1024]
                    mm(DEN[0:32, :], CSTB[:, 0:32], pbs_[:, 0:512], kt_ == 0,
                       kt_ == NKT - 1, tp=(0, 0))
                    mm(DEN[32:64, :], CSTB[:, 0:32], pbs_[:, 512:1024],
                       kt_ == 0, kt_ == NKT - 1, tp=(0, 32), inc=c_kt)

                def emit_epi():
                    nrc, nrb_res, b = att["epi"]
                    att["epi"] = None
                    wait("t", c_rc, nrc)
                    nrb = mm(b.ap[:, :], CSTF[0:64, 256:384], RCR[0:64, :],
                             True, True, inc=c_rb)
                    assert nrb == nrb_res

                def emit_pv(hp_, kt_, g_):
                    wait("t", c_p, exp_hist[g_])
                    if kt_ == 0:
                        acquire(bPV)
                    pbs_ = PB[:, (g_ % PBS) * 1024:(g_ % PBS) * 1024 + 1024]
                    va = VTB[:, kt_ * D + (2 * hp_) * 64:
                             kt_ * D + (2 * hp_) * 64 + 64]
                    vb = VTB[:, kt_ * D + (2 * hp_ + 1) * 64:
                             kt_ * D + (2 * hp_ + 1) * 64 + 64]
                    mm(PV[0:64, :], va, pbs_[:, 0:512], kt_ == 0, kt_ == NKT - 1,
                       tp=(0, 0))
                    mm(PV[64:128, :], vb, pbs_[:, 512:1024], kt_ == 0,
                       kt_ == NKT - 1, tp=(0, 64))

                def flush_pv():
                    hp, kt, g = pend.pop(0)
                    if att["epi"] is not None:
                        if att["epi_cd"] == 0:
                            emit_epi()
                        else:
                            att["epi_cd"] -= 1
                    if kt < 2:
                        att["den_q"].append((hp, kt, g))
                        return
                    if kt == 2:
                        dq = att["den_q"]
                        if dq:
                            for hp_, kt_, g_ in dq:
                                emit_pv(hp_, kt_, g_)
                            acquire(bDEN)
                            while dq:
                                emit_den(*dq.pop(0))
                    emit_pv(hp, kt, g)
                    emit_den(hp, kt, g)
                    if kt == NKT - 1:
                        wait("a", c_kt, c_kt.n)
                        achain_start()
                        if last_nrb[0]:
                            wait("a", c_rb, last_nrb[0])
                        nln = act(LTA[0:64, 0:512], DEN[0:64, :], AF.Ln,
                                  inc=c_ar)
                        wait("a", c_ar, nln)
                        nrc = act(RCR[0:64, :], LTA[0:64, 0:512], AF.Exp,
                                  scale=-1.0, inc=c_rc)
                        last_achain[0] = (c_rc, nrc)
                        bDEN.free = (c_ar, nln)
                        b = next_bank()
                        nrb_res = c_rb.n + 1
                        last_nrb[0] = nrb_res
                        att["epi"] = (nrc, nrb_res, b)
                        att["epi_cd"] = 1
                        wait("a", c_rb, nrb_res)
                        ncp = act(RSB[:, :], b.ap[:, :], AF.Copy, inc=c_ar)
                        b.free = (c_ar, ncp)
                        wait("v", c_ar, ncp)
                        nao = vop("tensor_tensor",
                                  (ABA[:, hp * QT:(hp + 1) * QT], PV[:, :],
                                   RSB[:, :], ALU.mult), inc=c_ao)
                        bPV.free = (c_ao, nao)

                for hp in range(NHP):
                    for kt in range(NKT):
                        g = emit_scores_exp(hp, kt)
                        pend.append((hp, kt, g))
                        if len(pend) >= 3:
                            flush_pv()
                        step[0] += 1
                        if (fillers and fillers[0][0] <= step[0]
                                and step[0] - last_inj[0] >= 2
                                and (att["epi"] is None
                                     or (len(fillers[0]) > 2 and fillers[0][2]))):
                            use4[0] = False
                            fillers.pop(0)[1]()
                            last_inj[0] = step[0]
                while pend:
                    flush_pv()
                if att["epi"] is not None:
                    emit_epi()
                while fillers:
                    fillers.pop(0)[1]()
                return ao_base + 8

            for t in range(NT):
                u = {"wo_si": None, "t": t, "seq": seq}
                fillers = []
                qev = {}

                def mk_q(oc, _t=t, _u=u, _qev=qev):
                    def f():
                        q_group(_t, oc, _qev, _u)
                    return f

                if prev is None:
                    wait("s", c_va, n_y1_init)
                    u["nxf"] = dma(
                        "s",
                        bass.AP(Y1, 4, [[NCH * 516, 128], [516, 8], [1, QT]]),
                        bass.AP(xT, seq * D * S + t * QT,
                                [[S, 128], [128 * S, 8], [1, QT]]), c_xf)
                if prev is not None and t > 0:
                    emit_wo_pairs(prev)
                u["nxb"] = kv_nxb0[0] if t == 0 else xra_nxb[0]
                use4[0] = True
                q_group(t, 0, qev, u)
                for oc in range(1, 8):
                    fillers.append((2 * oc - 1, mk_q(oc)))

                if prev is not None:
                    def u_loads(p=prev):
                        p["fr_si"], p["fr_n"] = load_weight(FR)
                        p["fk_si"], p["fk_n"] = load_weight(FK)
                    fillers.append((15, u_loads, True))
                    if t > 0:
                        def u_stats(p=prev):
                            emit_ln1_head(p)
                        lnst = {}
                        def u_bhead(p=prev, s=lnst):
                            s["st"] = emit_ln_bcast_head(p, Y1)
                        def mk_ap(op, s=lnst):
                            return lambda: emit_ln_apply_pair(s["st"], op)
                        def mk_mx(op, p=prev):
                            return lambda: emit_mix_pair(p, op)
                        fillers.append((17, u_stats))
                        fillers.append((19, u_bhead))
                        for i, op in enumerate(range(0, 8, 2)):
                            fillers.append((21 + 4 * i, mk_ap(op), True))
                            fillers.append((23 + 4 * i, mk_mx(op), True))
                        steps = [50 + 3 * i for i in range(17)]
                        steps += [100 + 2 * i for i in range(5)]
                    else:
                        steps = [24 + 3 * i for i in range(17)]
                        steps += [75 + 2 * i for i in range(5)]
                    mkfl = make_fillers(prev, load_wo=True)
                    for i, (st, f) in enumerate(zip(steps, mkfl)):
                        fillers.append((st, f, i >= len(mkfl) - 4))

                u["ao_done"] = emit_attention(t, fillers, qev)
                if prev is not None:
                    u["wo_si"], u["wo_n"] = prev["wo2_si"], prev["wo2_n"]
                    emit_ln2_tail(prev, prev["seq"], prev["t"], u)
                else:
                    wo_si0, wo_n0 = load_weight(WO)
                    u["wo_si"], u["wo_n"] = wo_si0, wo_n0
                if t < NT - 1:
                    wait("g", c_xb, c_xb.n)
                    if prev is not None:
                        wait("g", c_g, prev["fv_last"])
                    xra_nxb[0] = dma(
                        "g",
                        bass.AP(XRA, 0, [[NCH * QT, 128], [QT, 8], [1, QT]]),
                        bass.AP(xT, seq * D * S + (t + 1) * QT,
                                [[S, 128], [128 * S, 8], [1, QT]]), c_xb)
                prev = u

        emit_wo_pairs(prev)
        last_ycast[0] = prev["yadds"]
        prev["fr_si"], prev["fr_n"] = load_weight(FR)
        prev["fk_si"], prev["fk_n"] = load_weight(FK)
        emit_ln1_head(prev)
        emit_ln_bcast_apply(prev, Y1)
        emit_mix(prev)
        for f in make_fillers(prev, load_wo=False):
            f()
        emit_ln2_tail(prev, prev["seq"], prev["t"], None)

        wait("s", c_out, c_out.n)

        with nc.Block() as block:
            def runner(key):
                def _run(e):
                    for f in plan[key]:
                        f(e)
                return _run
            block.tensor(runner("t"))
            block.scalar(runner("a"))
            block.vector(runner("v"))
            block.sync(runner("s"))
            block.gpsimd(runner("g"))

    return nc


_NC_CACHE = [None]


def kernel(L0_attn_x, L0_attn_kernel, L0_attn_WO, L0_ffn_W_r, L0_ffn_W_k,
           L0_ffn_W_v, L0_ffn_mu_r, L0_ffn_mu_k):
    x = np.asarray(L0_attn_x, dtype=np.float32)
    ker = np.asarray(L0_attn_kernel, dtype=np.float32)
    wo = np.asarray(L0_attn_WO, dtype=np.float32)[0]
    wr = np.asarray(L0_ffn_W_r, dtype=np.float32)
    wk = np.asarray(L0_ffn_W_k, dtype=np.float32)
    wv = np.asarray(L0_ffn_W_v, dtype=np.float32)
    mur = np.asarray(L0_ffn_mu_r, dtype=np.float32)
    muk = np.asarray(L0_ffn_mu_k, dtype=np.float32)

    B = x.shape[0]
    ncore = 8
    wq_l = np.ascontiguousarray(ker[0].transpose(1, 0, 2).reshape(D, D))
    wk_l = np.ascontiguousarray(ker[1].transpose(1, 0, 2).reshape(D, D))
    wv_l = np.ascontiguousarray(ker[2].transpose(1, 0, 2).reshape(D, D))
    wo_l = np.ascontiguousarray(wo.reshape(64, 16, D).transpose(1, 0, 2).reshape(D, D))
    wts = np.stack([wk_l, wv_l, wq_l, wo_l,
                    np.ascontiguousarray(wr.T),
                    np.ascontiguousarray(wk.T),
                    np.ascontiguousarray(wv.T)], axis=0)

    mus = np.zeros((128, 18), np.float32)
    for c in range(8):
        mus[:, c] = mur[c * 128:(c + 1) * 128]
        mus[:, 8 + c] = muk[c * 128:(c + 1) * 128]
    mus[:, 16] = EPS
    mus[:, 17] = 1.0

    cst = np.zeros((128, 416), np.float32)
    cst[:, 0:32] = 1.0
    cst[0, 32:160] = 1.0
    cst[0, 160:288] = -1.0
    cst[32, 160:288] = 1.0
    cst[0, 288:352] = 1.0
    cst[32, 352:416] = 1.0

    xT = np.ascontiguousarray(x.transpose(0, 2, 1))

    if _NC_CACHE[0] is None:
        _NC_CACHE[0] = build_nc()
    nc = _NC_CACHE[0]

    in_maps = [{"xT": np.ascontiguousarray(xT[2 * c:2 * c + 2]),
                "wts": wts, "mus": mus, "cst": cst} for c in range(ncore)]
    res = run_bass_kernel_spmd(nc, in_maps, core_ids=list(range(ncore)))
    out = np.empty((B, S, D), np.float32)
    for c in range(ncore):
        o = res.results[c]["outT"]
        out[2 * c] = o[0].T
        out[2 * c + 1] = o[1].T
    return out


# version 30
# speedup vs baseline: 1.4060x; 1.0336x over previous
import contextlib
import numpy as np
import ml_dtypes
import concourse.bass as bass
import concourse.mybir as mybir
from concourse.bass_utils import run_bass_kernel_spmd

F32 = mybir.dt.float32
BF16 = mybir.dt.bfloat16
AF = mybir.ActivationFunctionType
ALU = mybir.AluOpType

D = 1024
S = 2048
NSEQ = 2
NCH = 8
QT = 512
NT = S // QT
NHP = 8
NKT = S // 128
PBS = 3
SCALE = 1.0 / 65536.0
EPS = 1e-5

WK, WV, WQ, WO, FR, FK, FV = range(7)


class Cnt:
    def __init__(self, name):
        self.name = name
        self.n = 0


def build_nc():
    nc = bass.Bass("TRN2", target_bir_lowering=False)

    xT = nc.dram_tensor("xT", [NSEQ, D, S], F32, kind="ExternalInput")
    wts = nc.dram_tensor("wts", [7, D, D], F32, kind="ExternalInput")
    mus = nc.dram_tensor("mus", [128, 18], F32, kind="ExternalInput")
    cst = nc.dram_tensor("cst", [128, 416], F32, kind="ExternalInput")
    outT = nc.dram_tensor("outT", [NSEQ, D, S], F32, kind="ExternalOutput")
    wsc = nc.dram_tensor("wsc", [7 * D, D], BF16, kind="Internal")

    plan = {k: [] for k in "tavsg"}
    sems = {}
    sem_handles = {}

    def em(eng, fn):
        plan[eng].append(fn)

    def C(name):
        c = Cnt(name)
        sems[name] = c
        return c

    c_cstb = C("c_cstb")
    c_wqr = C("c_wqr")
    c_cst = C("c_cst")
    c_wc0 = C("c_wc0")
    c_wc1 = C("c_wc1")
    c_ws0 = C("c_ws0")
    c_ws1 = C("c_ws1")
    c_w0 = C("c_w0")
    c_w1 = C("c_w1")
    c_xf = C("c_xf")
    c_xb = C("c_xb")
    c_out = C("c_out")
    c_sc = C("c_sc")
    c_p = C("c_p")
    c_kt = C("c_kt")
    c_rc = C("c_rc")
    c_g = C("c_g")
    c_vg = C("c_vg")
    c_ao = C("c_ao")
    c_rb = C("c_rb")
    c_tf = C("c_tf")
    c_va = C("c_va")
    c_ar = C("c_ar")

    def wait(eng, c, val=None):
        v = c.n if val is None else val
        if v <= 0:
            return
        em(eng, lambda e, _n=c.name, _v=v: e.wait_ge(sem_handles[_n], _v))

    def bump(c, by=1):
        c.n += by
        return c.n

    def dma(eng, dst, src, c):
        n = bump(c, 16)
        em(eng, lambda e, _d=dst, _s=src, _n=c.name:
           e.dma_start(_d, _s).then_inc(sem_handles[_n], 16))
        return n

    def mm(out, lhsT, rhs, start, stop, tp=None, inc=None):
        n = bump(inc) if inc is not None else None
        def f(e, _o=out, _l=lhsT, _r=rhs, _st=start, _sp=stop, _tp=tp, _inc=inc):
            i = e.matmul(_o, _l, _r, start=_st, stop=_sp, tile_position=_tp,
                         skip_group_check=True)
            if _inc is not None:
                i.then_inc(sem_handles[_inc.name], 1)
        em("t", f)
        return n

    def act(out, in_, func, scale=1.0, bias=0.0, inc=None):
        n = bump(inc) if inc is not None else None
        def f(e, _o=out, _i=in_, _f=func, _s=scale, _b=bias, _inc=inc):
            i = e.activation(_o, _i, _f, scale=_s, bias=_b)
            if _inc is not None:
                i.then_inc(sem_handles[_inc.name], 1)
        em("a", f)
        return n

    def vop(name, args, inc=None):
        n = bump(inc) if inc is not None else None
        def f(e, _nm=name, _a=args, _inc=inc):
            i = getattr(e, _nm)(*_a)
            if _inc is not None:
                i.then_inc(sem_handles[_inc.name], 1)
        em("v", f)
        return n

    def tnop_inc(c):
        n = bump(c)
        em("t", lambda e, _n=c.name: e.nop().then_inc(sem_handles[_n], 1))
        return n

    with contextlib.ExitStack() as ctx:
        sbt = lambda name, shape, dt: ctx.enter_context(nc.sbuf_tensor(name, shape, dt))
        pst = lambda name, shape: ctx.enter_context(nc.psum_tensor(name, shape, F32))

        WS = [sbt("ws0", [128, 8 * D], BF16), sbt("ws1", [128, 8 * D], BF16),
              sbt("ws2", [128, 8 * D], BF16)]
        KTB = sbt("ktb", [128, NHP * S], BF16)
        VTB = sbt("vtb", [128, NKT * D], BF16)
        QTB = sbt("qtb", [128, NCH * QT], BF16)
        PB = sbt("pb", [128, PBS * 1024], BF16)
        ABA = sbt("aba", [128, NCH * QT], BF16)
        XKA = sbt("xka", [128, NCH * QT], BF16)
        XRA = sbt("xra", [128, NCH * QT], BF16)
        RG = sbt("rg", [128, NCH * QT], BF16)
        OUTB = sbt("outb", [128, NCH * QT], F32)
        Y1 = sbt("y1", [128, NCH * 516], F32)
        STV = sbt("stv", [128, QT], F32)
        LTA = sbt("lta", [128, 1024], F32)
        RSB = sbt("rsb", [128, QT], F32)
        RECB = sbt("recb", [128, QT], F32)
        RCR = sbt("rcr", [64, QT], F32)
        SR = sbt("sr", [64, QT], F32)
        SR2 = sbt("sr2", [64, QT], F32)
        CSTF = sbt("cstf", [64, 384], F32)
        CSTB = sbt("cstb", [128, 32], BF16)
        MUS = sbt("musb", [128, 18], F32)

        SA = pst("psa", [128, 1024])
        SB2 = pst("psb", [128, 1024])
        PV = pst("pspv", [128, 512])
        DEN = pst("psden", [128, 512])
        G0 = pst("psg0", [128, 512])
        G1 = pst("psg1", [128, 512])

        for name in sems:
            sem_handles[name] = ctx.enter_context(nc.semaphore(name))

        class Bank:
            def __init__(self, ap):
                self.ap = ap
                self.free = None
        bG = [Bank(G0), Bank(G1)]
        bPV = Bank(PV)
        bDEN = Bank(DEN)
        pool4 = [bG[0], bG[1], bPV, bDEN]

        def acquire(b):
            if b.free is not None:
                wait("t", b.free[0], b.free[1])
                b.free = None

        gidx = [0]
        hidx = [0]
        use4 = [False]

        def next_bank():
            if use4[0]:
                b = pool4[hidx[0] % 4]
                hidx[0] += 1
            else:
                b = bG[gidx[0] % 2]
                gidx[0] += 1
            acquire(b)
            return b

        def proj_group(ws_slot, oc, rhs_chunks, evac, w_ready=None):
            b = next_bank()
            if w_ready is not None:
                wait("t", w_ready[0], w_ready[1])
            for k in range(8):
                lhsT = ws_slot[:, k * D + oc * 128: k * D + (oc + 1) * 128]
                mm(b.ap[:, :], lhsT, rhs_chunks[k], start=(k == 0), stop=(k == 7),
                   inc=(c_g if k == 7 else None))
            evac(b)
            return c_g.n

        def proj_group_nb(ws_slot, oc, rhs_chunks, w_ready=None):
            b = next_bank()
            if w_ready is not None:
                wait("t", w_ready[0], w_ready[1])
            for k in range(8):
                lhsT = ws_slot[:, k * D + oc * 128: k * D + (oc + 1) * 128]
                mm(b.ap[:, :], lhsT, rhs_chunks[k], start=(k == 0), stop=(k == 7),
                   inc=(c_g if k == 7 else None))
            return b

        def vnop():
            em("v", lambda e: e.engine_nop())

        def layer_norm(YBF, YSQ, Ybuf, dst_y1, ycast_n, free_aba_after_stats,
                       ycast_sem=None):
            if gidx[0] % 2 == 1:
                gidx[0] += 1
            b0 = next_bank()
            wait("t", ycast_sem if ycast_sem is not None else c_vg, ycast_n)
            for k in range(8):
                mm(b0.ap[0:32, :], CSTB[:, 0:32], YBF[:, k * QT:(k + 1) * QT],
                   k == 0, k == 7, tp=(0, 0))
                mm(b0.ap[32:64, :], CSTB[:, 0:32], YSQ[:, k * QT:(k + 1) * QT],
                   k == 0, k == 7, tp=(0, 32), inc=(c_g if k == 7 else None))
            nstats = c_g.n
            wait("v", c_g, nstats)
            vop("tensor_scalar", (SR[0:32, :], b0.ap[0:32, :], 1.0 / 1024.0, None,
                                  ALU.mult))
            ntsv = vop("tensor_scalar", (SR[32:64, :], b0.ap[32:64, :], 1.0 / 1024.0,
                                         None, ALU.mult), inc=c_va)
            wait("v", c_va, ntsv)
            vop("tensor_tensor", (SR2[0:32, :], SR[0:32, :], SR[0:32, :], ALU.mult))
            nsr = vop("tensor_copy", (SR2[32:64, :], SR[32:64, :]), inc=c_va)
            b0.free = (c_va, nsr)
            b1 = next_bank()
            wait("t", c_va, nsr)
            nvar = mm(b1.ap[:, :], CSTF[0:64, 128:256], SR2[0:64, :], True, True,
                      inc=c_g)
            b0b = next_bank()
            nmean = mm(b0b.ap[:, :], CSTF[0:64, 0:128], SR[0:64, :], True, True,
                       inc=c_g)
            wait("a", c_g, nvar)
            achain_start()
            if last_nap[0]:
                wait("a", c_va, last_nap[0])
            nln = act(LTA[:, 0:512], b1.ap[:, :], AF.Ln, bias=MUS[:, 16:17],
                      inc=c_ar)
            wait("a", c_ar, nln)
            nrs = act(RSB[:, :], LTA[:, 0:512], AF.Exp, scale=-0.5, inc=c_ar)
            last_achain[0] = (c_ar, nrs)
            b1.free = (c_ar, nrs)
            wait("v", c_g, nmean)
            wait("v", c_ar, nrs)
            nap = 0
            for op in range(0, 8, 2):
                if op:
                    wait("v", c_va, c_va.n)
                for j, scr in ((op, STV), (op + 1, RECB)):
                    np1 = vop("scalar_tensor_tensor",
                              (scr[:, :], b0b.ap[:, :], -1.0,
                               Ybuf[:, j * QT:(j + 1) * QT],
                               ALU.mult, ALU.add),
                              inc=(c_va if j == op + 1 else None))
                wait("v", c_va, np1)
                for j, scr in ((op, STV), (op + 1, RECB)):
                    dst = (dst_y1[:, j * 516 + 4: j * 516 + 4 + QT]
                           if dst_y1 is not None else Ybuf[:, j * QT:(j + 1) * QT])
                    nap = vop("tensor_tensor", (dst, scr[:, :], RSB[:, :], ALU.mult),
                              inc=c_va)
            b0b.free = (c_va, nap)
            last_nap[0] = nap
            return nap, nstats

        n_y1_init = vop("memset", (Y1[:, :], 0.0), inc=c_va)
        dma("g", CSTB[:, :], cst[:, 0:32], c_cstb)
        n_wq_res = dma("g", bass.AP(WS[2], 0, [[8 * D, 128], [D, 8], [1, D]]),
                       bass.AP(wts, WQ * D * D, [[D, 128], [128 * D, 8], [1, D]]),
                       c_wqr)
        dma("s", CSTF[:, :], cst[0:64, 32:416], c_cst)
        dma("s", MUS[:, :], mus[:, :], c_cst)
        wait("t", c_cstb, 16)
        wait("t", c_cst, 32)
        wait("v", c_cst, 32)

        c_wc = [c_wc0, c_wc1]
        c_wsx = [c_ws0, c_ws1]
        c_w = [c_w0, c_w1]
        kv0_gate = [0, 0]

        def emit_wcast(m):
            slot = WS[m % 2]
            if m >= 2:
                wait("g", c_wsx[m % 2], c_wsx[m % 2].n)
                wait("g", c_g, kv0_gate[m % 2])
            nld = dma("g", bass.AP(slot, 0, [[8 * D, 128], [D, 8], [1, D]]),
                      bass.AP(wts, m * D * D, [[D, 128], [128 * D, 8], [1, D]]),
                      c_wc[m % 2])
            wait("s", c_wc[m % 2], nld)
            dma("s", bass.AP(wsc, m * D * D, [[D, 128], [128 * D, 8], [1, D]]),
                bass.AP(slot, 0, [[8 * D, 128], [D, 8], [1, D]]), c_wsx[m % 2])
            return nld

        wk0_n = emit_wcast(0)
        wv0_n = emit_wcast(1)

        wnext = [0]
        scratch_all = [(c_ws0, c_ws0.n), (c_ws1, c_ws1.n)]
        slot_free = [None, None]
        first_load = [True]

        def load_weight(mat):
            si = wnext[0]
            wnext[0] = 1 - si
            if first_load[0]:
                wait("s", scratch_all[0][0], scratch_all[0][1])
                wait("s", scratch_all[1][0], scratch_all[1][1])
                first_load[0] = False
            fa = slot_free[si]
            if fa is not None:
                wait("s", fa[0], fa[1])
            n = dma("s", bass.AP(WS[si], 0, [[8 * D, 128], [D, 8], [1, D]]),
                    bass.AP(wsc, mat * D * D, [[D, 128], [128 * D, 8], [1, D]]),
                    c_w[si])
            return si, n

        exp_hist = {}
        den_hist = {}
        last_achain = [None]
        last_nrb = [0]
        last_nap = [0]

        def achain_start():
            if last_achain[0] is not None:
                wait("a", last_achain[0][0], last_achain[0][1])
        gkt = [0]
        xb_free = [0]
        out_done = [0]
        prev = None
        last_ycast = [0]

        for seq in range(NSEQ):
            def emit_kv(kv_fillers):
                use4[0] = True
                if seq == 0:
                    wk_si, wv_si = 0, 1
                    wait("t", c_wc[0], wk0_n)
                    wait("t", c_wc[1], wv0_n)
                else:
                    wk_si, wk_n = load_weight(WK)
                    wv_si, wv_n = load_weight(WV)
                    wait("t", c_w[wk_si], wk_n)
                    wait("t", c_w[wv_si], wv_n)
                wk_last = wv_last = 0
                order = (1, 2, 3, 0)
                dsts = {1: ABA, 2: RG, 3: ABA, 0: RG}
                buf_free = {id(ABA): (c_g, xb_free[0]), id(RG): None}
                loads = {}

                def emit_load(tb):
                    buf = dsts[tb]
                    wait("g", c_xb, c_xb.n)
                    fa = buf_free[id(buf)]
                    if fa is not None:
                        wait("g", fa[0], fa[1])
                    loads[tb] = dma(
                        "g",
                        bass.AP(buf, 0, [[NCH * QT, 128], [QT, 8], [1, QT]]),
                        bass.AP(xT, seq * D * S + tb * QT,
                                [[S, 128], [128 * S, 8], [1, QT]]), c_xb)

                emit_load(1)
                for idx, tb in enumerate(order):
                    buf = dsts[tb]
                    nxb = loads[tb]
                    if tb == 0:
                        kv_nxb0[0] = nxb
                    wait("t", c_xb, nxb)
                    xbch = [buf[:, k * QT:(k + 1) * QT] for k in range(8)]
                    for hp in range(NHP):
                        b = next_bank()
                        for k in range(8):
                            mm(b.ap[:, :],
                               WS[wk_si][:, k * D + hp * 128: k * D + (hp + 1) * 128],
                               xbch[k], start=(k == 0), stop=(k == 7),
                               inc=(c_g if k == 7 else None))
                        wk_last = c_g.n
                        wait("v", c_g)
                        nv = vop("tensor_copy",
                                 (KTB[:, hp * S + tb * QT: hp * S + (tb + 1) * QT],
                                  b.ap[:, :]), inc=c_vg)
                        b.free = (c_vg, nv)
                    if kv_fillers:
                        kv_fillers.pop(0)()
                    if idx + 1 < len(order):
                        if buf_free[id(RG)] is None:
                            buf_free[id(RG)] = (c_g, c_g.n)
                        emit_load(order[idx + 1])
                    for tt in range(4):
                        for half in range(2):
                            b = next_bank()
                            for k in range(8):
                                mm(b.ap[:, :], xbch[k][:, tt * 128:(tt + 1) * 128],
                                   WS[wv_si][:, k * D + half * 512:
                                             k * D + half * 512 + 512],
                                   start=(k == 0), stop=(k == 7),
                                   inc=(c_g if k == 7 else None))
                            wv_last = c_g.n
                            kti = tb * 4 + tt
                            wait("v", c_g)
                            nv = vop("tensor_copy",
                                     (VTB[:, kti * D + half * 512:
                                          kti * D + half * 512 + 512],
                                      b.ap[:, :]), inc=c_vg)
                            b.free = (c_vg, nv)
                    if kv_fillers:
                        kv_fillers.pop(0)()
                    buf_free[id(buf)] = (c_g, wv_last)
                    xb_free[0] = wv_last
                slot_free[wk_si] = (c_g, wk_last)
                slot_free[wv_si] = (c_g, wv_last)
                if seq == 0:
                    kv0_gate[0], kv0_gate[1] = wk_last, wv_last
                    for m in range(2, 7):
                        emit_wcast(m)
                    scratch_all[0] = (c_ws0, c_ws0.n)
                    scratch_all[1] = (c_ws1, c_ws1.n)
                while kv_fillers:
                    kv_fillers.pop(0)()
                use4[0] = False

            kvf = []
            kv_nxb0 = [0]
            xra_nxb = [0]
            if prev is not None:
                emit_wo_pairs(prev)
                last_ycast[0] = prev["yadds"]
                xb_free[0] = prev["aba_free"]
                kvf = [lambda: emit_ln1_head(prev),
                       lambda: (emit_ln_bcast_apply(prev, Y1)[0], emit_mix(prev))]
            emit_kv(kvf)

            wait("v", c_va, c_va.n)
            for cch in range(NCH):
                vop("memset", (Y1[:, cch * 516 + 3: cch * 516 + 4], 0.0))

            def emit_wo_pairs(u):
                use4[0] = True
                aoch = [ABA[:, k * QT:(k + 1) * QT] for k in range(8)]
                wait("t", c_ao, u["ao_done"])
                wo_last = 0
                nv = 0
                for op in range(0, 8, 2):
                    be = proj_group_nb(WS[u["wo_si"]], op, aoch,
                                       (c_w[u["wo_si"]], u["wo_n"]) if op == 0
                                       else None)
                    bo = proj_group_nb(WS[u["wo_si"]], op + 1, aoch)
                    wo_last = c_g.n
                    wait("v", c_g, wo_last)
                    if op == 0:
                        wait("v", c_xf, u["nxf"])
                        if out_done[0]:
                            wait("v", c_out, out_done[0])
                    for j, b_ in ((op, be), (op + 1, bo)):
                        nv = vop("tensor_tensor",
                                 (OUTB[:, j * QT:(j + 1) * QT], b_.ap[:, :],
                                  Y1[:, j * 516 + 4: j * 516 + 4 + QT], ALU.add),
                                 inc=(c_vg if j == op + 1 else None))
                    be.free = bo.free = (c_vg, nv)
                slot_free[u["wo_si"]] = (c_g, wo_last)
                u["aba_free"] = wo_last
                u["yadds"] = nv
                wait("v", c_vg, c_vg.n)
                ycast = 0
                for op in range(0, 8, 2):
                    if op:
                        wait("v", c_va, c_va.n)
                    for j in (op, op + 1):
                        nv = vop("tensor_tensor",
                                 (RG[:, j * QT:(j + 1) * QT],
                                  OUTB[:, j * QT:(j + 1) * QT],
                                  OUTB[:, j * QT:(j + 1) * QT], ALU.max),
                                 inc=(c_va if j == op + 1 else None))
                    wait("v", c_va, nv)
                    for j in (op, op + 1):
                        nv = vop("tensor_tensor",
                                 (XKA[:, j * QT:(j + 1) * QT],
                                  RG[:, j * QT:(j + 1) * QT],
                                  RG[:, j * QT:(j + 1) * QT], ALU.mult),
                                 inc=(c_va if j == op + 1 else None))
                    ycast = nv
                u["ycast"] = ycast

            def emit_ln1_head(u):
                if gidx[0] % 2 == 1:
                    gidx[0] += 1
                b0 = next_bank()
                wait("t", c_va, u["ycast"])
                for k in range(8):
                    mm(b0.ap[0:32, :], CSTB[:, 0:32], RG[:, k * QT:(k + 1) * QT],
                       k == 0, k == 7, tp=(0, 0))
                    mm(b0.ap[32:64, :], CSTB[:, 0:32], XKA[:, k * QT:(k + 1) * QT],
                       k == 0, k == 7, tp=(0, 32), inc=(c_g if k == 7 else None))
                nstats = c_g.n
                wait("v", c_g, nstats)
                vop("tensor_scalar", (SR[0:32, :], b0.ap[0:32, :], 1.0 / 1024.0,
                                      None, ALU.mult))
                ntsv = vop("tensor_scalar", (SR[32:64, :], b0.ap[32:64, :],
                                             1.0 / 1024.0, None, ALU.mult),
                           inc=c_va)
                wait("v", c_va, ntsv)
                vop("tensor_tensor", (SR2[0:32, :], SR[0:32, :], SR[0:32, :],
                                      ALU.mult))
                nsr = vop("tensor_copy", (SR2[32:64, :], SR[32:64, :]), inc=c_va)
                b0.free = (c_va, nsr)
                u["nsr"] = nsr

            def emit_ln_bcast_head(u, dst_y1):
                b1 = next_bank()
                wait("t", c_va, u["nsr"])
                nvar = mm(b1.ap[:, :], CSTF[0:64, 128:256], SR2[0:64, :], True,
                          True, inc=c_g)
                b0b = next_bank()
                nmean = mm(b0b.ap[:, :], CSTF[0:64, 0:128], SR[0:64, :], True,
                           True, inc=c_g)
                wait("a", c_g, nvar)
                achain_start()
                if last_nap[0]:
                    wait("a", c_va, last_nap[0])
                nln = act(LTA[:, 0:512], b1.ap[:, :], AF.Ln, bias=MUS[:, 16:17],
                          inc=c_ar)
                wait("a", c_ar, nln)
                nrs = act(LTA[:, 512:1024], LTA[:, 0:512], AF.Exp, scale=-0.5,
                          inc=c_ar)
                last_achain[0] = (c_ar, nrs)
                b1.free = (c_ar, nrs)
                return {"b0b": b0b, "nrs": nrs, "nmean": nmean, "naps": [],
                        "dst": dst_y1}

            def emit_ln_apply_pair(st, op):
                Ybuf = OUTB
                dst_y1 = st["dst"]
                if op == 0:
                    wait("v", c_g, st["nmean"])
                    wait("v", c_ar, st["nrs"])
                wait("v", c_va, c_va.n)
                for j, scr in ((op, STV), (op + 1, RECB)):
                    np1 = vop("scalar_tensor_tensor",
                              (scr[:, :], st["b0b"].ap[:, :], -1.0,
                               Ybuf[:, j * QT:(j + 1) * QT],
                               ALU.mult, ALU.add),
                              inc=(c_va if j == op + 1 else None))
                wait("v", c_va, np1)
                nap = 0
                for j, scr in ((op, STV), (op + 1, RECB)):
                    dst = (dst_y1[:, j * 516 + 4: j * 516 + 4 + QT]
                           if dst_y1 is not None
                           else Ybuf[:, j * QT:(j + 1) * QT])
                    nap = vop("tensor_tensor",
                              (dst, scr[:, :], LTA[:, 512:1024], ALU.mult),
                              inc=c_va)
                st["naps"].append(nap)
                if op == 6:
                    st["b0b"].free = (c_va, nap)
                    last_nap[0] = nap
                return nap

            def emit_ln_bcast_apply(u, dst_y1):
                st = emit_ln_bcast_head(u, dst_y1)
                for op in range(0, 8, 2):
                    emit_ln_apply_pair(st, op)
                return st["naps"][-1], st["naps"]

            def emit_mix_pair(u, op):
                wait("v", c_va, c_va.n)
                for j, scr in ((op, STV), (op + 1, RECB)):
                    nd = vop("tensor_tensor",
                             (scr[:, :], Y1[:, j * 516 + 4: j * 516 + 4 + QT],
                              Y1[:, j * 516 + 3: j * 516 + 3 + QT],
                              ALU.subtract),
                             inc=(c_va if j == op + 1 else None))
                wait("v", c_va, nd)
                for j, scr in ((op, STV), (op + 1, RECB)):
                    vop("scalar_tensor_tensor",
                        (XRA[:, j * QT:(j + 1) * QT], scr[:, :],
                         MUS[:, j:j + 1],
                         Y1[:, j * 516 + 3: j * 516 + 3 + QT],
                         ALU.mult, ALU.add))
                for j, scr in ((op, STV), (op + 1, RECB)):
                    nk = vop("scalar_tensor_tensor",
                             (XKA[:, j * QT:(j + 1) * QT], scr[:, :],
                              MUS[:, 8 + j:9 + j],
                              Y1[:, j * 516 + 3: j * 516 + 3 + QT],
                              ALU.mult, ALU.add),
                             inc=(c_va if j == op + 1 else None))
                wait("v", c_va, nk)
                if op == 6:
                    u["mix_done"] = vop(
                        "tensor_copy",
                        (bass.AP(Y1, 3, [[NCH * 516, 128], [516, 8], [1, 1]]),
                         bass.AP(Y1, 515, [[NCH * 516, 128], [516, 8], [1, 1]])),
                        inc=c_va)

            def emit_mix(u):
                for op in range(0, 8, 2):
                    emit_mix_pair(u, op)

            def make_fillers(u, load_wo=True):
                fl = []

                def r_one(oc):
                    def f():
                        if oc == 0:
                            wait("t", c_va, u["mix_done"])
                        xrch = [XRA[:, k * QT:(k + 1) * QT] for k in range(8)]
                        def sigevac(b):
                            wait("a", c_g)
                            achain_start()
                            if last_nap[0]:
                                wait("a", c_va, last_nap[0])
                            nu = act(LTA[:, 0:512], b.ap[:, :], AF.Exp,
                                     scale=-1.0, inc=c_ar)
                            b.free = (c_ar, nu)
                            wait("a", c_ar, nu)
                            nv_ = act(LTA[:, 512:1024], LTA[:, 0:512], AF.Ln,
                                      bias=MUS[:, 17:18], inc=c_ar)
                            wait("a", c_ar, nv_)
                            nr = act(RG[:, oc * QT:(oc + 1) * QT],
                                     LTA[:, 512:1024], AF.Exp, scale=-1.0,
                                     inc=c_ar)
                            last_achain[0] = (c_ar, nr)
                        u["fr_last"] = proj_group(
                            WS[u["fr_si"]], oc, xrch, sigevac,
                            (c_w[u["fr_si"]], u["fr_n"]) if oc == 0 else None)
                        if oc == 7:
                            slot_free[u["fr_si"]] = (c_g, u["fr_last"])
                            u["fv_si"], u["fv_n"] = load_weight(FV)
                    return f

                def k_pair(op):
                    def f():
                        xkch = [XKA[:, k * QT:(k + 1) * QT] for k in range(8)]
                        be = proj_group_nb(WS[u["fk_si"]], op, xkch,
                                           (c_w[u["fk_si"]], u["fk_n"])
                                           if op == 0 else None)
                        bo = proj_group_nb(WS[u["fk_si"]], op + 1, xkch)
                        u["fk_last"] = c_g.n
                        wait("v", c_g, u["fk_last"])
                        wait("v", c_va, c_va.n)
                        for j, b_, scr in ((op, be, STV), (op + 1, bo, RECB)):
                            nv = vop("tensor_single_scalar",
                                     (scr[:, :], b_.ap[:, :], 0.0, ALU.max),
                                     inc=(c_vg if j == op + 1 else None))
                        be.free = bo.free = (c_vg, nv)
                        wait("v", c_vg, nv)
                        for j, b_, scr in ((op, be, STV), (op + 1, bo, RECB)):
                            nv = vop("tensor_tensor",
                                     (XRA[:, j * QT:(j + 1) * QT], scr[:, :],
                                      scr[:, :], ALU.mult),
                                     inc=(c_va if j == op + 1 else None))
                        u["ka_ready"] = nv
                        if op == 6:
                            slot_free[u["fk_si"]] = (c_g, u["fk_last"])
                            if load_wo:
                                u["wo2_si"], u["wo2_n"] = load_weight(WO)
                    return f

                def v_pair(op):
                    def f():
                        kach = [XRA[:, k * QT:(k + 1) * QT] for k in range(8)]
                        if op == 0:
                            wait("t", c_va, u["ka_ready"])
                        be = proj_group_nb(WS[u["fv_si"]], op, kach,
                                           (c_w[u["fv_si"]], u["fv_n"])
                                           if op == 0 else None)
                        bo = proj_group_nb(WS[u["fv_si"]], op + 1, kach)
                        u["fv_last"] = c_g.n
                        wait("v", c_g, u["fv_last"])
                        wait("v", c_ar)
                        wait("v", c_va, c_va.n)
                        for j, b_, scr in ((op, be, STV), (op + 1, bo, RECB)):
                            nv = vop("tensor_tensor",
                                     (scr[:, :], RG[:, j * QT:(j + 1) * QT],
                                      b_.ap[:, :], ALU.mult),
                                     inc=(c_vg if j == op + 1 else None))
                        be.free = bo.free = (c_vg, nv)
                        wait("v", c_vg, nv)
                        for j, scr in ((op, STV), (op + 1, RECB)):
                            nv = vop("tensor_tensor",
                                     (OUTB[:, j * QT:(j + 1) * QT], scr[:, :],
                                      Y1[:, j * 516 + 4: j * 516 + 4 + QT],
                                      ALU.add),
                                     inc=(c_va if j == op + 1 else None))
                        wait("v", c_va, nv)
                        for j in (op, op + 1):
                            nv = vop("tensor_copy",
                                     (RG[:, j * QT:(j + 1) * QT],
                                      OUTB[:, j * QT:(j + 1) * QT]),
                                     inc=(c_va if j == op + 1 else None))
                        wait("v", c_va, nv)
                        for j in (op, op + 1):
                            nv = vop("tensor_tensor",
                                     (XKA[:, j * QT:(j + 1) * QT],
                                      RG[:, j * QT:(j + 1) * QT],
                                      RG[:, j * QT:(j + 1) * QT], ALU.mult),
                                     inc=(c_va if j == op + 1 else None))
                        u["y2cast"] = nv
                        if op == 6:
                            slot_free[u["fv_si"]] = (c_g, u["fv_last"])
                    return f

                def ln2_stats():
                    if gidx[0] % 2 == 1:
                        gidx[0] += 1
                    b0 = next_bank()
                    wait("t", c_va, u["y2cast"])
                    for k in range(8):
                        mm(b0.ap[0:32, :], CSTB[:, 0:32],
                           RG[:, k * QT:(k + 1) * QT], k == 0, k == 7, tp=(0, 0))
                        mm(b0.ap[32:64, :], CSTB[:, 0:32],
                           XKA[:, k * QT:(k + 1) * QT], k == 0, k == 7,
                           tp=(0, 32), inc=(c_g if k == 7 else None))
                    u["ln2_stats"] = c_g.n
                    wait("v", c_g, u["ln2_stats"])
                    vop("tensor_scalar", (SR[0:32, :], b0.ap[0:32, :],
                                          1.0 / 1024.0, None, ALU.mult))
                    ntsv = vop("tensor_scalar", (SR[32:64, :], b0.ap[32:64, :],
                                                 1.0 / 1024.0, None, ALU.mult),
                               inc=c_va)
                    wait("v", c_va, ntsv)
                    vop("tensor_tensor", (SR2[0:32, :], SR[0:32, :],
                                          SR[0:32, :], ALU.mult))
                    nsr = vop("tensor_copy", (SR2[32:64, :], SR[32:64, :]),
                              inc=c_va)
                    b0.free = (c_va, nsr)
                    u["nsr"] = nsr

                ln2s = {}

                def ln2_head():
                    ln2s["st"] = emit_ln_bcast_head(u, None)

                def ln2_pair(op):
                    def f():
                        emit_ln_apply_pair(ln2s["st"], op)
                        if op == 6:
                            u["naps"] = ln2s["st"]["naps"]
                    return f

                for oc in range(8):
                    fl.append(r_one(oc))
                for op in range(0, 8, 2):
                    fl.append(k_pair(op))
                for op in range(0, 8, 2):
                    fl.append(v_pair(op))
                fl.append(ln2_stats)
                fl.append(ln2_head)
                for op in range(0, 8, 2):
                    fl.append(ln2_pair(op))
                return fl

            def emit_ln2_tail(u, seq_, t_, unext):
                naps = u["naps"]
                xb_free[0] = u["ln2_stats"]
                if unext is not None:
                    wait("s", c_va, u["y2cast"])
                    unext["nxf"] = dma(
                        "s",
                        bass.AP(Y1, 4, [[NCH * 516, 128], [516, 8], [1, QT]]),
                        bass.AP(xT, unext["seq"] * D * S + unext["t"] * QT,
                                [[S, 128], [128 * S, 8], [1, QT]]), c_xf)
                for i, op in enumerate(range(0, 8, 2)):
                    wait("a", c_va, naps[i])
                    out_done[0] = dma(
                        "a",
                        bass.AP(outT, seq_ * D * S + op * 128 * S + t_ * QT,
                                [[S, 128], [128 * S, 2], [1, QT]]),
                        bass.AP(OUTB, op * QT,
                                [[NCH * QT, 128], [QT, 2], [1, QT]]),
                        c_out)

            def q_group(t, oc, qev, u):
                xsrc = RG if t == 0 else XRA
                xbch = [xsrc[:, k * QT:(k + 1) * QT] for k in range(8)]
                if oc == 0:
                    wait("t", c_xb, u["nxb"])
                    wait("t", c_wqr, n_wq_res)

                def qevac(b):
                    wait("v", c_g)
                    nv = vop("tensor_copy",
                             (QTB[:, oc * QT:(oc + 1) * QT], b.ap[:, :]),
                             inc=c_vg)
                    b.free = (c_vg, nv)
                qev[oc] = 0
                proj_group(WS[2], oc, xbch, qevac)
                qev[oc] = c_vg.n

            def emit_xb(t, u_prev, u):
                gate = u_prev["aba_free"] if u_prev is not None else xb_free[0]
                wait("g", c_g, gate)
                u["nxb"] = dma("g",
                               bass.AP(ABA, 0, [[NCH * QT, 128], [QT, 8], [1, QT]]),
                               bass.AP(xT, seq * D * S + t * QT,
                                       [[S, 128], [128 * S, 8], [1, QT]]), c_xb)

            def emit_attention(t, fillers, qev):
                att = {"den_q": [], "epi": None}
                use4[0] = False
                ao_base = c_ao.n
                pend = []
                step = [0]
                last_inj = [-2]

                def emit_scores_exp(hp, kt):
                    g = gkt[0]
                    sps = SA if (g % 2 == 0) else SB2
                    if kt == 0 and qev.get(hp):
                        wait("t", c_vg, qev[hp])
                    if g - 2 in exp_hist:
                        wait("t", c_p, exp_hist[g - 2])
                    qa = QTB[0:64, hp * QT:(hp + 1) * QT]
                    qb = QTB[64:128, hp * QT:(hp + 1) * QT]
                    ka = KTB[0:64, hp * S + kt * 128: hp * S + (kt + 1) * 128]
                    kb = KTB[64:128, hp * S + kt * 128: hp * S + (kt + 1) * 128]
                    mm(sps[:, 0:512], ka, qa, True, True, tp=(0, 0))
                    nsc = mm(sps[:, 512:1024], kb, qb, True, True, tp=(64, 0),
                             inc=c_sc)
                    pbs = PB[:, (g % PBS) * 1024:(g % PBS) * 1024 + 1024]
                    wait("a", c_sc, nsc)
                    if g - PBS >= 0:
                        wait("a", c_kt, g - PBS + 1)
                    npp = act(pbs, sps[:, 0:1024], AF.Exp, scale=SCALE, inc=c_p)
                    exp_hist[g] = npp
                    gkt[0] += 1
                    return g

                def emit_den(hp_, kt_, g_):
                    pbs_ = PB[:, (g_ % PBS) * 1024:(g_ % PBS) * 1024 + 1024]
                    mm(DEN[0:32, :], CSTB[:, 0:32], pbs_[:, 0:512], kt_ == 0,
                       kt_ == NKT - 1, tp=(0, 0))
                    mm(DEN[32:64, :], CSTB[:, 0:32], pbs_[:, 512:1024],
                       kt_ == 0, kt_ == NKT - 1, tp=(0, 32), inc=c_kt)

                def emit_epi():
                    nrc, nrb_res, b = att["epi"]
                    att["epi"] = None
                    wait("t", c_rc, nrc)
                    nrb = mm(b.ap[:, :], CSTF[0:64, 256:384], RCR[0:64, :],
                             True, True, inc=c_rb)
                    assert nrb == nrb_res

                def emit_pv(hp_, kt_, g_):
                    wait("t", c_p, exp_hist[g_])
                    if kt_ == 0:
                        acquire(bPV)
                    pbs_ = PB[:, (g_ % PBS) * 1024:(g_ % PBS) * 1024 + 1024]
                    va = VTB[:, kt_ * D + (2 * hp_) * 64:
                             kt_ * D + (2 * hp_) * 64 + 64]
                    vb = VTB[:, kt_ * D + (2 * hp_ + 1) * 64:
                             kt_ * D + (2 * hp_ + 1) * 64 + 64]
                    mm(PV[0:64, :], va, pbs_[:, 0:512], kt_ == 0, kt_ == NKT - 1,
                       tp=(0, 0))
                    mm(PV[64:128, :], vb, pbs_[:, 512:1024], kt_ == 0,
                       kt_ == NKT - 1, tp=(0, 64))

                def flush_pv():
                    hp, kt, g = pend.pop(0)
                    if att["epi"] is not None:
                        if att["epi_cd"] == 0:
                            emit_epi()
                        else:
                            att["epi_cd"] -= 1
                    if kt < 2:
                        att["den_q"].append((hp, kt, g))
                        return
                    if kt == 2:
                        dq = att["den_q"]
                        if dq:
                            for hp_, kt_, g_ in dq:
                                emit_pv(hp_, kt_, g_)
                            acquire(bDEN)
                            while dq:
                                emit_den(*dq.pop(0))
                    emit_pv(hp, kt, g)
                    emit_den(hp, kt, g)
                    if kt == NKT - 1:
                        wait("a", c_kt, c_kt.n)
                        achain_start()
                        if last_nrb[0]:
                            wait("a", c_rb, last_nrb[0])
                        nln = act(LTA[0:64, 0:512], DEN[0:64, :], AF.Ln,
                                  inc=c_ar)
                        wait("a", c_ar, nln)
                        nrc = act(RCR[0:64, :], LTA[0:64, 0:512], AF.Exp,
                                  scale=-1.0, inc=c_rc)
                        last_achain[0] = (c_rc, nrc)
                        bDEN.free = (c_ar, nln)
                        b = next_bank()
                        nrb_res = c_rb.n + 1
                        last_nrb[0] = nrb_res
                        att["epi"] = (nrc, nrb_res, b)
                        att["epi_cd"] = 1
                        wait("a", c_rb, nrb_res)
                        ncp = act(RSB[:, :], b.ap[:, :], AF.Copy, inc=c_ar)
                        b.free = (c_ar, ncp)
                        wait("v", c_ar, ncp)
                        nao = vop("tensor_tensor",
                                  (ABA[:, hp * QT:(hp + 1) * QT], PV[:, :],
                                   RSB[:, :], ALU.mult), inc=c_ao)
                        bPV.free = (c_ao, nao)

                for hp in range(NHP):
                    for kt in range(NKT):
                        g = emit_scores_exp(hp, kt)
                        pend.append((hp, kt, g))
                        if len(pend) >= 3:
                            flush_pv()
                        step[0] += 1
                        if (fillers and fillers[0][0] <= step[0]
                                and step[0] - last_inj[0] >= 2
                                and (att["epi"] is None
                                     or (len(fillers[0]) > 2 and fillers[0][2]))):
                            use4[0] = False
                            fillers.pop(0)[1]()
                            last_inj[0] = step[0]
                while pend:
                    flush_pv()
                if att["epi"] is not None:
                    emit_epi()
                while fillers:
                    fillers.pop(0)[1]()
                return ao_base + 8

            for t in range(NT):
                u = {"wo_si": None, "t": t, "seq": seq}
                fillers = []
                qev = {}

                def mk_q(oc, _t=t, _u=u, _qev=qev):
                    def f():
                        q_group(_t, oc, _qev, _u)
                    return f

                if prev is None:
                    wait("s", c_va, n_y1_init)
                    u["nxf"] = dma(
                        "s",
                        bass.AP(Y1, 4, [[NCH * 516, 128], [516, 8], [1, QT]]),
                        bass.AP(xT, seq * D * S + t * QT,
                                [[S, 128], [128 * S, 8], [1, QT]]), c_xf)
                if prev is not None and t > 0:
                    emit_wo_pairs(prev)
                u["nxb"] = kv_nxb0[0] if t == 0 else xra_nxb[0]
                use4[0] = True
                q_group(t, 0, qev, u)
                for oc in range(1, 8):
                    fillers.append((2 * oc - 1, mk_q(oc)))

                if prev is not None:
                    def u_loads(p=prev):
                        p["fr_si"], p["fr_n"] = load_weight(FR)
                        p["fk_si"], p["fk_n"] = load_weight(FK)
                    fillers.append((15, u_loads, True))
                    if t > 0:
                        def u_stats(p=prev):
                            emit_ln1_head(p)
                        lnst = {}
                        def u_bhead(p=prev, s=lnst):
                            s["st"] = emit_ln_bcast_head(p, Y1)
                        def mk_ap(op, s=lnst):
                            return lambda: emit_ln_apply_pair(s["st"], op)
                        def mk_mx(op, p=prev):
                            return lambda: emit_mix_pair(p, op)
                        fillers.append((17, u_stats))
                        fillers.append((19, u_bhead))
                        for i, op in enumerate(range(0, 8, 2)):
                            fillers.append((21 + 4 * i, mk_ap(op), True))
                            fillers.append((23 + 4 * i, mk_mx(op), True))
                        steps = [50 + 3 * i for i in range(17)]
                        steps += [100 + 2 * i for i in range(5)]
                    else:
                        steps = [24 + 3 * i for i in range(17)]
                        steps += [75 + 2 * i for i in range(5)]
                    mkfl = make_fillers(prev, load_wo=True)
                    for i, (st, f) in enumerate(zip(steps, mkfl)):
                        fillers.append((st, f, i >= len(mkfl) - 4))

                u["ao_done"] = emit_attention(t, fillers, qev)
                if prev is not None:
                    u["wo_si"], u["wo_n"] = prev["wo2_si"], prev["wo2_n"]
                    emit_ln2_tail(prev, prev["seq"], prev["t"], u)
                else:
                    wo_si0, wo_n0 = load_weight(WO)
                    u["wo_si"], u["wo_n"] = wo_si0, wo_n0
                if t < NT - 1:
                    wait("g", c_xb, c_xb.n)
                    if prev is not None:
                        wait("g", c_g, prev["fv_last"])
                    xra_nxb[0] = dma(
                        "g",
                        bass.AP(XRA, 0, [[NCH * QT, 128], [QT, 8], [1, QT]]),
                        bass.AP(xT, seq * D * S + (t + 1) * QT,
                                [[S, 128], [128 * S, 8], [1, QT]]), c_xb)
                prev = u

        emit_wo_pairs(prev)
        last_ycast[0] = prev["yadds"]
        prev["fr_si"], prev["fr_n"] = load_weight(FR)
        prev["fk_si"], prev["fk_n"] = load_weight(FK)
        emit_ln1_head(prev)
        emit_ln_bcast_apply(prev, Y1)
        emit_mix(prev)
        for f in make_fillers(prev, load_wo=False):
            f()
        emit_ln2_tail(prev, prev["seq"], prev["t"], None)

        wait("s", c_out, c_out.n)

        with nc.Block() as block:
            def runner(key):
                def _run(e):
                    for f in plan[key]:
                        f(e)
                return _run
            block.tensor(runner("t"))
            block.scalar(runner("a"))
            block.vector(runner("v"))
            block.sync(runner("s"))
            block.gpsimd(runner("g"))

    return nc


_NC_CACHE = [None]


def kernel(L0_attn_x, L0_attn_kernel, L0_attn_WO, L0_ffn_W_r, L0_ffn_W_k,
           L0_ffn_W_v, L0_ffn_mu_r, L0_ffn_mu_k):
    x = np.asarray(L0_attn_x, dtype=np.float32)
    ker = np.asarray(L0_attn_kernel, dtype=np.float32)
    wo = np.asarray(L0_attn_WO, dtype=np.float32)[0]
    wr = np.asarray(L0_ffn_W_r, dtype=np.float32)
    wk = np.asarray(L0_ffn_W_k, dtype=np.float32)
    wv = np.asarray(L0_ffn_W_v, dtype=np.float32)
    mur = np.asarray(L0_ffn_mu_r, dtype=np.float32)
    muk = np.asarray(L0_ffn_mu_k, dtype=np.float32)

    B = x.shape[0]
    ncore = 8
    wq_l = np.ascontiguousarray(ker[0].transpose(1, 0, 2).reshape(D, D))
    wk_l = np.ascontiguousarray(ker[1].transpose(1, 0, 2).reshape(D, D))
    wv_l = np.ascontiguousarray(ker[2].transpose(1, 0, 2).reshape(D, D))
    wo_l = np.ascontiguousarray(wo.reshape(64, 16, D).transpose(1, 0, 2).reshape(D, D))
    wts = np.stack([wk_l, wv_l, wq_l, wo_l,
                    np.ascontiguousarray(wr.T),
                    np.ascontiguousarray(wk.T),
                    np.ascontiguousarray(wv.T)], axis=0)

    mus = np.zeros((128, 18), np.float32)
    for c in range(8):
        mus[:, c] = mur[c * 128:(c + 1) * 128]
        mus[:, 8 + c] = muk[c * 128:(c + 1) * 128]
    mus[:, 16] = EPS
    mus[:, 17] = 1.0

    cst = np.zeros((128, 416), np.float32)
    cst[:, 0:32] = 1.0
    cst[0, 32:160] = 1.0
    cst[0, 160:288] = -1.0
    cst[32, 160:288] = 1.0
    cst[0, 288:352] = 1.0
    cst[32, 352:416] = 1.0

    xT = np.ascontiguousarray(x.transpose(0, 2, 1))

    if _NC_CACHE[0] is None:
        _NC_CACHE[0] = build_nc()
    nc = _NC_CACHE[0]

    in_maps = [{"xT": np.ascontiguousarray(xT[2 * c:2 * c + 2]),
                "wts": wts, "mus": mus, "cst": cst} for c in range(ncore)]
    res = run_bass_kernel_spmd(nc, in_maps, core_ids=list(range(ncore)))
    out = np.empty((B, S, D), np.float32)
    for c in range(ncore):
        o = res.results[c]["outT"]
        out[2 * c] = o[0].T
        out[2 * c + 1] = o[1].T
    return out


# version 32
# speedup vs baseline: 1.4231x; 1.0121x over previous
import contextlib
import numpy as np
import ml_dtypes
import concourse.bass as bass
import concourse.mybir as mybir
from concourse.bass_utils import run_bass_kernel_spmd

F32 = mybir.dt.float32
BF16 = mybir.dt.bfloat16
AF = mybir.ActivationFunctionType
ALU = mybir.AluOpType

D = 1024
S = 2048
NSEQ = 2
NCH = 8
QT = 512
NT = S // QT
NHP = 8
NKT = S // 128
PBS = 3
SCALE = 1.0 / 65536.0
EPS = 1e-5

WK, WV, WQ, WO, FR, FK, FV = range(7)


class Cnt:
    def __init__(self, name):
        self.name = name
        self.n = 0


def build_nc():
    nc = bass.Bass("TRN2", target_bir_lowering=False)

    xT = nc.dram_tensor("xT", [NSEQ, D, S], F32, kind="ExternalInput")
    wts = nc.dram_tensor("wts", [7, D, D], F32, kind="ExternalInput")
    mus = nc.dram_tensor("mus", [128, 18], F32, kind="ExternalInput")
    cst = nc.dram_tensor("cst", [128, 416], F32, kind="ExternalInput")
    outT = nc.dram_tensor("outT", [NSEQ, D, S], F32, kind="ExternalOutput")
    wsc = nc.dram_tensor("wsc", [7 * D, D], BF16, kind="Internal")

    plan = {k: [] for k in "tavsg"}
    sems = {}
    sem_handles = {}

    def em(eng, fn):
        plan[eng].append(fn)

    def C(name):
        c = Cnt(name)
        sems[name] = c
        return c

    c_cstb = C("c_cstb")
    c_wqr = C("c_wqr")
    c_cst = C("c_cst")
    c_wc0 = C("c_wc0")
    c_wc1 = C("c_wc1")
    c_ws0 = C("c_ws0")
    c_ws1 = C("c_ws1")
    c_w0 = C("c_w0")
    c_w1 = C("c_w1")
    c_xf = C("c_xf")
    c_xb = C("c_xb")
    c_out = C("c_out")
    c_sc = C("c_sc")
    c_p = C("c_p")
    c_kt = C("c_kt")
    c_rc = C("c_rc")
    c_g = C("c_g")
    c_vg = C("c_vg")
    c_ao = C("c_ao")
    c_rb = C("c_rb")
    c_tf = C("c_tf")
    c_va = C("c_va")
    c_ar = C("c_ar")

    def wait(eng, c, val=None):
        v = c.n if val is None else val
        if v <= 0:
            return
        em(eng, lambda e, _n=c.name, _v=v: e.wait_ge(sem_handles[_n], _v))

    def bump(c, by=1):
        c.n += by
        return c.n

    def dma(eng, dst, src, c):
        n = bump(c, 16)
        em(eng, lambda e, _d=dst, _s=src, _n=c.name:
           e.dma_start(_d, _s).then_inc(sem_handles[_n], 16))
        return n

    def mm(out, lhsT, rhs, start, stop, tp=None, inc=None):
        n = bump(inc) if inc is not None else None
        def f(e, _o=out, _l=lhsT, _r=rhs, _st=start, _sp=stop, _tp=tp, _inc=inc):
            i = e.matmul(_o, _l, _r, start=_st, stop=_sp, tile_position=_tp,
                         skip_group_check=True)
            if _inc is not None:
                i.then_inc(sem_handles[_inc.name], 1)
        em("t", f)
        return n

    def act(out, in_, func, scale=1.0, bias=0.0, inc=None):
        n = bump(inc) if inc is not None else None
        def f(e, _o=out, _i=in_, _f=func, _s=scale, _b=bias, _inc=inc):
            i = e.activation(_o, _i, _f, scale=_s, bias=_b)
            if _inc is not None:
                i.then_inc(sem_handles[_inc.name], 1)
        em("a", f)
        return n

    def vop(name, args, inc=None):
        n = bump(inc) if inc is not None else None
        def f(e, _nm=name, _a=args, _inc=inc):
            i = getattr(e, _nm)(*_a)
            if _inc is not None:
                i.then_inc(sem_handles[_inc.name], 1)
        em("v", f)
        return n

    def tnop_inc(c):
        n = bump(c)
        em("t", lambda e, _n=c.name: e.nop().then_inc(sem_handles[_n], 1))
        return n

    with contextlib.ExitStack() as ctx:
        sbt = lambda name, shape, dt: ctx.enter_context(nc.sbuf_tensor(name, shape, dt))
        pst = lambda name, shape: ctx.enter_context(nc.psum_tensor(name, shape, F32))

        WS = [sbt("ws0", [128, 8 * D], BF16), sbt("ws1", [128, 8 * D], BF16),
              sbt("ws2", [128, 8 * D], BF16)]
        KTB = sbt("ktb", [128, NHP * S], BF16)
        VTB = sbt("vtb", [128, NKT * D], BF16)
        QTB = sbt("qtb", [128, NCH * QT], BF16)
        PB = sbt("pb", [128, PBS * 1024], BF16)
        ABA = sbt("aba", [128, NCH * QT], BF16)
        XKA = sbt("xka", [128, NCH * QT], BF16)
        XRA = sbt("xra", [128, NCH * QT], BF16)
        RG = sbt("rg", [128, NCH * QT], BF16)
        OUTB = sbt("outb", [128, NCH * QT], F32)
        Y1 = sbt("y1", [128, NCH * 516], F32)
        STV = sbt("stv", [128, QT], F32)
        LTA = sbt("lta", [128, 1024], F32)
        RSB = sbt("rsb", [128, QT], F32)
        RECB = sbt("recb", [128, QT], F32)
        RCR = sbt("rcr", [64, QT], F32)
        SR = sbt("sr", [64, QT], F32)
        SR2 = sbt("sr2", [64, QT], F32)
        CSTF = sbt("cstf", [64, 384], F32)
        CSTB = sbt("cstb", [128, 32], BF16)
        MUS = sbt("musb", [128, 18], F32)

        SA = pst("psa", [128, 1024])
        SB2 = pst("psb", [128, 1024])
        PV = pst("pspv", [128, 512])
        DEN = pst("psden", [128, 512])
        G0 = pst("psg0", [128, 512])
        G1 = pst("psg1", [128, 512])

        for name in sems:
            sem_handles[name] = ctx.enter_context(nc.semaphore(name))

        class Bank:
            def __init__(self, ap):
                self.ap = ap
                self.free = None
        bG = [Bank(G0), Bank(G1)]
        bPV = Bank(PV)
        bDEN = Bank(DEN)
        pool4 = [bG[0], bG[1], bPV, bDEN]

        def acquire(b):
            if b.free is not None:
                wait("t", b.free[0], b.free[1])
                b.free = None

        gidx = [0]
        hidx = [0]
        use4 = [False]

        def next_bank():
            if use4[0]:
                b = pool4[hidx[0] % 4]
                hidx[0] += 1
            else:
                b = bG[gidx[0] % 2]
                gidx[0] += 1
            acquire(b)
            return b

        def proj_group(ws_slot, oc, rhs_chunks, evac, w_ready=None):
            b = next_bank()
            if w_ready is not None:
                wait("t", w_ready[0], w_ready[1])
            for k in range(8):
                lhsT = ws_slot[:, k * D + oc * 128: k * D + (oc + 1) * 128]
                mm(b.ap[:, :], lhsT, rhs_chunks[k], start=(k == 0), stop=(k == 7),
                   inc=(c_g if k == 7 else None))
            evac(b)
            return c_g.n

        def proj_group_nb(ws_slot, oc, rhs_chunks, w_ready=None):
            b = next_bank()
            if w_ready is not None:
                wait("t", w_ready[0], w_ready[1])
            for k in range(8):
                lhsT = ws_slot[:, k * D + oc * 128: k * D + (oc + 1) * 128]
                mm(b.ap[:, :], lhsT, rhs_chunks[k], start=(k == 0), stop=(k == 7),
                   inc=(c_g if k == 7 else None))
            return b

        def vnop():
            em("v", lambda e: e.engine_nop())

        def layer_norm(YBF, YSQ, Ybuf, dst_y1, ycast_n, free_aba_after_stats,
                       ycast_sem=None):
            if gidx[0] % 2 == 1:
                gidx[0] += 1
            b0 = next_bank()
            wait("t", ycast_sem if ycast_sem is not None else c_vg, ycast_n)
            for k in range(8):
                mm(b0.ap[0:32, :], CSTB[:, 0:32], YBF[:, k * QT:(k + 1) * QT],
                   k == 0, k == 7, tp=(0, 0))
                mm(b0.ap[32:64, :], CSTB[:, 0:32], YSQ[:, k * QT:(k + 1) * QT],
                   k == 0, k == 7, tp=(0, 32), inc=(c_g if k == 7 else None))
            nstats = c_g.n
            wait("v", c_g, nstats)
            vop("tensor_scalar", (SR[0:32, :], b0.ap[0:32, :], 1.0 / 1024.0, None,
                                  ALU.mult))
            ntsv = vop("tensor_scalar", (SR[32:64, :], b0.ap[32:64, :], 1.0 / 1024.0,
                                         None, ALU.mult), inc=c_va)
            wait("v", c_va, ntsv)
            vop("tensor_tensor", (SR2[0:32, :], SR[0:32, :], SR[0:32, :], ALU.mult))
            nsr = vop("tensor_copy", (SR2[32:64, :], SR[32:64, :]), inc=c_va)
            b0.free = (c_va, nsr)
            b1 = next_bank()
            wait("t", c_va, nsr)
            nvar = mm(b1.ap[:, :], CSTF[0:64, 128:256], SR2[0:64, :], True, True,
                      inc=c_g)
            b0b = next_bank()
            nmean = mm(b0b.ap[:, :], CSTF[0:64, 0:128], SR[0:64, :], True, True,
                       inc=c_g)
            wait("a", c_g, nvar)
            achain_start()
            if last_nap[0]:
                wait("a", c_va, last_nap[0])
            nln = act(LTA[:, 0:512], b1.ap[:, :], AF.Ln, bias=MUS[:, 16:17],
                      inc=c_ar)
            wait("a", c_ar, nln)
            nrs = act(RSB[:, :], LTA[:, 0:512], AF.Exp, scale=-0.5, inc=c_ar)
            last_achain[0] = (c_ar, nrs)
            b1.free = (c_ar, nrs)
            wait("v", c_g, nmean)
            wait("v", c_ar, nrs)
            nap = 0
            for op in range(0, 8, 2):
                if op:
                    wait("v", c_va, c_va.n)
                for j, scr in ((op, STV), (op + 1, RECB)):
                    np1 = vop("scalar_tensor_tensor",
                              (scr[:, :], b0b.ap[:, :], -1.0,
                               Ybuf[:, j * QT:(j + 1) * QT],
                               ALU.mult, ALU.add),
                              inc=(c_va if j == op + 1 else None))
                wait("v", c_va, np1)
                for j, scr in ((op, STV), (op + 1, RECB)):
                    dst = (dst_y1[:, j * 516 + 4: j * 516 + 4 + QT]
                           if dst_y1 is not None else Ybuf[:, j * QT:(j + 1) * QT])
                    nap = vop("tensor_tensor", (dst, scr[:, :], RSB[:, :], ALU.mult),
                              inc=c_va)
            b0b.free = (c_va, nap)
            last_nap[0] = nap
            return nap, nstats

        n_y1_init = vop("memset", (Y1[:, :], 0.0), inc=c_va)
        dma("g", CSTB[:, :], cst[:, 0:32], c_cstb)
        n_wq_res = dma("g", bass.AP(WS[2], 0, [[8 * D, 128], [D, 8], [1, D]]),
                       bass.AP(wts, WQ * D * D, [[D, 128], [128 * D, 8], [1, D]]),
                       c_wqr)
        dma("s", CSTF[:, :], cst[0:64, 32:416], c_cst)
        dma("s", MUS[:, :], mus[:, :], c_cst)
        wait("t", c_cstb, 16)
        wait("t", c_cst, 32)
        wait("v", c_cst, 32)

        c_wc = [c_wc0, c_wc1]
        c_wsx = [c_ws0, c_ws1]
        c_w = [c_w0, c_w1]
        kv0_gate = [0, 0]

        def emit_wcast(m):
            slot = WS[m % 2]
            if m >= 2:
                wait("g", c_wsx[m % 2], c_wsx[m % 2].n)
                wait("g", c_g, kv0_gate[m % 2])
            nld = dma("g", bass.AP(slot, 0, [[8 * D, 128], [D, 8], [1, D]]),
                      bass.AP(wts, m * D * D, [[D, 128], [128 * D, 8], [1, D]]),
                      c_wc[m % 2])
            wait("s", c_wc[m % 2], nld)
            dma("s", bass.AP(wsc, m * D * D, [[D, 128], [128 * D, 8], [1, D]]),
                bass.AP(slot, 0, [[8 * D, 128], [D, 8], [1, D]]), c_wsx[m % 2])
            return nld

        wk0_n = emit_wcast(0)
        wv0_n = emit_wcast(1)

        wnext = [0]
        scratch_all = [(c_ws0, c_ws0.n), (c_ws1, c_ws1.n)]
        slot_free = [None, None]
        first_load = [True]

        def load_weight(mat):
            si = wnext[0]
            wnext[0] = 1 - si
            if first_load[0]:
                wait("s", scratch_all[0][0], scratch_all[0][1])
                wait("s", scratch_all[1][0], scratch_all[1][1])
                first_load[0] = False
            fa = slot_free[si]
            if fa is not None:
                wait("s", fa[0], fa[1])
            n = dma("s", bass.AP(WS[si], 0, [[8 * D, 128], [D, 8], [1, D]]),
                    bass.AP(wsc, mat * D * D, [[D, 128], [128 * D, 8], [1, D]]),
                    c_w[si])
            return si, n

        exp_hist = {}
        den_hist = {}
        last_achain = [None]
        last_nrb = [0]
        last_nap = [0]

        def achain_start():
            if last_achain[0] is not None:
                wait("a", last_achain[0][0], last_achain[0][1])
        gkt = [0]
        xb_free = [0]
        out_done = [0]
        prev = None
        last_ycast = [0]

        for seq in range(NSEQ):
            def emit_kv(kv_fillers):
                use4[0] = True
                if seq == 0:
                    wk_si, wv_si = 0, 1
                    wait("t", c_wc[0], wk0_n)
                    wait("t", c_wc[1], wv0_n)
                else:
                    wk_si, wk_n = load_weight(WK)
                    wv_si, wv_n = load_weight(WV)
                    wait("t", c_w[wk_si], wk_n)
                    wait("t", c_w[wv_si], wv_n)
                wk_last = wv_last = 0
                order = (1, 2, 3, 0)
                dsts = {1: ABA, 2: RG, 3: ABA, 0: RG}
                buf_free = {id(ABA): (c_g, xb_free[0]), id(RG): None}
                loads = {}

                def emit_load(tb):
                    buf = dsts[tb]
                    wait("g", c_xb, c_xb.n)
                    fa = buf_free[id(buf)]
                    if fa is not None:
                        wait("g", fa[0], fa[1])
                    loads[tb] = dma(
                        "g",
                        bass.AP(buf, 0, [[NCH * QT, 128], [QT, 8], [1, QT]]),
                        bass.AP(xT, seq * D * S + tb * QT,
                                [[S, 128], [128 * S, 8], [1, QT]]), c_xb)

                emit_load(1)
                for idx, tb in enumerate(order):
                    buf = dsts[tb]
                    nxb = loads[tb]
                    if tb == 0:
                        kv_nxb0[0] = nxb
                    wait("t", c_xb, nxb)
                    xbch = [buf[:, k * QT:(k + 1) * QT] for k in range(8)]
                    for hp in range(NHP):
                        b = next_bank()
                        for k in range(8):
                            mm(b.ap[:, :],
                               WS[wk_si][:, k * D + hp * 128: k * D + (hp + 1) * 128],
                               xbch[k], start=(k == 0), stop=(k == 7),
                               inc=(c_g if k == 7 else None))
                        wk_last = c_g.n
                        wait("v", c_g)
                        nv = vop("tensor_copy",
                                 (KTB[:, hp * S + tb * QT: hp * S + (tb + 1) * QT],
                                  b.ap[:, :]), inc=c_vg)
                        b.free = (c_vg, nv)
                    if kv_fillers:
                        kv_fillers.pop(0)()
                    if idx + 1 < len(order):
                        if buf_free[id(RG)] is None:
                            buf_free[id(RG)] = (c_g, c_g.n)
                        emit_load(order[idx + 1])
                    for tt in range(4):
                        for half in range(2):
                            b = next_bank()
                            for k in range(8):
                                mm(b.ap[:, :], xbch[k][:, tt * 128:(tt + 1) * 128],
                                   WS[wv_si][:, k * D + half * 512:
                                             k * D + half * 512 + 512],
                                   start=(k == 0), stop=(k == 7),
                                   inc=(c_g if k == 7 else None))
                            wv_last = c_g.n
                            kti = tb * 4 + tt
                            wait("v", c_g)
                            nv = vop("tensor_copy",
                                     (VTB[:, kti * D + half * 512:
                                          kti * D + half * 512 + 512],
                                      b.ap[:, :]), inc=c_vg)
                            b.free = (c_vg, nv)
                    if kv_fillers:
                        kv_fillers.pop(0)()
                    buf_free[id(buf)] = (c_g, wv_last)
                    xb_free[0] = wv_last
                slot_free[wk_si] = (c_g, wk_last)
                slot_free[wv_si] = (c_g, wv_last)
                if seq == 0:
                    kv0_gate[0], kv0_gate[1] = wk_last, wv_last
                    for m in range(2, 7):
                        emit_wcast(m)
                    scratch_all[0] = (c_ws0, c_ws0.n)
                    scratch_all[1] = (c_ws1, c_ws1.n)
                while kv_fillers:
                    kv_fillers.pop(0)()
                use4[0] = False

            kvf = []
            kv_nxb0 = [0]
            xra_nxb = [0]
            if prev is not None:
                emit_wo_pairs(prev)
                last_ycast[0] = prev["yadds"]
                xb_free[0] = prev["aba_free"]
                def kv_mx(op, p=prev):
                    return lambda: emit_mix_pair(p, op)

                kvf = [lambda p=prev: emit_ln1_head(p),
                       lambda p=prev: emit_ln_bcast_apply(p, Y1)]
                for op in range(0, 8, 2):
                    kvf.append(kv_mx(op))
            emit_kv(kvf)

            wait("v", c_va, c_va.n)
            for cch in range(NCH):
                vop("memset", (Y1[:, cch * 516 + 3: cch * 516 + 4], 0.0))

            def emit_wo_pairs(u):
                use4[0] = True
                aoch = [ABA[:, k * QT:(k + 1) * QT] for k in range(8)]
                wait("t", c_ao, u["ao_done"])
                wo_last = 0
                nv = 0
                for op in range(0, 8, 2):
                    be = proj_group_nb(WS[u["wo_si"]], op, aoch,
                                       (c_w[u["wo_si"]], u["wo_n"]) if op == 0
                                       else None)
                    bo = proj_group_nb(WS[u["wo_si"]], op + 1, aoch)
                    wo_last = c_g.n
                    wait("v", c_g, wo_last)
                    if op == 0:
                        wait("v", c_xf, u["nxf"])
                        if out_done[0]:
                            wait("v", c_out, out_done[0])
                    for j, b_ in ((op, be), (op + 1, bo)):
                        nv = vop("tensor_tensor",
                                 (OUTB[:, j * QT:(j + 1) * QT], b_.ap[:, :],
                                  Y1[:, j * 516 + 4: j * 516 + 4 + QT], ALU.add),
                                 inc=(c_vg if j == op + 1 else None))
                    be.free = bo.free = (c_vg, nv)
                slot_free[u["wo_si"]] = (c_g, wo_last)
                u["aba_free"] = wo_last
                u["yadds"] = nv
                wait("v", c_vg, c_vg.n)
                ycast = 0
                for op in range(0, 8, 2):
                    if op:
                        wait("v", c_va, c_va.n)
                    for j in (op, op + 1):
                        nv = vop("tensor_tensor",
                                 (RG[:, j * QT:(j + 1) * QT],
                                  OUTB[:, j * QT:(j + 1) * QT],
                                  OUTB[:, j * QT:(j + 1) * QT], ALU.max),
                                 inc=(c_va if j == op + 1 else None))
                    wait("v", c_va, nv)
                    for j in (op, op + 1):
                        nv = vop("tensor_tensor",
                                 (XKA[:, j * QT:(j + 1) * QT],
                                  RG[:, j * QT:(j + 1) * QT],
                                  RG[:, j * QT:(j + 1) * QT], ALU.mult),
                                 inc=(c_va if j == op + 1 else None))
                    ycast = nv
                u["ycast"] = ycast

            def emit_ln1_head(u):
                if gidx[0] % 2 == 1:
                    gidx[0] += 1
                b0 = next_bank()
                wait("t", c_va, u["ycast"])
                for k in range(8):
                    mm(b0.ap[0:32, :], CSTB[:, 0:32], RG[:, k * QT:(k + 1) * QT],
                       k == 0, k == 7, tp=(0, 0))
                    mm(b0.ap[32:64, :], CSTB[:, 0:32], XKA[:, k * QT:(k + 1) * QT],
                       k == 0, k == 7, tp=(0, 32), inc=(c_g if k == 7 else None))
                nstats = c_g.n
                wait("v", c_g, nstats)
                vop("tensor_scalar", (SR[0:32, :], b0.ap[0:32, :], 1.0 / 1024.0,
                                      None, ALU.mult))
                ntsv = vop("tensor_scalar", (SR[32:64, :], b0.ap[32:64, :],
                                             1.0 / 1024.0, None, ALU.mult),
                           inc=c_va)
                wait("v", c_va, ntsv)
                vop("tensor_tensor", (SR2[0:32, :], SR[0:32, :], SR[0:32, :],
                                      ALU.mult))
                nsr = vop("tensor_copy", (SR2[32:64, :], SR[32:64, :]), inc=c_va)
                b0.free = (c_va, nsr)
                u["nsr"] = nsr

            def emit_ln_bcast_head(u, dst_y1):
                b1 = next_bank()
                wait("t", c_va, u["nsr"])
                nvar = mm(b1.ap[:, :], CSTF[0:64, 128:256], SR2[0:64, :], True,
                          True, inc=c_g)
                b0b = next_bank()
                nmean = mm(b0b.ap[:, :], CSTF[0:64, 0:128], SR[0:64, :], True,
                           True, inc=c_g)
                wait("a", c_g, nvar)
                achain_start()
                if last_nap[0]:
                    wait("a", c_va, last_nap[0])
                nln = act(LTA[:, 0:512], b1.ap[:, :], AF.Ln, bias=MUS[:, 16:17],
                          inc=c_ar)
                wait("a", c_ar, nln)
                nrs = act(LTA[:, 512:1024], LTA[:, 0:512], AF.Exp, scale=-0.5,
                          inc=c_ar)
                last_achain[0] = (c_ar, nrs)
                b1.free = (c_ar, nrs)
                return {"b0b": b0b, "nrs": nrs, "nmean": nmean, "naps": [],
                        "dst": dst_y1}

            def emit_ln_apply_pair(st, op):
                Ybuf = OUTB
                dst_y1 = st["dst"]
                if op == 0:
                    wait("v", c_g, st["nmean"])
                    wait("v", c_ar, st["nrs"])
                wait("v", c_va, c_va.n)
                for j, scr in ((op, STV), (op + 1, RECB)):
                    np1 = vop("scalar_tensor_tensor",
                              (scr[:, :], st["b0b"].ap[:, :], -1.0,
                               Ybuf[:, j * QT:(j + 1) * QT],
                               ALU.mult, ALU.add),
                              inc=(c_va if j == op + 1 else None))
                wait("v", c_va, np1)
                nap = 0
                for j, scr in ((op, STV), (op + 1, RECB)):
                    dst = (dst_y1[:, j * 516 + 4: j * 516 + 4 + QT]
                           if dst_y1 is not None
                           else Ybuf[:, j * QT:(j + 1) * QT])
                    nap = vop("tensor_tensor",
                              (dst, scr[:, :], LTA[:, 512:1024], ALU.mult),
                              inc=c_va)
                st["naps"].append(nap)
                if op == 6:
                    st["b0b"].free = (c_va, nap)
                    last_nap[0] = nap
                return nap

            def emit_ln_bcast_apply(u, dst_y1):
                st = emit_ln_bcast_head(u, dst_y1)
                for op in range(0, 8, 2):
                    emit_ln_apply_pair(st, op)
                return st["naps"][-1], st["naps"]

            def emit_mix_pair(u, op):
                wait("v", c_va, c_va.n)
                for j, scr in ((op, STV), (op + 1, RECB)):
                    nd = vop("tensor_tensor",
                             (scr[:, :], Y1[:, j * 516 + 4: j * 516 + 4 + QT],
                              Y1[:, j * 516 + 3: j * 516 + 3 + QT],
                              ALU.subtract),
                             inc=(c_va if j == op + 1 else None))
                wait("v", c_va, nd)
                for j, scr in ((op, STV), (op + 1, RECB)):
                    vop("scalar_tensor_tensor",
                        (XRA[:, j * QT:(j + 1) * QT], scr[:, :],
                         MUS[:, j:j + 1],
                         Y1[:, j * 516 + 3: j * 516 + 3 + QT],
                         ALU.mult, ALU.add))
                for j, scr in ((op, STV), (op + 1, RECB)):
                    nk = vop("scalar_tensor_tensor",
                             (XKA[:, j * QT:(j + 1) * QT], scr[:, :],
                              MUS[:, 8 + j:9 + j],
                              Y1[:, j * 516 + 3: j * 516 + 3 + QT],
                              ALU.mult, ALU.add),
                             inc=(c_va if j == op + 1 else None))
                wait("v", c_va, nk)
                if op == 6:
                    u["mix_done"] = vop(
                        "tensor_copy",
                        (bass.AP(Y1, 3, [[NCH * 516, 128], [516, 8], [1, 1]]),
                         bass.AP(Y1, 515, [[NCH * 516, 128], [516, 8], [1, 1]])),
                        inc=c_va)

            def emit_mix(u):
                for op in range(0, 8, 2):
                    emit_mix_pair(u, op)

            def make_fillers(u, load_wo=True):
                fl = []

                def r_one(oc):
                    def f():
                        if oc == 0:
                            wait("t", c_va, u["mix_done"])
                        xrch = [XRA[:, k * QT:(k + 1) * QT] for k in range(8)]
                        def sigevac(b):
                            wait("a", c_g)
                            achain_start()
                            if last_nap[0]:
                                wait("a", c_va, last_nap[0])
                            nu = act(LTA[:, 0:512], b.ap[:, :], AF.Exp,
                                     scale=-1.0, inc=c_ar)
                            b.free = (c_ar, nu)
                            wait("a", c_ar, nu)
                            nv_ = act(LTA[:, 512:1024], LTA[:, 0:512], AF.Ln,
                                      bias=MUS[:, 17:18], inc=c_ar)
                            wait("a", c_ar, nv_)
                            nr = act(RG[:, oc * QT:(oc + 1) * QT],
                                     LTA[:, 512:1024], AF.Exp, scale=-1.0,
                                     inc=c_ar)
                            last_achain[0] = (c_ar, nr)
                        u["fr_last"] = proj_group(
                            WS[u["fr_si"]], oc, xrch, sigevac,
                            (c_w[u["fr_si"]], u["fr_n"]) if oc == 0 else None)
                        if oc == 7:
                            slot_free[u["fr_si"]] = (c_g, u["fr_last"])
                            u["fv_si"], u["fv_n"] = load_weight(FV)
                    return f

                def k_pair(op):
                    def f():
                        xkch = [XKA[:, k * QT:(k + 1) * QT] for k in range(8)]
                        be = proj_group_nb(WS[u["fk_si"]], op, xkch,
                                           (c_w[u["fk_si"]], u["fk_n"])
                                           if op == 0 else None)
                        bo = proj_group_nb(WS[u["fk_si"]], op + 1, xkch)
                        u["fk_last"] = c_g.n
                        wait("v", c_g, u["fk_last"])
                        wait("v", c_va, c_va.n)
                        for j, b_, scr in ((op, be, STV), (op + 1, bo, RECB)):
                            nv = vop("tensor_single_scalar",
                                     (scr[:, :], b_.ap[:, :], 0.0, ALU.max),
                                     inc=(c_vg if j == op + 1 else None))
                        be.free = bo.free = (c_vg, nv)
                        wait("v", c_vg, nv)
                        for j, b_, scr in ((op, be, STV), (op + 1, bo, RECB)):
                            nv = vop("tensor_tensor",
                                     (XRA[:, j * QT:(j + 1) * QT], scr[:, :],
                                      scr[:, :], ALU.mult),
                                     inc=(c_va if j == op + 1 else None))
                        u["ka_ready"] = nv
                        if op == 6:
                            slot_free[u["fk_si"]] = (c_g, u["fk_last"])
                            if load_wo:
                                u["wo2_si"], u["wo2_n"] = load_weight(WO)
                    return f

                def v_pair(op):
                    def f():
                        kach = [XRA[:, k * QT:(k + 1) * QT] for k in range(8)]
                        if op == 0:
                            wait("t", c_va, u["ka_ready"])
                        be = proj_group_nb(WS[u["fv_si"]], op, kach,
                                           (c_w[u["fv_si"]], u["fv_n"])
                                           if op == 0 else None)
                        bo = proj_group_nb(WS[u["fv_si"]], op + 1, kach)
                        u["fv_last"] = c_g.n
                        wait("v", c_g, u["fv_last"])
                        wait("v", c_ar)
                        wait("v", c_va, c_va.n)
                        for j, b_, scr in ((op, be, STV), (op + 1, bo, RECB)):
                            nv = vop("tensor_tensor",
                                     (scr[:, :], RG[:, j * QT:(j + 1) * QT],
                                      b_.ap[:, :], ALU.mult),
                                     inc=(c_vg if j == op + 1 else None))
                        be.free = bo.free = (c_vg, nv)
                        wait("v", c_vg, nv)
                        for j, scr in ((op, STV), (op + 1, RECB)):
                            nv = vop("tensor_tensor",
                                     (OUTB[:, j * QT:(j + 1) * QT], scr[:, :],
                                      Y1[:, j * 516 + 4: j * 516 + 4 + QT],
                                      ALU.add),
                                     inc=(c_va if j == op + 1 else None))
                        wait("v", c_va, nv)
                        for j in (op, op + 1):
                            nv = vop("tensor_copy",
                                     (RG[:, j * QT:(j + 1) * QT],
                                      OUTB[:, j * QT:(j + 1) * QT]),
                                     inc=(c_va if j == op + 1 else None))
                        wait("v", c_va, nv)
                        for j in (op, op + 1):
                            nv = vop("tensor_tensor",
                                     (XKA[:, j * QT:(j + 1) * QT],
                                      RG[:, j * QT:(j + 1) * QT],
                                      RG[:, j * QT:(j + 1) * QT], ALU.mult),
                                     inc=(c_va if j == op + 1 else None))
                        u["y2cast"] = nv
                        if op == 6:
                            slot_free[u["fv_si"]] = (c_g, u["fv_last"])
                    return f

                def ln2_stats():
                    if gidx[0] % 2 == 1:
                        gidx[0] += 1
                    b0 = next_bank()
                    wait("t", c_va, u["y2cast"])
                    for k in range(8):
                        mm(b0.ap[0:32, :], CSTB[:, 0:32],
                           RG[:, k * QT:(k + 1) * QT], k == 0, k == 7, tp=(0, 0))
                        mm(b0.ap[32:64, :], CSTB[:, 0:32],
                           XKA[:, k * QT:(k + 1) * QT], k == 0, k == 7,
                           tp=(0, 32), inc=(c_g if k == 7 else None))
                    u["ln2_stats"] = c_g.n
                    wait("v", c_g, u["ln2_stats"])
                    vop("tensor_scalar", (SR[0:32, :], b0.ap[0:32, :],
                                          1.0 / 1024.0, None, ALU.mult))
                    ntsv = vop("tensor_scalar", (SR[32:64, :], b0.ap[32:64, :],
                                                 1.0 / 1024.0, None, ALU.mult),
                               inc=c_va)
                    wait("v", c_va, ntsv)
                    vop("tensor_tensor", (SR2[0:32, :], SR[0:32, :],
                                          SR[0:32, :], ALU.mult))
                    nsr = vop("tensor_copy", (SR2[32:64, :], SR[32:64, :]),
                              inc=c_va)
                    b0.free = (c_va, nsr)
                    u["nsr"] = nsr

                ln2s = {}

                def ln2_head():
                    ln2s["st"] = emit_ln_bcast_head(u, None)

                def ln2_pair(op):
                    def f():
                        emit_ln_apply_pair(ln2s["st"], op)
                        if op == 6:
                            u["naps"] = ln2s["st"]["naps"]
                    return f

                for oc in range(8):
                    fl.append(r_one(oc))
                for op in range(0, 8, 2):
                    fl.append(k_pair(op))
                for op in range(0, 8, 2):
                    fl.append(v_pair(op))
                fl.append(ln2_stats)
                fl.append(ln2_head)
                for op in range(0, 8, 2):
                    fl.append(ln2_pair(op))
                return fl

            def emit_ln2_tail(u, seq_, t_, unext):
                naps = u["naps"]
                xb_free[0] = u["ln2_stats"]
                if unext is not None:
                    wait("s", c_va, u["y2cast"])
                    unext["nxf"] = dma(
                        "s",
                        bass.AP(Y1, 4, [[NCH * 516, 128], [516, 8], [1, QT]]),
                        bass.AP(xT, unext["seq"] * D * S + unext["t"] * QT,
                                [[S, 128], [128 * S, 8], [1, QT]]), c_xf)
                for i, op in enumerate(range(0, 8, 2)):
                    wait("a", c_va, naps[i])
                    out_done[0] = dma(
                        "a",
                        bass.AP(outT, seq_ * D * S + op * 128 * S + t_ * QT,
                                [[S, 128], [128 * S, 2], [1, QT]]),
                        bass.AP(OUTB, op * QT,
                                [[NCH * QT, 128], [QT, 2], [1, QT]]),
                        c_out)

            def q_group(t, oc, qev, u):
                xsrc = RG if t == 0 else XRA
                xbch = [xsrc[:, k * QT:(k + 1) * QT] for k in range(8)]
                if oc == 0:
                    wait("t", c_xb, u["nxb"])
                    wait("t", c_wqr, n_wq_res)

                def qevac(b):
                    wait("v", c_g)
                    nv = vop("tensor_copy",
                             (QTB[:, oc * QT:(oc + 1) * QT], b.ap[:, :]),
                             inc=c_vg)
                    b.free = (c_vg, nv)
                qev[oc] = 0
                proj_group(WS[2], oc, xbch, qevac)
                qev[oc] = c_vg.n

            def emit_xb(t, u_prev, u):
                gate = u_prev["aba_free"] if u_prev is not None else xb_free[0]
                wait("g", c_g, gate)
                u["nxb"] = dma("g",
                               bass.AP(ABA, 0, [[NCH * QT, 128], [QT, 8], [1, QT]]),
                               bass.AP(xT, seq * D * S + t * QT,
                                       [[S, 128], [128 * S, 8], [1, QT]]), c_xb)

            def emit_attention(t, fillers, qev):
                att = {"den_q": [], "epi": None}
                use4[0] = False
                ao_base = c_ao.n
                pend = []
                step = [0]
                last_inj = [-2]

                def emit_scores_exp(hp, kt):
                    g = gkt[0]
                    sps = SA if (g % 2 == 0) else SB2
                    if kt == 0 and qev.get(hp):
                        wait("t", c_vg, qev[hp])
                    if g - 2 in exp_hist:
                        wait("t", c_p, exp_hist[g - 2])
                    qa = QTB[0:64, hp * QT:(hp + 1) * QT]
                    qb = QTB[64:128, hp * QT:(hp + 1) * QT]
                    ka = KTB[0:64, hp * S + kt * 128: hp * S + (kt + 1) * 128]
                    kb = KTB[64:128, hp * S + kt * 128: hp * S + (kt + 1) * 128]
                    mm(sps[:, 0:512], ka, qa, True, True, tp=(0, 0))
                    nsc = mm(sps[:, 512:1024], kb, qb, True, True, tp=(64, 0),
                             inc=c_sc)
                    pbs = PB[:, (g % PBS) * 1024:(g % PBS) * 1024 + 1024]
                    wait("a", c_sc, nsc)
                    if g - PBS >= 0:
                        wait("a", c_kt, g - PBS + 1)
                    npp = act(pbs, sps[:, 0:1024], AF.Exp, scale=SCALE, inc=c_p)
                    exp_hist[g] = npp
                    gkt[0] += 1
                    return g

                def emit_den(hp_, kt_, g_):
                    pbs_ = PB[:, (g_ % PBS) * 1024:(g_ % PBS) * 1024 + 1024]
                    mm(DEN[0:32, :], CSTB[:, 0:32], pbs_[:, 0:512], kt_ == 0,
                       kt_ == NKT - 1, tp=(0, 0))
                    mm(DEN[32:64, :], CSTB[:, 0:32], pbs_[:, 512:1024],
                       kt_ == 0, kt_ == NKT - 1, tp=(0, 32), inc=c_kt)

                def emit_epi():
                    nrc, nrb_res, b = att["epi"]
                    att["epi"] = None
                    wait("t", c_rc, nrc)
                    nrb = mm(b.ap[:, :], CSTF[0:64, 256:384], RCR[0:64, :],
                             True, True, inc=c_rb)
                    assert nrb == nrb_res

                def emit_pv(hp_, kt_, g_):
                    wait("t", c_p, exp_hist[g_])
                    if kt_ == 0:
                        acquire(bPV)
                    pbs_ = PB[:, (g_ % PBS) * 1024:(g_ % PBS) * 1024 + 1024]
                    va = VTB[:, kt_ * D + (2 * hp_) * 64:
                             kt_ * D + (2 * hp_) * 64 + 64]
                    vb = VTB[:, kt_ * D + (2 * hp_ + 1) * 64:
                             kt_ * D + (2 * hp_ + 1) * 64 + 64]
                    mm(PV[0:64, :], va, pbs_[:, 0:512], kt_ == 0, kt_ == NKT - 1,
                       tp=(0, 0))
                    mm(PV[64:128, :], vb, pbs_[:, 512:1024], kt_ == 0,
                       kt_ == NKT - 1, tp=(0, 64))

                def flush_pv():
                    hp, kt, g = pend.pop(0)
                    if att["epi"] is not None:
                        if att["epi_cd"] == 0:
                            emit_epi()
                        else:
                            att["epi_cd"] -= 1
                    if kt < 2:
                        att["den_q"].append((hp, kt, g))
                        return
                    if kt == 2:
                        dq = att["den_q"]
                        if dq:
                            for hp_, kt_, g_ in dq:
                                emit_pv(hp_, kt_, g_)
                            acquire(bDEN)
                            while dq:
                                emit_den(*dq.pop(0))
                    emit_pv(hp, kt, g)
                    emit_den(hp, kt, g)
                    if kt == NKT - 1:
                        wait("a", c_kt, c_kt.n)
                        achain_start()
                        if last_nrb[0]:
                            wait("a", c_rb, last_nrb[0])
                        nln = act(LTA[0:64, 0:512], DEN[0:64, :], AF.Ln,
                                  inc=c_ar)
                        wait("a", c_ar, nln)
                        nrc = act(RCR[0:64, :], LTA[0:64, 0:512], AF.Exp,
                                  scale=-1.0, inc=c_rc)
                        last_achain[0] = (c_rc, nrc)
                        bDEN.free = (c_ar, nln)
                        b = next_bank()
                        nrb_res = c_rb.n + 1
                        last_nrb[0] = nrb_res
                        att["epi"] = (nrc, nrb_res, b)
                        att["epi_cd"] = 1
                        wait("a", c_rb, nrb_res)
                        ncp = act(RSB[:, :], b.ap[:, :], AF.Copy, inc=c_ar)
                        b.free = (c_ar, ncp)
                        wait("v", c_ar, ncp)
                        nao = vop("tensor_tensor",
                                  (ABA[:, hp * QT:(hp + 1) * QT], PV[:, :],
                                   RSB[:, :], ALU.mult), inc=c_ao)
                        bPV.free = (c_ao, nao)

                for hp in range(NHP):
                    for kt in range(NKT):
                        g = emit_scores_exp(hp, kt)
                        pend.append((hp, kt, g))
                        if len(pend) >= 3:
                            flush_pv()
                        step[0] += 1
                        if (fillers and fillers[0][0] <= step[0]
                                and step[0] - last_inj[0] >= 2
                                and (att["epi"] is None
                                     or (len(fillers[0]) > 2 and fillers[0][2]))):
                            use4[0] = False
                            fillers.pop(0)[1]()
                            last_inj[0] = step[0]
                while pend:
                    flush_pv()
                if att["epi"] is not None:
                    emit_epi()
                while fillers:
                    fillers.pop(0)[1]()
                return ao_base + 8

            for t in range(NT):
                u = {"wo_si": None, "t": t, "seq": seq}
                fillers = []
                qev = {}

                def mk_q(oc, _t=t, _u=u, _qev=qev):
                    def f():
                        q_group(_t, oc, _qev, _u)
                    return f

                if prev is None:
                    wait("s", c_va, n_y1_init)
                    u["nxf"] = dma(
                        "s",
                        bass.AP(Y1, 4, [[NCH * 516, 128], [516, 8], [1, QT]]),
                        bass.AP(xT, seq * D * S + t * QT,
                                [[S, 128], [128 * S, 8], [1, QT]]), c_xf)
                if prev is not None and t > 0:
                    emit_wo_pairs(prev)
                u["nxb"] = kv_nxb0[0] if t == 0 else xra_nxb[0]
                use4[0] = True
                q_group(t, 0, qev, u)
                for oc in range(1, 8):
                    fillers.append((2 * oc - 1, mk_q(oc)))

                if prev is not None:
                    def u_loads(p=prev):
                        p["fr_si"], p["fr_n"] = load_weight(FR)
                        p["fk_si"], p["fk_n"] = load_weight(FK)
                    fillers.append((15, u_loads, True))
                    if t > 0:
                        def u_stats(p=prev):
                            emit_ln1_head(p)
                        lnst = {}
                        def u_bhead(p=prev, s=lnst):
                            s["st"] = emit_ln_bcast_head(p, Y1)
                        def mk_ap(op, s=lnst):
                            return lambda: emit_ln_apply_pair(s["st"], op)
                        def mk_mx(op, p=prev):
                            return lambda: emit_mix_pair(p, op)
                        fillers.append((17, u_stats))
                        fillers.append((19, u_bhead))
                        for i, op in enumerate(range(0, 8, 2)):
                            fillers.append((21 + 4 * i, mk_ap(op), True))
                            fillers.append((23 + 4 * i, mk_mx(op), True))
                        steps = [50 + 3 * i for i in range(17)]
                        steps += [100 + 2 * i for i in range(5)]
                    else:
                        steps = [24 + 3 * i for i in range(17)]
                        steps += [75 + 2 * i for i in range(5)]
                    mkfl = make_fillers(prev, load_wo=True)
                    for i, (st, f) in enumerate(zip(steps, mkfl)):
                        fillers.append((st, f, i >= len(mkfl) - 4))

                u["ao_done"] = emit_attention(t, fillers, qev)
                if prev is not None:
                    u["wo_si"], u["wo_n"] = prev["wo2_si"], prev["wo2_n"]
                    emit_ln2_tail(prev, prev["seq"], prev["t"], u)
                else:
                    wo_si0, wo_n0 = load_weight(WO)
                    u["wo_si"], u["wo_n"] = wo_si0, wo_n0
                if t < NT - 1:
                    wait("g", c_xb, c_xb.n)
                    if prev is not None:
                        wait("g", c_g, prev["fv_last"])
                    xra_nxb[0] = dma(
                        "g",
                        bass.AP(XRA, 0, [[NCH * QT, 128], [QT, 8], [1, QT]]),
                        bass.AP(xT, seq * D * S + (t + 1) * QT,
                                [[S, 128], [128 * S, 8], [1, QT]]), c_xb)
                prev = u

        emit_wo_pairs(prev)
        last_ycast[0] = prev["yadds"]
        prev["fr_si"], prev["fr_n"] = load_weight(FR)
        prev["fk_si"], prev["fk_n"] = load_weight(FK)
        emit_ln1_head(prev)
        emit_ln_bcast_apply(prev, Y1)
        emit_mix(prev)
        for f in make_fillers(prev, load_wo=False):
            f()
        emit_ln2_tail(prev, prev["seq"], prev["t"], None)

        wait("s", c_out, c_out.n)

        with nc.Block() as block:
            def runner(key):
                def _run(e):
                    for f in plan[key]:
                        f(e)
                return _run
            block.tensor(runner("t"))
            block.scalar(runner("a"))
            block.vector(runner("v"))
            block.sync(runner("s"))
            block.gpsimd(runner("g"))

    return nc


_NC_CACHE = [None]


def kernel(L0_attn_x, L0_attn_kernel, L0_attn_WO, L0_ffn_W_r, L0_ffn_W_k,
           L0_ffn_W_v, L0_ffn_mu_r, L0_ffn_mu_k):
    x = np.asarray(L0_attn_x, dtype=np.float32)
    ker = np.asarray(L0_attn_kernel, dtype=np.float32)
    wo = np.asarray(L0_attn_WO, dtype=np.float32)[0]
    wr = np.asarray(L0_ffn_W_r, dtype=np.float32)
    wk = np.asarray(L0_ffn_W_k, dtype=np.float32)
    wv = np.asarray(L0_ffn_W_v, dtype=np.float32)
    mur = np.asarray(L0_ffn_mu_r, dtype=np.float32)
    muk = np.asarray(L0_ffn_mu_k, dtype=np.float32)

    B = x.shape[0]
    ncore = 8
    wq_l = np.ascontiguousarray(ker[0].transpose(1, 0, 2).reshape(D, D))
    wk_l = np.ascontiguousarray(ker[1].transpose(1, 0, 2).reshape(D, D))
    wv_l = np.ascontiguousarray(ker[2].transpose(1, 0, 2).reshape(D, D))
    wo_l = np.ascontiguousarray(wo.reshape(64, 16, D).transpose(1, 0, 2).reshape(D, D))
    wts = np.stack([wk_l, wv_l, wq_l, wo_l,
                    np.ascontiguousarray(wr.T),
                    np.ascontiguousarray(wk.T),
                    np.ascontiguousarray(wv.T)], axis=0)

    mus = np.zeros((128, 18), np.float32)
    for c in range(8):
        mus[:, c] = mur[c * 128:(c + 1) * 128]
        mus[:, 8 + c] = muk[c * 128:(c + 1) * 128]
    mus[:, 16] = EPS
    mus[:, 17] = 1.0

    cst = np.zeros((128, 416), np.float32)
    cst[:, 0:32] = 1.0
    cst[0, 32:160] = 1.0
    cst[0, 160:288] = -1.0
    cst[32, 160:288] = 1.0
    cst[0, 288:352] = 1.0
    cst[32, 352:416] = 1.0

    xT = np.ascontiguousarray(x.transpose(0, 2, 1))

    if _NC_CACHE[0] is None:
        _NC_CACHE[0] = build_nc()
    nc = _NC_CACHE[0]

    in_maps = [{"xT": np.ascontiguousarray(xT[2 * c:2 * c + 2]),
                "wts": wts, "mus": mus, "cst": cst} for c in range(ncore)]
    res = run_bass_kernel_spmd(nc, in_maps, core_ids=list(range(ncore)))
    out = np.empty((B, S, D), np.float32)
    for c in range(ncore):
        o = res.results[c]["outT"]
        out[2 * c] = o[0].T
        out[2 * c + 1] = o[1].T
    return out


# version 33 (final)
# speedup vs baseline: 1.4394x; 1.0114x over previous
import contextlib
import numpy as np
import ml_dtypes
import concourse.bass as bass
import concourse.mybir as mybir
from concourse.bass_utils import run_bass_kernel_spmd

F32 = mybir.dt.float32
BF16 = mybir.dt.bfloat16
AF = mybir.ActivationFunctionType
ALU = mybir.AluOpType

D = 1024
S = 2048
NSEQ = 2
NCH = 8
QT = 512
NT = S // QT
NHP = 8
NKT = S // 128
PBS = 3
SCALE = 1.0 / 65536.0
EPS = 1e-5

WK, WV, WQ, WO, FR, FK, FV = range(7)


class Cnt:
    def __init__(self, name):
        self.name = name
        self.n = 0


def build_nc():
    nc = bass.Bass("TRN2", target_bir_lowering=False)

    xT = nc.dram_tensor("xT", [NSEQ, D, S], F32, kind="ExternalInput")
    wts = nc.dram_tensor("wts", [7, D, D], F32, kind="ExternalInput")
    mus = nc.dram_tensor("mus", [128, 18], F32, kind="ExternalInput")
    cst = nc.dram_tensor("cst", [128, 416], F32, kind="ExternalInput")
    outT = nc.dram_tensor("outT", [NSEQ, D, S], F32, kind="ExternalOutput")
    wsc = nc.dram_tensor("wsc", [7 * D, D], BF16, kind="Internal")

    plan = {k: [] for k in "tavsg"}
    sems = {}
    sem_handles = {}

    def em(eng, fn):
        plan[eng].append(fn)

    def C(name):
        c = Cnt(name)
        sems[name] = c
        return c

    c_cstb = C("c_cstb")
    c_wqr = C("c_wqr")
    c_cst = C("c_cst")
    c_wc0 = C("c_wc0")
    c_wc1 = C("c_wc1")
    c_ws0 = C("c_ws0")
    c_ws1 = C("c_ws1")
    c_w0 = C("c_w0")
    c_w1 = C("c_w1")
    c_xf = C("c_xf")
    c_xb = C("c_xb")
    c_out = C("c_out")
    c_sc = C("c_sc")
    c_p = C("c_p")
    c_kt = C("c_kt")
    c_rc = C("c_rc")
    c_g = C("c_g")
    c_vg = C("c_vg")
    c_ao = C("c_ao")
    c_rb = C("c_rb")
    c_tf = C("c_tf")
    c_va = C("c_va")
    c_ar = C("c_ar")

    def wait(eng, c, val=None):
        v = c.n if val is None else val
        if v <= 0:
            return
        em(eng, lambda e, _n=c.name, _v=v: e.wait_ge(sem_handles[_n], _v))

    def bump(c, by=1):
        c.n += by
        return c.n

    def dma(eng, dst, src, c):
        n = bump(c, 16)
        em(eng, lambda e, _d=dst, _s=src, _n=c.name:
           e.dma_start(_d, _s).then_inc(sem_handles[_n], 16))
        return n

    def mm(out, lhsT, rhs, start, stop, tp=None, inc=None):
        n = bump(inc) if inc is not None else None
        def f(e, _o=out, _l=lhsT, _r=rhs, _st=start, _sp=stop, _tp=tp, _inc=inc):
            i = e.matmul(_o, _l, _r, start=_st, stop=_sp, tile_position=_tp,
                         skip_group_check=True)
            if _inc is not None:
                i.then_inc(sem_handles[_inc.name], 1)
        em("t", f)
        return n

    def act(out, in_, func, scale=1.0, bias=0.0, inc=None):
        n = bump(inc) if inc is not None else None
        def f(e, _o=out, _i=in_, _f=func, _s=scale, _b=bias, _inc=inc):
            i = e.activation(_o, _i, _f, scale=_s, bias=_b)
            if _inc is not None:
                i.then_inc(sem_handles[_inc.name], 1)
        em("a", f)
        return n

    def vop(name, args, inc=None):
        n = bump(inc) if inc is not None else None
        def f(e, _nm=name, _a=args, _inc=inc):
            i = getattr(e, _nm)(*_a)
            if _inc is not None:
                i.then_inc(sem_handles[_inc.name], 1)
        em("v", f)
        return n

    def tnop_inc(c):
        n = bump(c)
        em("t", lambda e, _n=c.name: e.nop().then_inc(sem_handles[_n], 1))
        return n

    with contextlib.ExitStack() as ctx:
        sbt = lambda name, shape, dt: ctx.enter_context(nc.sbuf_tensor(name, shape, dt))
        pst = lambda name, shape: ctx.enter_context(nc.psum_tensor(name, shape, F32))

        WS = [sbt("ws0", [128, 8 * D], BF16), sbt("ws1", [128, 8 * D], BF16),
              sbt("ws2", [128, 8 * D], BF16)]
        KTB = sbt("ktb", [128, NHP * S], BF16)
        VTB = sbt("vtb", [128, NKT * D], BF16)
        QTB = sbt("qtb", [128, NCH * QT], BF16)
        PB = sbt("pb", [128, PBS * 1024], BF16)
        ABA = sbt("aba", [128, NCH * QT], BF16)
        XKA = sbt("xka", [128, NCH * QT], BF16)
        XRA = sbt("xra", [128, NCH * QT], BF16)
        RG = sbt("rg", [128, NCH * QT], BF16)
        OUTB = sbt("outb", [128, NCH * QT], F32)
        Y1 = sbt("y1", [128, NCH * 516], F32)
        STV = sbt("stv", [128, QT], F32)
        LTA = sbt("lta", [128, 1024], F32)
        RSB = sbt("rsb", [128, QT], F32)
        RECB = sbt("recb", [128, QT], F32)
        RCR = sbt("rcr", [64, QT], F32)
        SR = sbt("sr", [64, QT], F32)
        SR2 = sbt("sr2", [64, QT], F32)
        CSTF = sbt("cstf", [64, 384], F32)
        CSTB = sbt("cstb", [128, 32], BF16)
        MUS = sbt("musb", [128, 18], F32)

        SA = pst("psa", [128, 1024])
        SB2 = pst("psb", [128, 1024])
        PV = pst("pspv", [128, 512])
        DEN = pst("psden", [128, 512])
        G0 = pst("psg0", [128, 512])
        G1 = pst("psg1", [128, 512])

        for name in sems:
            sem_handles[name] = ctx.enter_context(nc.semaphore(name))

        class Bank:
            def __init__(self, ap):
                self.ap = ap
                self.free = None
        bG = [Bank(G0), Bank(G1)]
        bPV = Bank(PV)
        bDEN = Bank(DEN)
        pool4 = [bG[0], bG[1], bPV, bDEN]

        def acquire(b):
            if b.free is not None:
                wait("t", b.free[0], b.free[1])
                b.free = None

        gidx = [0]
        hidx = [0]
        use4 = [False]

        def next_bank():
            if use4[0]:
                b = pool4[hidx[0] % 4]
                hidx[0] += 1
            else:
                b = bG[gidx[0] % 2]
                gidx[0] += 1
            acquire(b)
            return b

        def proj_group(ws_slot, oc, rhs_chunks, evac, w_ready=None):
            b = next_bank()
            if w_ready is not None:
                wait("t", w_ready[0], w_ready[1])
            for k in range(8):
                lhsT = ws_slot[:, k * D + oc * 128: k * D + (oc + 1) * 128]
                mm(b.ap[:, :], lhsT, rhs_chunks[k], start=(k == 0), stop=(k == 7),
                   inc=(c_g if k == 7 else None))
            evac(b)
            return c_g.n

        def proj_group_nb(ws_slot, oc, rhs_chunks, w_ready=None):
            b = next_bank()
            if w_ready is not None:
                wait("t", w_ready[0], w_ready[1])
            for k in range(8):
                lhsT = ws_slot[:, k * D + oc * 128: k * D + (oc + 1) * 128]
                mm(b.ap[:, :], lhsT, rhs_chunks[k], start=(k == 0), stop=(k == 7),
                   inc=(c_g if k == 7 else None))
            return b

        def vnop():
            em("v", lambda e: e.engine_nop())

        def layer_norm(YBF, YSQ, Ybuf, dst_y1, ycast_n, free_aba_after_stats,
                       ycast_sem=None):
            if gidx[0] % 2 == 1:
                gidx[0] += 1
            b0 = next_bank()
            wait("t", ycast_sem if ycast_sem is not None else c_vg, ycast_n)
            for k in range(8):
                mm(b0.ap[0:32, :], CSTB[:, 0:32], YBF[:, k * QT:(k + 1) * QT],
                   k == 0, k == 7, tp=(0, 0))
                mm(b0.ap[32:64, :], CSTB[:, 0:32], YSQ[:, k * QT:(k + 1) * QT],
                   k == 0, k == 7, tp=(0, 32), inc=(c_g if k == 7 else None))
            nstats = c_g.n
            wait("v", c_g, nstats)
            vop("tensor_scalar", (SR[0:32, :], b0.ap[0:32, :], 1.0 / 1024.0, None,
                                  ALU.mult))
            ntsv = vop("tensor_scalar", (SR[32:64, :], b0.ap[32:64, :], 1.0 / 1024.0,
                                         None, ALU.mult), inc=c_va)
            wait("v", c_va, ntsv)
            vop("tensor_tensor", (SR2[0:32, :], SR[0:32, :], SR[0:32, :], ALU.mult))
            nsr = vop("tensor_copy", (SR2[32:64, :], SR[32:64, :]), inc=c_va)
            b0.free = (c_va, nsr)
            b1 = next_bank()
            wait("t", c_va, nsr)
            nvar = mm(b1.ap[:, :], CSTF[0:64, 128:256], SR2[0:64, :], True, True,
                      inc=c_g)
            b0b = next_bank()
            nmean = mm(b0b.ap[:, :], CSTF[0:64, 0:128], SR[0:64, :], True, True,
                       inc=c_g)
            wait("a", c_g, nvar)
            achain_start()
            if last_nap[0]:
                wait("a", c_va, last_nap[0])
            nln = act(LTA[:, 0:512], b1.ap[:, :], AF.Ln, bias=MUS[:, 16:17],
                      inc=c_ar)
            wait("a", c_ar, nln)
            nrs = act(RSB[:, :], LTA[:, 0:512], AF.Exp, scale=-0.5, inc=c_ar)
            last_achain[0] = (c_ar, nrs)
            b1.free = (c_ar, nrs)
            wait("v", c_g, nmean)
            wait("v", c_ar, nrs)
            nap = 0
            for op in range(0, 8, 2):
                if op:
                    wait("v", c_va, c_va.n)
                for j, scr in ((op, STV), (op + 1, RECB)):
                    np1 = vop("scalar_tensor_tensor",
                              (scr[:, :], b0b.ap[:, :], -1.0,
                               Ybuf[:, j * QT:(j + 1) * QT],
                               ALU.mult, ALU.add),
                              inc=(c_va if j == op + 1 else None))
                wait("v", c_va, np1)
                for j, scr in ((op, STV), (op + 1, RECB)):
                    dst = (dst_y1[:, j * 516 + 4: j * 516 + 4 + QT]
                           if dst_y1 is not None else Ybuf[:, j * QT:(j + 1) * QT])
                    nap = vop("tensor_tensor", (dst, scr[:, :], RSB[:, :], ALU.mult),
                              inc=c_va)
            b0b.free = (c_va, nap)
            last_nap[0] = nap
            return nap, nstats

        n_y1_init = vop("memset", (Y1[:, :], 0.0), inc=c_va)
        dma("g", CSTB[:, :], cst[:, 0:32], c_cstb)
        n_wq_res = dma("g", bass.AP(WS[2], 0, [[8 * D, 128], [D, 8], [1, D]]),
                       bass.AP(wts, WQ * D * D, [[D, 128], [128 * D, 8], [1, D]]),
                       c_wqr)
        dma("s", CSTF[:, :], cst[0:64, 32:416], c_cst)
        dma("s", MUS[:, :], mus[:, :], c_cst)
        wait("t", c_cstb, 16)
        wait("t", c_cst, 32)
        wait("v", c_cst, 32)

        c_wc = [c_wc0, c_wc1]
        c_wsx = [c_ws0, c_ws1]
        c_w = [c_w0, c_w1]
        kv0_gate = [0, 0]

        def emit_wcast(m):
            slot = WS[m % 2]
            if m >= 2:
                wait("g", c_wsx[m % 2], c_wsx[m % 2].n)
                wait("g", c_g, kv0_gate[m % 2])
            nld = dma("g", bass.AP(slot, 0, [[8 * D, 128], [D, 8], [1, D]]),
                      bass.AP(wts, m * D * D, [[D, 128], [128 * D, 8], [1, D]]),
                      c_wc[m % 2])
            wait("s", c_wc[m % 2], nld)
            dma("s", bass.AP(wsc, m * D * D, [[D, 128], [128 * D, 8], [1, D]]),
                bass.AP(slot, 0, [[8 * D, 128], [D, 8], [1, D]]), c_wsx[m % 2])
            return nld

        wk0_n = emit_wcast(0)
        wv0_n = emit_wcast(1)

        wnext = [0]
        scratch_all = [(c_ws0, c_ws0.n), (c_ws1, c_ws1.n)]
        slot_free = [None, None]
        first_load = [True]

        def load_weight(mat):
            si = wnext[0]
            wnext[0] = 1 - si
            if first_load[0]:
                wait("s", scratch_all[0][0], scratch_all[0][1])
                wait("s", scratch_all[1][0], scratch_all[1][1])
                first_load[0] = False
            fa = slot_free[si]
            if fa is not None:
                wait("s", fa[0], fa[1])
            n = dma("s", bass.AP(WS[si], 0, [[8 * D, 128], [D, 8], [1, D]]),
                    bass.AP(wsc, mat * D * D, [[D, 128], [128 * D, 8], [1, D]]),
                    c_w[si])
            return si, n

        exp_hist = {}
        den_hist = {}
        last_achain = [None]
        last_nrb = [0]
        last_nap = [0]

        def achain_start():
            if last_achain[0] is not None:
                wait("a", last_achain[0][0], last_achain[0][1])
        gkt = [0]
        xb_free = [0]
        out_done = [0]
        prev = None
        last_ycast = [0]

        for seq in range(NSEQ):
            def emit_kv(kv_fillers):
                use4[0] = True
                if seq == 0:
                    wk_si, wv_si = 0, 1
                    wait("t", c_wc[0], wk0_n)
                    wait("t", c_wc[1], wv0_n)
                else:
                    wk_si, wk_n = load_weight(WK)
                    wv_si, wv_n = load_weight(WV)
                    wait("t", c_w[wk_si], wk_n)
                    wait("t", c_w[wv_si], wv_n)
                wk_last = wv_last = 0
                order = (1, 2, 3, 0)
                dsts = {1: ABA, 2: RG, 3: ABA, 0: RG}
                buf_free = {id(ABA): (c_g, xb_free[0]), id(RG): None}
                loads = {}

                def emit_load(tb):
                    buf = dsts[tb]
                    wait("g", c_xb, c_xb.n)
                    fa = buf_free[id(buf)]
                    if fa is not None:
                        wait("g", fa[0], fa[1])
                    loads[tb] = dma(
                        "g",
                        bass.AP(buf, 0, [[NCH * QT, 128], [QT, 8], [1, QT]]),
                        bass.AP(xT, seq * D * S + tb * QT,
                                [[S, 128], [128 * S, 8], [1, QT]]), c_xb)

                emit_load(1)
                for idx, tb in enumerate(order):
                    buf = dsts[tb]
                    nxb = loads[tb]
                    if tb == 0:
                        kv_nxb0[0] = nxb
                    wait("t", c_xb, nxb)
                    xbch = [buf[:, k * QT:(k + 1) * QT] for k in range(8)]
                    for hp in range(NHP):
                        b = next_bank()
                        for k in range(8):
                            mm(b.ap[:, :],
                               WS[wk_si][:, k * D + hp * 128: k * D + (hp + 1) * 128],
                               xbch[k], start=(k == 0), stop=(k == 7),
                               inc=(c_g if k == 7 else None))
                        wk_last = c_g.n
                        wait("v", c_g)
                        nv = vop("tensor_copy",
                                 (KTB[:, hp * S + tb * QT: hp * S + (tb + 1) * QT],
                                  b.ap[:, :]), inc=c_vg)
                        b.free = (c_vg, nv)
                    if kv_fillers:
                        kv_fillers.pop(0)()
                    if idx + 1 < len(order):
                        if buf_free[id(RG)] is None:
                            buf_free[id(RG)] = (c_g, c_g.n)
                        emit_load(order[idx + 1])
                    for tt in range(4):
                        for half in range(2):
                            b = next_bank()
                            for k in range(8):
                                mm(b.ap[:, :], xbch[k][:, tt * 128:(tt + 1) * 128],
                                   WS[wv_si][:, k * D + half * 512:
                                             k * D + half * 512 + 512],
                                   start=(k == 0), stop=(k == 7),
                                   inc=(c_g if k == 7 else None))
                            wv_last = c_g.n
                            kti = tb * 4 + tt
                            wait("v", c_g)
                            nv = vop("tensor_copy",
                                     (VTB[:, kti * D + half * 512:
                                          kti * D + half * 512 + 512],
                                      b.ap[:, :]), inc=c_vg)
                            b.free = (c_vg, nv)
                    if kv_fillers:
                        kv_fillers.pop(0)()
                    buf_free[id(buf)] = (c_g, wv_last)
                    xb_free[0] = wv_last
                slot_free[wk_si] = (c_g, wk_last)
                slot_free[wv_si] = (c_g, wv_last)
                if seq == 0:
                    kv0_gate[0], kv0_gate[1] = wk_last, wv_last
                    for m in range(2, 7):
                        emit_wcast(m)
                    scratch_all[0] = (c_ws0, c_ws0.n)
                    scratch_all[1] = (c_ws1, c_ws1.n)
                while kv_fillers:
                    kv_fillers.pop(0)()
                use4[0] = False

            kvf = []
            kv_nxb0 = [0]
            xra_nxb = [0]
            if prev is not None:
                emit_wo_pairs(prev)
                last_ycast[0] = prev["yadds"]
                xb_free[0] = prev["aba_free"]
                def kv_mx(op, p=prev):
                    return lambda: emit_mix_pair(p, op)

                kvf = [lambda p=prev: emit_ln1_head(p),
                       lambda p=prev: emit_ln_bcast_apply(p, Y1)]
                for op in range(0, 8, 2):
                    kvf.append(kv_mx(op))
            emit_kv(kvf)

            wait("v", c_va, c_va.n)
            for cch in range(NCH):
                vop("memset", (Y1[:, cch * 516 + 3: cch * 516 + 4], 0.0))

            def emit_wo_pairs(u):
                use4[0] = True
                aoch = [ABA[:, k * QT:(k + 1) * QT] for k in range(8)]
                wait("t", c_ao, u["ao_done"])
                wo_last = 0
                nv = 0
                for op in range(0, 8, 2):
                    be = proj_group_nb(WS[u["wo_si"]], op, aoch,
                                       (c_w[u["wo_si"]], u["wo_n"]) if op == 0
                                       else None)
                    bo = proj_group_nb(WS[u["wo_si"]], op + 1, aoch)
                    wo_last = c_g.n
                    wait("v", c_g, wo_last)
                    if op == 0:
                        wait("v", c_xf, u["nxf"])
                        if out_done[0]:
                            wait("v", c_out, out_done[0])
                    for j, b_ in ((op, be), (op + 1, bo)):
                        nv = vop("tensor_tensor",
                                 (OUTB[:, j * QT:(j + 1) * QT], b_.ap[:, :],
                                  Y1[:, j * 516 + 4: j * 516 + 4 + QT], ALU.add),
                                 inc=(c_vg if j == op + 1 else None))
                    be.free = bo.free = (c_vg, nv)
                slot_free[u["wo_si"]] = (c_g, wo_last)
                u["aba_free"] = wo_last
                u["yadds"] = nv
                wait("v", c_vg, c_vg.n)
                ycast = 0
                for op in range(0, 8, 2):
                    if op:
                        wait("v", c_va, c_va.n)
                    for j in (op, op + 1):
                        nv = vop("tensor_tensor",
                                 (RG[:, j * QT:(j + 1) * QT],
                                  OUTB[:, j * QT:(j + 1) * QT],
                                  OUTB[:, j * QT:(j + 1) * QT], ALU.max),
                                 inc=(c_va if j == op + 1 else None))
                    wait("v", c_va, nv)
                    for j in (op, op + 1):
                        nv = vop("tensor_tensor",
                                 (XKA[:, j * QT:(j + 1) * QT],
                                  RG[:, j * QT:(j + 1) * QT],
                                  RG[:, j * QT:(j + 1) * QT], ALU.mult),
                                 inc=(c_va if j == op + 1 else None))
                    ycast = nv
                u["ycast"] = ycast

            def emit_ln1_head(u):
                if gidx[0] % 2 == 1:
                    gidx[0] += 1
                b0 = next_bank()
                wait("t", c_va, u["ycast"])
                for k in range(8):
                    mm(b0.ap[0:32, :], CSTB[:, 0:32], RG[:, k * QT:(k + 1) * QT],
                       k == 0, k == 7, tp=(0, 0))
                    mm(b0.ap[32:64, :], CSTB[:, 0:32], XKA[:, k * QT:(k + 1) * QT],
                       k == 0, k == 7, tp=(0, 32), inc=(c_g if k == 7 else None))
                nstats = c_g.n
                wait("v", c_g, nstats)
                vop("tensor_scalar", (SR[0:32, :], b0.ap[0:32, :], 1.0 / 1024.0,
                                      None, ALU.mult))
                ntsv = vop("tensor_scalar", (SR[32:64, :], b0.ap[32:64, :],
                                             1.0 / 1024.0, None, ALU.mult),
                           inc=c_va)
                wait("v", c_va, ntsv)
                vop("tensor_tensor", (SR2[0:32, :], SR[0:32, :], SR[0:32, :],
                                      ALU.mult))
                nsr = vop("tensor_copy", (SR2[32:64, :], SR[32:64, :]), inc=c_va)
                b0.free = (c_va, nsr)
                u["nsr"] = nsr

            def emit_ln_bcast_head(u, dst_y1):
                b1 = next_bank()
                wait("t", c_va, u["nsr"])
                nvar = mm(b1.ap[:, :], CSTF[0:64, 128:256], SR2[0:64, :], True,
                          True, inc=c_g)
                b0b = next_bank()
                nmean = mm(b0b.ap[:, :], CSTF[0:64, 0:128], SR[0:64, :], True,
                           True, inc=c_g)
                wait("a", c_g, nvar)
                achain_start()
                if last_nap[0]:
                    wait("a", c_va, last_nap[0])
                nln = act(LTA[:, 0:512], b1.ap[:, :], AF.Ln, bias=MUS[:, 16:17],
                          inc=c_ar)
                wait("a", c_ar, nln)
                nrs = act(LTA[:, 512:1024], LTA[:, 0:512], AF.Exp, scale=-0.5,
                          inc=c_ar)
                last_achain[0] = (c_ar, nrs)
                b1.free = (c_ar, nrs)
                return {"b0b": b0b, "nrs": nrs, "nmean": nmean, "naps": [],
                        "dst": dst_y1}

            def emit_ln_apply_pair(st, op):
                Ybuf = OUTB
                dst_y1 = st["dst"]
                if op == 0:
                    wait("v", c_g, st["nmean"])
                    wait("v", c_ar, st["nrs"])
                wait("v", c_va, c_va.n)
                for j, scr in ((op, STV), (op + 1, RECB)):
                    np1 = vop("scalar_tensor_tensor",
                              (scr[:, :], st["b0b"].ap[:, :], -1.0,
                               Ybuf[:, j * QT:(j + 1) * QT],
                               ALU.mult, ALU.add),
                              inc=(c_va if j == op + 1 else None))
                wait("v", c_va, np1)
                nap = 0
                for j, scr in ((op, STV), (op + 1, RECB)):
                    dst = (dst_y1[:, j * 516 + 4: j * 516 + 4 + QT]
                           if dst_y1 is not None
                           else Ybuf[:, j * QT:(j + 1) * QT])
                    nap = vop("tensor_tensor",
                              (dst, scr[:, :], LTA[:, 512:1024], ALU.mult),
                              inc=c_va)
                st["naps"].append(nap)
                if op == 6:
                    st["b0b"].free = (c_va, nap)
                    last_nap[0] = nap
                return nap

            def emit_ln_bcast_apply(u, dst_y1):
                st = emit_ln_bcast_head(u, dst_y1)
                for op in range(0, 8, 2):
                    emit_ln_apply_pair(st, op)
                return st["naps"][-1], st["naps"]

            def emit_mix_pair(u, op):
                wait("v", c_va, c_va.n)
                for j, scr in ((op, STV), (op + 1, RECB)):
                    nd = vop("tensor_tensor",
                             (scr[:, :], Y1[:, j * 516 + 4: j * 516 + 4 + QT],
                              Y1[:, j * 516 + 3: j * 516 + 3 + QT],
                              ALU.subtract),
                             inc=(c_va if j == op + 1 else None))
                wait("v", c_va, nd)
                for j, scr in ((op, STV), (op + 1, RECB)):
                    vop("scalar_tensor_tensor",
                        (XRA[:, j * QT:(j + 1) * QT], scr[:, :],
                         MUS[:, j:j + 1],
                         Y1[:, j * 516 + 3: j * 516 + 3 + QT],
                         ALU.mult, ALU.add))
                for j, scr in ((op, STV), (op + 1, RECB)):
                    nk = vop("scalar_tensor_tensor",
                             (XKA[:, j * QT:(j + 1) * QT], scr[:, :],
                              MUS[:, 8 + j:9 + j],
                              Y1[:, j * 516 + 3: j * 516 + 3 + QT],
                              ALU.mult, ALU.add),
                             inc=(c_va if j == op + 1 else None))
                wait("v", c_va, nk)
                if op == 6:
                    u["mix_done"] = vop(
                        "tensor_copy",
                        (bass.AP(Y1, 3, [[NCH * 516, 128], [516, 8], [1, 1]]),
                         bass.AP(Y1, 515, [[NCH * 516, 128], [516, 8], [1, 1]])),
                        inc=c_va)

            def emit_mix(u):
                for op in range(0, 8, 2):
                    emit_mix_pair(u, op)

            def make_fillers(u, load_wo=True):
                fl = []

                def r_one(oc):
                    def f():
                        if oc == 0:
                            wait("t", c_va, u["mix_done"])
                        xrch = [XRA[:, k * QT:(k + 1) * QT] for k in range(8)]
                        def sigevac(b):
                            wait("a", c_g)
                            achain_start()
                            if last_nap[0]:
                                wait("a", c_va, last_nap[0])
                            nu = act(LTA[:, 0:512], b.ap[:, :], AF.Exp,
                                     scale=-1.0, inc=c_ar)
                            b.free = (c_ar, nu)
                            wait("a", c_ar, nu)
                            nv_ = act(LTA[:, 512:1024], LTA[:, 0:512], AF.Ln,
                                      bias=MUS[:, 17:18], inc=c_ar)
                            wait("a", c_ar, nv_)
                            nr = act(RG[:, oc * QT:(oc + 1) * QT],
                                     LTA[:, 512:1024], AF.Exp, scale=-1.0,
                                     inc=c_ar)
                            last_achain[0] = (c_ar, nr)
                        u["fr_last"] = proj_group(
                            WS[u["fr_si"]], oc, xrch, sigevac,
                            (c_w[u["fr_si"]], u["fr_n"]) if oc == 0 else None)
                        if oc == 7:
                            slot_free[u["fr_si"]] = (c_g, u["fr_last"])
                            u["fv_si"], u["fv_n"] = load_weight(FV)
                    return f

                def k_pair(op):
                    def f():
                        xkch = [XKA[:, k * QT:(k + 1) * QT] for k in range(8)]
                        be = proj_group_nb(WS[u["fk_si"]], op, xkch,
                                           (c_w[u["fk_si"]], u["fk_n"])
                                           if op == 0 else None)
                        bo = proj_group_nb(WS[u["fk_si"]], op + 1, xkch)
                        u["fk_last"] = c_g.n
                        wait("v", c_g, u["fk_last"])
                        wait("v", c_va, c_va.n)
                        for j, b_, scr in ((op, be, STV), (op + 1, bo, RECB)):
                            nv = vop("tensor_single_scalar",
                                     (scr[:, :], b_.ap[:, :], 0.0, ALU.max),
                                     inc=(c_vg if j == op + 1 else None))
                        be.free = bo.free = (c_vg, nv)
                        wait("v", c_vg, nv)
                        for j, b_, scr in ((op, be, STV), (op + 1, bo, RECB)):
                            nv = vop("tensor_tensor",
                                     (XRA[:, j * QT:(j + 1) * QT], scr[:, :],
                                      scr[:, :], ALU.mult),
                                     inc=(c_va if j == op + 1 else None))
                        u["ka_ready"] = nv
                        if op == 6:
                            slot_free[u["fk_si"]] = (c_g, u["fk_last"])
                            if load_wo:
                                u["wo2_si"], u["wo2_n"] = load_weight(WO)
                    return f

                def v_pair(op):
                    def f():
                        kach = [XRA[:, k * QT:(k + 1) * QT] for k in range(8)]
                        if op == 0:
                            wait("t", c_va, u["ka_ready"])
                        be = proj_group_nb(WS[u["fv_si"]], op, kach,
                                           (c_w[u["fv_si"]], u["fv_n"])
                                           if op == 0 else None)
                        bo = proj_group_nb(WS[u["fv_si"]], op + 1, kach)
                        u["fv_last"] = c_g.n
                        wait("v", c_g, u["fv_last"])
                        wait("v", c_ar)
                        wait("v", c_va, c_va.n)
                        for j, b_, scr in ((op, be, STV), (op + 1, bo, RECB)):
                            nv = vop("tensor_tensor",
                                     (scr[:, :], RG[:, j * QT:(j + 1) * QT],
                                      b_.ap[:, :], ALU.mult),
                                     inc=(c_vg if j == op + 1 else None))
                        be.free = bo.free = (c_vg, nv)
                        wait("v", c_vg, nv)
                        for j, scr in ((op, STV), (op + 1, RECB)):
                            nv = vop("tensor_tensor",
                                     (OUTB[:, j * QT:(j + 1) * QT], scr[:, :],
                                      Y1[:, j * 516 + 4: j * 516 + 4 + QT],
                                      ALU.add),
                                     inc=(c_va if j == op + 1 else None))
                        wait("v", c_va, nv)
                        for j in (op, op + 1):
                            nv = vop("tensor_copy",
                                     (RG[:, j * QT:(j + 1) * QT],
                                      OUTB[:, j * QT:(j + 1) * QT]),
                                     inc=(c_va if j == op + 1 else None))
                        wait("v", c_va, nv)
                        for j in (op, op + 1):
                            nv = vop("tensor_tensor",
                                     (XKA[:, j * QT:(j + 1) * QT],
                                      RG[:, j * QT:(j + 1) * QT],
                                      RG[:, j * QT:(j + 1) * QT], ALU.mult),
                                     inc=(c_va if j == op + 1 else None))
                        u["y2cast"] = nv
                        if op == 6:
                            slot_free[u["fv_si"]] = (c_g, u["fv_last"])
                    return f

                def ln2_stats():
                    if gidx[0] % 2 == 1:
                        gidx[0] += 1
                    b0 = next_bank()
                    wait("t", c_va, u["y2cast"])
                    for k in range(8):
                        mm(b0.ap[0:32, :], CSTB[:, 0:32],
                           RG[:, k * QT:(k + 1) * QT], k == 0, k == 7, tp=(0, 0))
                        mm(b0.ap[32:64, :], CSTB[:, 0:32],
                           XKA[:, k * QT:(k + 1) * QT], k == 0, k == 7,
                           tp=(0, 32), inc=(c_g if k == 7 else None))
                    u["ln2_stats"] = c_g.n
                    wait("v", c_g, u["ln2_stats"])
                    vop("tensor_scalar", (SR[0:32, :], b0.ap[0:32, :],
                                          1.0 / 1024.0, None, ALU.mult))
                    ntsv = vop("tensor_scalar", (SR[32:64, :], b0.ap[32:64, :],
                                                 1.0 / 1024.0, None, ALU.mult),
                               inc=c_va)
                    wait("v", c_va, ntsv)
                    vop("tensor_tensor", (SR2[0:32, :], SR[0:32, :],
                                          SR[0:32, :], ALU.mult))
                    nsr = vop("tensor_copy", (SR2[32:64, :], SR[32:64, :]),
                              inc=c_va)
                    b0.free = (c_va, nsr)
                    u["nsr"] = nsr

                ln2s = {}

                def ln2_head():
                    ln2s["st"] = emit_ln_bcast_head(u, None)

                def ln2_pair(op):
                    def f():
                        emit_ln_apply_pair(ln2s["st"], op)
                        if op == 6:
                            u["naps"] = ln2s["st"]["naps"]
                    return f

                for oc in range(8):
                    fl.append(r_one(oc))
                for op in range(0, 8, 2):
                    fl.append(k_pair(op))
                for op in range(0, 8, 2):
                    fl.append(v_pair(op))
                fl.append(ln2_stats)
                fl.append(ln2_head)
                for op in range(0, 8, 2):
                    fl.append(ln2_pair(op))
                return fl

            def emit_ln2_tail(u, seq_, t_, unext):
                naps = u["naps"]
                xb_free[0] = u["ln2_stats"]
                if unext is not None:
                    wait("s", c_va, u["y2cast"])
                    unext["nxf"] = dma(
                        "s",
                        bass.AP(Y1, 4, [[NCH * 516, 128], [516, 8], [1, QT]]),
                        bass.AP(xT, unext["seq"] * D * S + unext["t"] * QT,
                                [[S, 128], [128 * S, 8], [1, QT]]), c_xf)
                for i, op in enumerate(range(0, 8, 2)):
                    wait("a", c_va, naps[i])
                    out_done[0] = dma(
                        "a",
                        bass.AP(outT, seq_ * D * S + op * 128 * S + t_ * QT,
                                [[S, 128], [128 * S, 2], [1, QT]]),
                        bass.AP(OUTB, op * QT,
                                [[NCH * QT, 128], [QT, 2], [1, QT]]),
                        c_out)

            def q_group(t, oc, qev, u):
                xsrc = RG if t == 0 else XRA
                xbch = [xsrc[:, k * QT:(k + 1) * QT] for k in range(8)]
                if oc == 0:
                    wait("t", c_xb, u["nxb"])
                    wait("t", c_wqr, n_wq_res)

                def qevac(b):
                    if oc == 0:
                        wait("a", c_g)
                        nv = act(QTB[:, 0:QT], b.ap[:, :], AF.Copy, inc=c_ar)
                        b.free = (c_ar, nv)
                        qev[oc] = (c_ar, nv)
                    else:
                        wait("v", c_g)
                        nv = vop("tensor_copy",
                                 (QTB[:, oc * QT:(oc + 1) * QT], b.ap[:, :]),
                                 inc=c_vg)
                        b.free = (c_vg, nv)
                        qev[oc] = (c_vg, nv)
                proj_group(WS[2], oc, xbch, qevac)

            def emit_xb(t, u_prev, u):
                gate = u_prev["aba_free"] if u_prev is not None else xb_free[0]
                wait("g", c_g, gate)
                u["nxb"] = dma("g",
                               bass.AP(ABA, 0, [[NCH * QT, 128], [QT, 8], [1, QT]]),
                               bass.AP(xT, seq * D * S + t * QT,
                                       [[S, 128], [128 * S, 8], [1, QT]]), c_xb)

            def emit_attention(t, fillers, qev):
                att = {"den_q": [], "epi": None}
                use4[0] = False
                ao_base = c_ao.n
                pend = []
                step = [0]
                last_inj = [-2]

                def emit_scores_exp(hp, kt):
                    g = gkt[0]
                    sps = SA if (g % 2 == 0) else SB2
                    if kt == 0 and qev.get(hp):
                        wait("t", qev[hp][0], qev[hp][1])
                    if g - 2 in exp_hist:
                        wait("t", c_p, exp_hist[g - 2])
                    qa = QTB[0:64, hp * QT:(hp + 1) * QT]
                    qb = QTB[64:128, hp * QT:(hp + 1) * QT]
                    ka = KTB[0:64, hp * S + kt * 128: hp * S + (kt + 1) * 128]
                    kb = KTB[64:128, hp * S + kt * 128: hp * S + (kt + 1) * 128]
                    mm(sps[:, 0:512], ka, qa, True, True, tp=(0, 0))
                    nsc = mm(sps[:, 512:1024], kb, qb, True, True, tp=(64, 0),
                             inc=c_sc)
                    pbs = PB[:, (g % PBS) * 1024:(g % PBS) * 1024 + 1024]
                    wait("a", c_sc, nsc)
                    if g - PBS >= 0:
                        wait("a", c_kt, g - PBS + 1)
                    npp = act(pbs, sps[:, 0:1024], AF.Exp, scale=SCALE, inc=c_p)
                    exp_hist[g] = npp
                    gkt[0] += 1
                    return g

                def emit_den(hp_, kt_, g_):
                    pbs_ = PB[:, (g_ % PBS) * 1024:(g_ % PBS) * 1024 + 1024]
                    mm(DEN[0:32, :], CSTB[:, 0:32], pbs_[:, 0:512], kt_ == 0,
                       kt_ == NKT - 1, tp=(0, 0))
                    mm(DEN[32:64, :], CSTB[:, 0:32], pbs_[:, 512:1024],
                       kt_ == 0, kt_ == NKT - 1, tp=(0, 32), inc=c_kt)

                def emit_epi():
                    nrc, nrb_res, b = att["epi"]
                    att["epi"] = None
                    wait("t", c_rc, nrc)
                    nrb = mm(b.ap[:, :], CSTF[0:64, 256:384], RCR[0:64, :],
                             True, True, inc=c_rb)
                    assert nrb == nrb_res

                def emit_pv(hp_, kt_, g_):
                    wait("t", c_p, exp_hist[g_])
                    if kt_ == 0:
                        acquire(bPV)
                    pbs_ = PB[:, (g_ % PBS) * 1024:(g_ % PBS) * 1024 + 1024]
                    va = VTB[:, kt_ * D + (2 * hp_) * 64:
                             kt_ * D + (2 * hp_) * 64 + 64]
                    vb = VTB[:, kt_ * D + (2 * hp_ + 1) * 64:
                             kt_ * D + (2 * hp_ + 1) * 64 + 64]
                    mm(PV[0:64, :], va, pbs_[:, 0:512], kt_ == 0, kt_ == NKT - 1,
                       tp=(0, 0))
                    mm(PV[64:128, :], vb, pbs_[:, 512:1024], kt_ == 0,
                       kt_ == NKT - 1, tp=(0, 64))

                def flush_pv():
                    hp, kt, g = pend.pop(0)
                    if att["epi"] is not None:
                        if att["epi_cd"] == 0:
                            emit_epi()
                        else:
                            att["epi_cd"] -= 1
                    if kt < 2:
                        att["den_q"].append((hp, kt, g))
                        return
                    if kt == 2:
                        dq = att["den_q"]
                        if dq:
                            for hp_, kt_, g_ in dq:
                                emit_pv(hp_, kt_, g_)
                            acquire(bDEN)
                            while dq:
                                emit_den(*dq.pop(0))
                    emit_pv(hp, kt, g)
                    emit_den(hp, kt, g)
                    if kt == NKT - 1:
                        wait("a", c_kt, c_kt.n)
                        achain_start()
                        if last_nrb[0]:
                            wait("a", c_rb, last_nrb[0])
                        nln = act(LTA[0:64, 0:512], DEN[0:64, :], AF.Ln,
                                  inc=c_ar)
                        wait("a", c_ar, nln)
                        nrc = act(RCR[0:64, :], LTA[0:64, 0:512], AF.Exp,
                                  scale=-1.0, inc=c_rc)
                        last_achain[0] = (c_rc, nrc)
                        bDEN.free = (c_ar, nln)
                        b = next_bank()
                        nrb_res = c_rb.n + 1
                        last_nrb[0] = nrb_res
                        att["epi"] = (nrc, nrb_res, b)
                        att["epi_cd"] = 1
                        wait("a", c_rb, nrb_res)
                        ncp = act(RSB[:, :], b.ap[:, :], AF.Copy, inc=c_ar)
                        b.free = (c_ar, ncp)
                        wait("v", c_ar, ncp)
                        nao = vop("tensor_tensor",
                                  (ABA[:, hp * QT:(hp + 1) * QT], PV[:, :],
                                   RSB[:, :], ALU.mult), inc=c_ao)
                        bPV.free = (c_ao, nao)

                for hp in range(NHP):
                    for kt in range(NKT):
                        g = emit_scores_exp(hp, kt)
                        pend.append((hp, kt, g))
                        if len(pend) >= 3:
                            flush_pv()
                        step[0] += 1
                        if (fillers and fillers[0][0] <= step[0]
                                and step[0] - last_inj[0] >= 2
                                and (att["epi"] is None
                                     or (len(fillers[0]) > 2 and fillers[0][2]))):
                            use4[0] = False
                            fillers.pop(0)[1]()
                            last_inj[0] = step[0]
                while pend:
                    flush_pv()
                if att["epi"] is not None:
                    emit_epi()
                while fillers:
                    fillers.pop(0)[1]()
                return ao_base + 8

            for t in range(NT):
                u = {"wo_si": None, "t": t, "seq": seq}
                fillers = []
                qev = {}

                def mk_q(oc, _t=t, _u=u, _qev=qev):
                    def f():
                        q_group(_t, oc, _qev, _u)
                    return f

                if prev is None:
                    wait("s", c_va, n_y1_init)
                    u["nxf"] = dma(
                        "s",
                        bass.AP(Y1, 4, [[NCH * 516, 128], [516, 8], [1, QT]]),
                        bass.AP(xT, seq * D * S + t * QT,
                                [[S, 128], [128 * S, 8], [1, QT]]), c_xf)
                if prev is not None and t > 0:
                    emit_wo_pairs(prev)
                u["nxb"] = kv_nxb0[0] if t == 0 else xra_nxb[0]
                use4[0] = True
                q_group(t, 0, qev, u)
                for oc in range(1, 8):
                    fillers.append((2 * oc - 1, mk_q(oc)))

                if prev is not None:
                    def u_loads(p=prev):
                        p["fr_si"], p["fr_n"] = load_weight(FR)
                        p["fk_si"], p["fk_n"] = load_weight(FK)
                    fillers.append((15, u_loads, True))
                    if t > 0:
                        def u_stats(p=prev):
                            emit_ln1_head(p)
                        lnst = {}
                        def u_bhead(p=prev, s=lnst):
                            s["st"] = emit_ln_bcast_head(p, Y1)
                        def mk_ap(op, s=lnst):
                            return lambda: emit_ln_apply_pair(s["st"], op)
                        def mk_mx(op, p=prev):
                            return lambda: emit_mix_pair(p, op)
                        fillers.append((17, u_stats))
                        fillers.append((19, u_bhead))
                        for i, op in enumerate(range(0, 8, 2)):
                            fillers.append((21 + 4 * i, mk_ap(op), True))
                            fillers.append((23 + 4 * i, mk_mx(op), True))
                        steps = [50 + 3 * i for i in range(17)]
                        steps += [100 + 2 * i for i in range(5)]
                    else:
                        steps = [24 + 3 * i for i in range(17)]
                        steps += [75 + 2 * i for i in range(5)]
                    mkfl = make_fillers(prev, load_wo=True)
                    for i, (st, f) in enumerate(zip(steps, mkfl)):
                        fillers.append((st, f, i >= len(mkfl) - 4))

                u["ao_done"] = emit_attention(t, fillers, qev)
                if prev is not None:
                    u["wo_si"], u["wo_n"] = prev["wo2_si"], prev["wo2_n"]
                    emit_ln2_tail(prev, prev["seq"], prev["t"], u)
                else:
                    wo_si0, wo_n0 = load_weight(WO)
                    u["wo_si"], u["wo_n"] = wo_si0, wo_n0
                if t < NT - 1:
                    wait("g", c_xb, c_xb.n)
                    if prev is not None:
                        wait("g", c_g, prev["fv_last"])
                    xra_nxb[0] = dma(
                        "g",
                        bass.AP(XRA, 0, [[NCH * QT, 128], [QT, 8], [1, QT]]),
                        bass.AP(xT, seq * D * S + (t + 1) * QT,
                                [[S, 128], [128 * S, 8], [1, QT]]), c_xb)
                prev = u

        emit_wo_pairs(prev)
        last_ycast[0] = prev["yadds"]
        prev["fr_si"], prev["fr_n"] = load_weight(FR)
        prev["fk_si"], prev["fk_n"] = load_weight(FK)
        emit_ln1_head(prev)
        emit_ln_bcast_apply(prev, Y1)
        emit_mix(prev)
        for f in make_fillers(prev, load_wo=False):
            f()
        emit_ln2_tail(prev, prev["seq"], prev["t"], None)

        wait("s", c_out, c_out.n)

        with nc.Block() as block:
            def runner(key):
                def _run(e):
                    for f in plan[key]:
                        f(e)
                return _run
            block.tensor(runner("t"))
            block.scalar(runner("a"))
            block.vector(runner("v"))
            block.sync(runner("s"))
            block.gpsimd(runner("g"))

    return nc


_NC_CACHE = [None]


def kernel(L0_attn_x, L0_attn_kernel, L0_attn_WO, L0_ffn_W_r, L0_ffn_W_k,
           L0_ffn_W_v, L0_ffn_mu_r, L0_ffn_mu_k):
    x = np.asarray(L0_attn_x, dtype=np.float32)
    ker = np.asarray(L0_attn_kernel, dtype=np.float32)
    wo = np.asarray(L0_attn_WO, dtype=np.float32)[0]
    wr = np.asarray(L0_ffn_W_r, dtype=np.float32)
    wk = np.asarray(L0_ffn_W_k, dtype=np.float32)
    wv = np.asarray(L0_ffn_W_v, dtype=np.float32)
    mur = np.asarray(L0_ffn_mu_r, dtype=np.float32)
    muk = np.asarray(L0_ffn_mu_k, dtype=np.float32)

    B = x.shape[0]
    ncore = 8
    wq_l = np.ascontiguousarray(ker[0].transpose(1, 0, 2).reshape(D, D))
    wk_l = np.ascontiguousarray(ker[1].transpose(1, 0, 2).reshape(D, D))
    wv_l = np.ascontiguousarray(ker[2].transpose(1, 0, 2).reshape(D, D))
    wo_l = np.ascontiguousarray(wo.reshape(64, 16, D).transpose(1, 0, 2).reshape(D, D))
    wts = np.stack([wk_l, wv_l, wq_l, wo_l,
                    np.ascontiguousarray(wr.T),
                    np.ascontiguousarray(wk.T),
                    np.ascontiguousarray(wv.T)], axis=0)

    mus = np.zeros((128, 18), np.float32)
    for c in range(8):
        mus[:, c] = mur[c * 128:(c + 1) * 128]
        mus[:, 8 + c] = muk[c * 128:(c + 1) * 128]
    mus[:, 16] = EPS
    mus[:, 17] = 1.0

    cst = np.zeros((128, 416), np.float32)
    cst[:, 0:32] = 1.0
    cst[0, 32:160] = 1.0
    cst[0, 160:288] = -1.0
    cst[32, 160:288] = 1.0
    cst[0, 288:352] = 1.0
    cst[32, 352:416] = 1.0

    xT = np.ascontiguousarray(x.transpose(0, 2, 1))

    if _NC_CACHE[0] is None:
        _NC_CACHE[0] = build_nc()
    nc = _NC_CACHE[0]

    in_maps = [{"xT": np.ascontiguousarray(xT[2 * c:2 * c + 2]),
                "wts": wts, "mus": mus, "cst": cst} for c in range(ncore)]
    res = run_bass_kernel_spmd(nc, in_maps, core_ids=list(range(ncore)))
    out = np.empty((B, S, D), np.float32)
    for c in range(ncore):
        o = res.results[c]["outT"]
        out[2 * c] = o[0].T
        out[2 * c + 1] = o[1].T
    return out


# version 34
# speedup vs baseline: 1.4534x; 1.0098x over previous
import contextlib
import numpy as np
import ml_dtypes
import concourse.bass as bass
import concourse.mybir as mybir
from concourse.bass_utils import run_bass_kernel_spmd

F32 = mybir.dt.float32
BF16 = mybir.dt.bfloat16
AF = mybir.ActivationFunctionType
ALU = mybir.AluOpType

D = 1024
S = 2048
NSEQ = 2
NCH = 8
QT = 512
NT = S // QT
NHP = 8
NKT = S // 128
PBS = 3
SCALE = 1.0 / 65536.0
EPS = 1e-5

WK, WV, WQ, WO, FR, FK, FV = range(7)


class Cnt:
    def __init__(self, name):
        self.name = name
        self.n = 0


def build_nc():
    nc = bass.Bass("TRN2", target_bir_lowering=False)

    xT = nc.dram_tensor("xT", [NSEQ, D, S], F32, kind="ExternalInput")
    wts = nc.dram_tensor("wts", [7, D, D], F32, kind="ExternalInput")
    mus = nc.dram_tensor("mus", [128, 18], F32, kind="ExternalInput")
    cst = nc.dram_tensor("cst", [128, 416], F32, kind="ExternalInput")
    outT = nc.dram_tensor("outT", [NSEQ, D, S], F32, kind="ExternalOutput")
    wsc = nc.dram_tensor("wsc", [7 * D, D], BF16, kind="Internal")

    plan = {k: [] for k in "tavsg"}
    sems = {}
    sem_handles = {}

    def em(eng, fn):
        plan[eng].append(fn)

    def C(name):
        c = Cnt(name)
        sems[name] = c
        return c

    c_cstb = C("c_cstb")
    c_wqr = C("c_wqr")
    c_cst = C("c_cst")
    c_wc0 = C("c_wc0")
    c_wc1 = C("c_wc1")
    c_ws0 = C("c_ws0")
    c_ws1 = C("c_ws1")
    c_w0 = C("c_w0")
    c_w1 = C("c_w1")
    c_xf = C("c_xf")
    c_xb = C("c_xb")
    c_out = C("c_out")
    c_sc = C("c_sc")
    c_p = C("c_p")
    c_kt = C("c_kt")
    c_rc = C("c_rc")
    c_g = C("c_g")
    c_vg = C("c_vg")
    c_ao = C("c_ao")
    c_rb = C("c_rb")
    c_tf = C("c_tf")
    c_va = C("c_va")
    c_ar = C("c_ar")

    def wait(eng, c, val=None):
        v = c.n if val is None else val
        if v <= 0:
            return
        em(eng, lambda e, _n=c.name, _v=v: e.wait_ge(sem_handles[_n], _v))

    def bump(c, by=1):
        c.n += by
        return c.n

    def dma(eng, dst, src, c):
        n = bump(c, 16)
        em(eng, lambda e, _d=dst, _s=src, _n=c.name:
           e.dma_start(_d, _s).then_inc(sem_handles[_n], 16))
        return n

    def mm(out, lhsT, rhs, start, stop, tp=None, inc=None):
        n = bump(inc) if inc is not None else None
        def f(e, _o=out, _l=lhsT, _r=rhs, _st=start, _sp=stop, _tp=tp, _inc=inc):
            i = e.matmul(_o, _l, _r, start=_st, stop=_sp, tile_position=_tp,
                         skip_group_check=True)
            if _inc is not None:
                i.then_inc(sem_handles[_inc.name], 1)
        em("t", f)
        return n

    def act(out, in_, func, scale=1.0, bias=0.0, inc=None):
        n = bump(inc) if inc is not None else None
        def f(e, _o=out, _i=in_, _f=func, _s=scale, _b=bias, _inc=inc):
            i = e.activation(_o, _i, _f, scale=_s, bias=_b)
            if _inc is not None:
                i.then_inc(sem_handles[_inc.name], 1)
        em("a", f)
        return n

    def vop(name, args, inc=None):
        n = bump(inc) if inc is not None else None
        def f(e, _nm=name, _a=args, _inc=inc):
            i = getattr(e, _nm)(*_a)
            if _inc is not None:
                i.then_inc(sem_handles[_inc.name], 1)
        em("v", f)
        return n

    def tnop_inc(c):
        n = bump(c)
        em("t", lambda e, _n=c.name: e.nop().then_inc(sem_handles[_n], 1))
        return n

    with contextlib.ExitStack() as ctx:
        sbt = lambda name, shape, dt: ctx.enter_context(nc.sbuf_tensor(name, shape, dt))
        pst = lambda name, shape: ctx.enter_context(nc.psum_tensor(name, shape, F32))

        WS = [sbt("ws0", [128, 8 * D], BF16), sbt("ws1", [128, 8 * D], BF16),
              sbt("ws2", [128, 8 * D], BF16)]
        KTB = sbt("ktb", [128, NHP * S], BF16)
        VTB = sbt("vtb", [128, NKT * D], BF16)
        QTB = sbt("qtb", [128, NCH * QT], BF16)
        PB = sbt("pb", [128, PBS * 1024], BF16)
        ABA = sbt("aba", [128, NCH * QT], BF16)
        XKA = sbt("xka", [128, NCH * QT], BF16)
        XRA = sbt("xra", [128, NCH * QT], BF16)
        RG = sbt("rg", [128, NCH * QT], BF16)
        OUTB = sbt("outb", [128, NCH * QT], F32)
        Y1 = sbt("y1", [128, NCH * 516], F32)
        STV = sbt("stv", [128, QT], F32)
        LTA = sbt("lta", [128, 1024], F32)
        RSB = sbt("rsb", [128, QT], F32)
        RECB = sbt("recb", [128, QT], F32)
        RCR = sbt("rcr", [64, QT], F32)
        SR = sbt("sr", [64, QT], F32)
        SR2 = sbt("sr2", [64, QT], F32)
        CSTF = sbt("cstf", [64, 384], F32)
        CSTB = sbt("cstb", [128, 32], BF16)
        MUS = sbt("musb", [128, 18], F32)

        SA = pst("psa", [128, 1024])
        SB2 = pst("psb", [128, 1024])
        PV = pst("pspv", [128, 512])
        DEN = pst("psden", [128, 512])
        G0 = pst("psg0", [128, 512])
        G1 = pst("psg1", [128, 512])

        for name in sems:
            sem_handles[name] = ctx.enter_context(nc.semaphore(name))

        class Bank:
            def __init__(self, ap):
                self.ap = ap
                self.free = None
        bG = [Bank(G0), Bank(G1)]
        bPV = Bank(PV)
        bDEN = Bank(DEN)
        pool4 = [bG[0], bG[1], bPV, bDEN]

        def acquire(b):
            if b.free is not None:
                wait("t", b.free[0], b.free[1])
                b.free = None

        gidx = [0]
        hidx = [0]
        use4 = [False]

        def next_bank():
            if use4[0]:
                b = pool4[hidx[0] % 4]
                hidx[0] += 1
            else:
                b = bG[gidx[0] % 2]
                gidx[0] += 1
            acquire(b)
            return b

        def proj_group(ws_slot, oc, rhs_chunks, evac, w_ready=None):
            b = next_bank()
            if w_ready is not None:
                wait("t", w_ready[0], w_ready[1])
            for k in range(8):
                lhsT = ws_slot[:, k * D + oc * 128: k * D + (oc + 1) * 128]
                mm(b.ap[:, :], lhsT, rhs_chunks[k], start=(k == 0), stop=(k == 7),
                   inc=(c_g if k == 7 else None))
            evac(b)
            return c_g.n

        def proj_group_nb(ws_slot, oc, rhs_chunks, w_ready=None):
            b = next_bank()
            if w_ready is not None:
                wait("t", w_ready[0], w_ready[1])
            for k in range(8):
                lhsT = ws_slot[:, k * D + oc * 128: k * D + (oc + 1) * 128]
                mm(b.ap[:, :], lhsT, rhs_chunks[k], start=(k == 0), stop=(k == 7),
                   inc=(c_g if k == 7 else None))
            return b

        def vnop():
            em("v", lambda e: e.engine_nop())

        def layer_norm(YBF, YSQ, Ybuf, dst_y1, ycast_n, free_aba_after_stats,
                       ycast_sem=None):
            if gidx[0] % 2 == 1:
                gidx[0] += 1
            b0 = next_bank()
            wait("t", ycast_sem if ycast_sem is not None else c_vg, ycast_n)
            for k in range(8):
                mm(b0.ap[0:32, :], CSTB[:, 0:32], YBF[:, k * QT:(k + 1) * QT],
                   k == 0, k == 7, tp=(0, 0))
                mm(b0.ap[32:64, :], CSTB[:, 0:32], YSQ[:, k * QT:(k + 1) * QT],
                   k == 0, k == 7, tp=(0, 32), inc=(c_g if k == 7 else None))
            nstats = c_g.n
            wait("v", c_g, nstats)
            vop("tensor_scalar", (SR[0:32, :], b0.ap[0:32, :], 1.0 / 1024.0, None,
                                  ALU.mult))
            ntsv = vop("tensor_scalar", (SR[32:64, :], b0.ap[32:64, :], 1.0 / 1024.0,
                                         None, ALU.mult), inc=c_va)
            wait("v", c_va, ntsv)
            vop("tensor_tensor", (SR2[0:32, :], SR[0:32, :], SR[0:32, :], ALU.mult))
            nsr = vop("tensor_copy", (SR2[32:64, :], SR[32:64, :]), inc=c_va)
            b0.free = (c_va, nsr)
            b1 = next_bank()
            wait("t", c_va, nsr)
            nvar = mm(b1.ap[:, :], CSTF[0:64, 128:256], SR2[0:64, :], True, True,
                      inc=c_g)
            b0b = next_bank()
            nmean = mm(b0b.ap[:, :], CSTF[0:64, 0:128], SR[0:64, :], True, True,
                       inc=c_g)
            wait("a", c_g, nvar)
            achain_start()
            if last_nap[0]:
                wait("a", c_va, last_nap[0])
            nln = act(LTA[:, 0:512], b1.ap[:, :], AF.Ln, bias=MUS[:, 16:17],
                      inc=c_ar)
            wait("a", c_ar, nln)
            nrs = act(RSB[:, :], LTA[:, 0:512], AF.Exp, scale=-0.5, inc=c_ar)
            last_achain[0] = (c_ar, nrs)
            b1.free = (c_ar, nrs)
            wait("v", c_g, nmean)
            wait("v", c_ar, nrs)
            nap = 0
            for op in range(0, 8, 2):
                if op:
                    wait("v", c_va, c_va.n)
                for j, scr in ((op, STV), (op + 1, RECB)):
                    np1 = vop("scalar_tensor_tensor",
                              (scr[:, :], b0b.ap[:, :], -1.0,
                               Ybuf[:, j * QT:(j + 1) * QT],
                               ALU.mult, ALU.add),
                              inc=(c_va if j == op + 1 else None))
                wait("v", c_va, np1)
                for j, scr in ((op, STV), (op + 1, RECB)):
                    dst = (dst_y1[:, j * 516 + 4: j * 516 + 4 + QT]
                           if dst_y1 is not None else Ybuf[:, j * QT:(j + 1) * QT])
                    nap = vop("tensor_tensor", (dst, scr[:, :], RSB[:, :], ALU.mult),
                              inc=c_va)
            b0b.free = (c_va, nap)
            last_nap[0] = nap
            return nap, nstats

        n_y1_init = vop("memset", (Y1[:, :], 0.0), inc=c_va)
        dma("g", CSTB[:, :], cst[:, 0:32], c_cstb)
        n_wq_res = dma("g", bass.AP(WS[2], 0, [[8 * D, 128], [D, 8], [1, D]]),
                       bass.AP(wts, WQ * D * D, [[D, 128], [128 * D, 8], [1, D]]),
                       c_wqr)
        dma("s", CSTF[:, :], cst[0:64, 32:416], c_cst)
        dma("s", MUS[:, :], mus[:, :], c_cst)
        wait("t", c_cstb, 16)
        wait("t", c_cst, 32)
        wait("v", c_cst, 32)

        c_wc = [c_wc0, c_wc1]
        c_wsx = [c_ws0, c_ws1]
        c_w = [c_w0, c_w1]
        kv0_gate = [0, 0]

        def emit_wcast(m):
            slot = WS[m % 2]
            if m >= 2:
                wait("g", c_wsx[m % 2], c_wsx[m % 2].n)
                wait("g", c_g, kv0_gate[m % 2])
            nld = dma("g", bass.AP(slot, 0, [[8 * D, 128], [D, 8], [1, D]]),
                      bass.AP(wts, m * D * D, [[D, 128], [128 * D, 8], [1, D]]),
                      c_wc[m % 2])
            wait("s", c_wc[m % 2], nld)
            dma("s", bass.AP(wsc, m * D * D, [[D, 128], [128 * D, 8], [1, D]]),
                bass.AP(slot, 0, [[8 * D, 128], [D, 8], [1, D]]), c_wsx[m % 2])
            return nld

        wk0_n = emit_wcast(0)
        wv0_n = emit_wcast(1)

        wnext = [0]
        scratch_all = [(c_ws0, c_ws0.n), (c_ws1, c_ws1.n)]
        slot_free = [None, None]
        first_load = [True]

        def load_weight(mat):
            si = wnext[0]
            wnext[0] = 1 - si
            if first_load[0]:
                wait("s", scratch_all[0][0], scratch_all[0][1])
                wait("s", scratch_all[1][0], scratch_all[1][1])
                first_load[0] = False
            fa = slot_free[si]
            if fa is not None:
                wait("s", fa[0], fa[1])
            n = dma("s", bass.AP(WS[si], 0, [[8 * D, 128], [D, 8], [1, D]]),
                    bass.AP(wsc, mat * D * D, [[D, 128], [128 * D, 8], [1, D]]),
                    c_w[si])
            return si, n

        exp_hist = {}
        den_hist = {}
        last_achain = [None]
        last_nrb = [0]
        last_nap = [0]

        def achain_start():
            if last_achain[0] is not None:
                wait("a", last_achain[0][0], last_achain[0][1])
        gkt = [0]
        kv_nxb0 = [0]
        xra_nxb = [0]
        xb_free = [0]
        out_done = [0]
        prev = None
        last_ycast = [0]

        for seq in range(NSEQ):
            def emit_kv(kv_fillers):
                use4[0] = True
                if seq == 0:
                    wk_si, wv_si = 0, 1
                    wait("t", c_wc[0], wk0_n)
                    wait("t", c_wc[1], wv0_n)
                else:
                    wk_si, wk_n = load_weight(WK)
                    wv_si, wv_n = load_weight(WV)
                    wait("t", c_w[wk_si], wk_n)
                    wait("t", c_w[wv_si], wv_n)
                wk_last = wv_last = 0
                order = (1, 2, 3, 0)
                dsts = {1: ABA if seq == 0 else XRA, 2: RG, 3: ABA, 0: RG}
                buf_free = {id(ABA): (c_g, xb_free[0]), id(RG): None,
                            id(XRA): None}
                loads = {}

                def emit_load(tb):
                    buf = dsts[tb]
                    wait("g", c_xb, c_xb.n)
                    fa = buf_free[id(buf)]
                    if fa is not None:
                        wait("g", fa[0], fa[1])
                    loads[tb] = dma(
                        "g",
                        bass.AP(buf, 0, [[NCH * QT, 128], [QT, 8], [1, QT]]),
                        bass.AP(xT, seq * D * S + tb * QT,
                                [[S, 128], [128 * S, 8], [1, QT]]), c_xb)

                if seq == 0:
                    emit_load(1)
                else:
                    loads[1] = xra_nxb[0]
                for idx, tb in enumerate(order):
                    buf = dsts[tb]
                    nxb = loads[tb]
                    if tb == 0:
                        kv_nxb0[0] = nxb
                    wait("t", c_xb, nxb)
                    xbch = [buf[:, k * QT:(k + 1) * QT] for k in range(8)]
                    for hp in range(NHP):
                        b = next_bank()
                        for k in range(8):
                            mm(b.ap[:, :],
                               WS[wk_si][:, k * D + hp * 128: k * D + (hp + 1) * 128],
                               xbch[k], start=(k == 0), stop=(k == 7),
                               inc=(c_g if k == 7 else None))
                        wk_last = c_g.n
                        wait("v", c_g)
                        nv = vop("tensor_copy",
                                 (KTB[:, hp * S + tb * QT: hp * S + (tb + 1) * QT],
                                  b.ap[:, :]), inc=c_vg)
                        b.free = (c_vg, nv)
                    if kv_fillers:
                        kv_fillers.pop(0)()
                    if idx + 1 < len(order):
                        if buf_free[id(RG)] is None:
                            buf_free[id(RG)] = (c_g, c_g.n)
                        emit_load(order[idx + 1])
                    for tt in range(4):
                        for half in range(2):
                            b = next_bank()
                            for k in range(8):
                                mm(b.ap[:, :], xbch[k][:, tt * 128:(tt + 1) * 128],
                                   WS[wv_si][:, k * D + half * 512:
                                             k * D + half * 512 + 512],
                                   start=(k == 0), stop=(k == 7),
                                   inc=(c_g if k == 7 else None))
                            wv_last = c_g.n
                            kti = tb * 4 + tt
                            wait("v", c_g)
                            nv = vop("tensor_copy",
                                     (VTB[:, kti * D + half * 512:
                                          kti * D + half * 512 + 512],
                                      b.ap[:, :]), inc=c_vg)
                            b.free = (c_vg, nv)
                    if kv_fillers:
                        kv_fillers.pop(0)()
                    buf_free[id(buf)] = (c_g, wv_last)
                    xb_free[0] = wv_last
                slot_free[wk_si] = (c_g, wk_last)
                slot_free[wv_si] = (c_g, wv_last)
                if seq == 0:
                    kv0_gate[0], kv0_gate[1] = wk_last, wv_last
                    for m in range(2, 7):
                        emit_wcast(m)
                    scratch_all[0] = (c_ws0, c_ws0.n)
                    scratch_all[1] = (c_ws1, c_ws1.n)
                while kv_fillers:
                    kv_fillers.pop(0)()
                use4[0] = False

            kvf = []
            if prev is not None:
                emit_wo_pairs(prev)
                last_ycast[0] = prev["yadds"]
                xb_free[0] = prev["aba_free"]
                def kv_mx(op, p=prev):
                    return lambda: emit_mix_pair(p, op)

                kvf = [lambda p=prev: emit_ln1_head(p),
                       lambda p=prev: emit_ln_bcast_apply(p, Y1)]
                for op in range(0, 8, 2):
                    kvf.append(kv_mx(op))
            emit_kv(kvf)

            wait("v", c_va, c_va.n)
            for cch in range(NCH):
                vop("memset", (Y1[:, cch * 516 + 3: cch * 516 + 4], 0.0))

            def emit_wo_pairs(u):
                use4[0] = True
                aoch = [ABA[:, k * QT:(k + 1) * QT] for k in range(8)]
                wait("t", c_ao, u["ao_done"])
                wo_last = 0
                nv = 0
                for op in range(0, 8, 2):
                    be = proj_group_nb(WS[u["wo_si"]], op, aoch,
                                       (c_w[u["wo_si"]], u["wo_n"]) if op == 0
                                       else None)
                    bo = proj_group_nb(WS[u["wo_si"]], op + 1, aoch)
                    wo_last = c_g.n
                    wait("v", c_g, wo_last)
                    if op == 0:
                        wait("v", c_xf, u["nxf"])
                        if out_done[0]:
                            wait("v", c_out, out_done[0])
                    for j, b_ in ((op, be), (op + 1, bo)):
                        nv = vop("tensor_tensor",
                                 (OUTB[:, j * QT:(j + 1) * QT], b_.ap[:, :],
                                  Y1[:, j * 516 + 4: j * 516 + 4 + QT], ALU.add),
                                 inc=(c_vg if j == op + 1 else None))
                    be.free = bo.free = (c_vg, nv)
                slot_free[u["wo_si"]] = (c_g, wo_last)
                u["aba_free"] = wo_last
                u["yadds"] = nv
                wait("v", c_vg, c_vg.n)
                ycast = 0
                for op in range(0, 8, 2):
                    if op:
                        wait("v", c_va, c_va.n)
                    for j in (op, op + 1):
                        nv = vop("tensor_tensor",
                                 (RG[:, j * QT:(j + 1) * QT],
                                  OUTB[:, j * QT:(j + 1) * QT],
                                  OUTB[:, j * QT:(j + 1) * QT], ALU.max),
                                 inc=(c_va if j == op + 1 else None))
                    wait("v", c_va, nv)
                    for j in (op, op + 1):
                        nv = vop("tensor_tensor",
                                 (XKA[:, j * QT:(j + 1) * QT],
                                  RG[:, j * QT:(j + 1) * QT],
                                  RG[:, j * QT:(j + 1) * QT], ALU.mult),
                                 inc=(c_va if j == op + 1 else None))
                    ycast = nv
                u["ycast"] = ycast

            def emit_ln1_head(u):
                if gidx[0] % 2 == 1:
                    gidx[0] += 1
                b0 = next_bank()
                wait("t", c_va, u["ycast"])
                for k in range(8):
                    mm(b0.ap[0:32, :], CSTB[:, 0:32], RG[:, k * QT:(k + 1) * QT],
                       k == 0, k == 7, tp=(0, 0))
                    mm(b0.ap[32:64, :], CSTB[:, 0:32], XKA[:, k * QT:(k + 1) * QT],
                       k == 0, k == 7, tp=(0, 32), inc=(c_g if k == 7 else None))
                nstats = c_g.n
                wait("v", c_g, nstats)
                vop("tensor_scalar", (SR[0:32, :], b0.ap[0:32, :], 1.0 / 1024.0,
                                      None, ALU.mult))
                ntsv = vop("tensor_scalar", (SR[32:64, :], b0.ap[32:64, :],
                                             1.0 / 1024.0, None, ALU.mult),
                           inc=c_va)
                wait("v", c_va, ntsv)
                vop("tensor_tensor", (SR2[0:32, :], SR[0:32, :], SR[0:32, :],
                                      ALU.mult))
                nsr = vop("tensor_copy", (SR2[32:64, :], SR[32:64, :]), inc=c_va)
                b0.free = (c_va, nsr)
                u["nsr"] = nsr

            def emit_ln_bcast_head(u, dst_y1):
                b1 = next_bank()
                wait("t", c_va, u["nsr"])
                nvar = mm(b1.ap[:, :], CSTF[0:64, 128:256], SR2[0:64, :], True,
                          True, inc=c_g)
                b0b = next_bank()
                nmean = mm(b0b.ap[:, :], CSTF[0:64, 0:128], SR[0:64, :], True,
                           True, inc=c_g)
                wait("a", c_g, nvar)
                achain_start()
                if last_nap[0]:
                    wait("a", c_va, last_nap[0])
                nln = act(LTA[:, 0:512], b1.ap[:, :], AF.Ln, bias=MUS[:, 16:17],
                          inc=c_ar)
                wait("a", c_ar, nln)
                nrs = act(LTA[:, 512:1024], LTA[:, 0:512], AF.Exp, scale=-0.5,
                          inc=c_ar)
                last_achain[0] = (c_ar, nrs)
                b1.free = (c_ar, nrs)
                return {"b0b": b0b, "nrs": nrs, "nmean": nmean, "naps": [],
                        "dst": dst_y1}

            def emit_ln_apply_pair(st, op):
                Ybuf = OUTB
                dst_y1 = st["dst"]
                if op == 0:
                    wait("v", c_g, st["nmean"])
                    wait("v", c_ar, st["nrs"])
                wait("v", c_va, c_va.n)
                for j, scr in ((op, STV), (op + 1, RECB)):
                    np1 = vop("scalar_tensor_tensor",
                              (scr[:, :], st["b0b"].ap[:, :], -1.0,
                               Ybuf[:, j * QT:(j + 1) * QT],
                               ALU.mult, ALU.add),
                              inc=(c_va if j == op + 1 else None))
                wait("v", c_va, np1)
                nap = 0
                for j, scr in ((op, STV), (op + 1, RECB)):
                    dst = (dst_y1[:, j * 516 + 4: j * 516 + 4 + QT]
                           if dst_y1 is not None
                           else Ybuf[:, j * QT:(j + 1) * QT])
                    nap = vop("tensor_tensor",
                              (dst, scr[:, :], LTA[:, 512:1024], ALU.mult),
                              inc=c_va)
                st["naps"].append(nap)
                if op == 6:
                    st["b0b"].free = (c_va, nap)
                    last_nap[0] = nap
                return nap

            def emit_ln_bcast_apply(u, dst_y1):
                st = emit_ln_bcast_head(u, dst_y1)
                for op in range(0, 8, 2):
                    emit_ln_apply_pair(st, op)
                return st["naps"][-1], st["naps"]

            def emit_mix_pair(u, op):
                wait("v", c_va, c_va.n)
                for j, scr in ((op, STV), (op + 1, RECB)):
                    nd = vop("tensor_tensor",
                             (scr[:, :], Y1[:, j * 516 + 4: j * 516 + 4 + QT],
                              Y1[:, j * 516 + 3: j * 516 + 3 + QT],
                              ALU.subtract),
                             inc=(c_va if j == op + 1 else None))
                wait("v", c_va, nd)
                for j, scr in ((op, STV), (op + 1, RECB)):
                    vop("scalar_tensor_tensor",
                        (XRA[:, j * QT:(j + 1) * QT], scr[:, :],
                         MUS[:, j:j + 1],
                         Y1[:, j * 516 + 3: j * 516 + 3 + QT],
                         ALU.mult, ALU.add))
                for j, scr in ((op, STV), (op + 1, RECB)):
                    nk = vop("scalar_tensor_tensor",
                             (XKA[:, j * QT:(j + 1) * QT], scr[:, :],
                              MUS[:, 8 + j:9 + j],
                              Y1[:, j * 516 + 3: j * 516 + 3 + QT],
                              ALU.mult, ALU.add),
                             inc=(c_va if j == op + 1 else None))
                wait("v", c_va, nk)
                if op == 6:
                    u["mix_done"] = vop(
                        "tensor_copy",
                        (bass.AP(Y1, 3, [[NCH * 516, 128], [516, 8], [1, 1]]),
                         bass.AP(Y1, 515, [[NCH * 516, 128], [516, 8], [1, 1]])),
                        inc=c_va)

            def emit_mix(u):
                for op in range(0, 8, 2):
                    emit_mix_pair(u, op)

            def make_fillers(u, load_wo=True):
                fl = []

                def r_one(oc):
                    def f():
                        if oc == 0:
                            wait("t", c_va, u["mix_done"])
                        xrch = [XRA[:, k * QT:(k + 1) * QT] for k in range(8)]
                        def sigevac(b):
                            wait("a", c_g)
                            achain_start()
                            if last_nap[0]:
                                wait("a", c_va, last_nap[0])
                            nu = act(LTA[:, 0:512], b.ap[:, :], AF.Exp,
                                     scale=-1.0, inc=c_ar)
                            b.free = (c_ar, nu)
                            wait("a", c_ar, nu)
                            nv_ = act(LTA[:, 512:1024], LTA[:, 0:512], AF.Ln,
                                      bias=MUS[:, 17:18], inc=c_ar)
                            wait("a", c_ar, nv_)
                            nr = act(RG[:, oc * QT:(oc + 1) * QT],
                                     LTA[:, 512:1024], AF.Exp, scale=-1.0,
                                     inc=c_ar)
                            last_achain[0] = (c_ar, nr)
                        u["fr_last"] = proj_group(
                            WS[u["fr_si"]], oc, xrch, sigevac,
                            (c_w[u["fr_si"]], u["fr_n"]) if oc == 0 else None)
                        if oc == 7:
                            slot_free[u["fr_si"]] = (c_g, u["fr_last"])
                            u["fv_si"], u["fv_n"] = load_weight(FV)
                    return f

                def k_pair(op):
                    def f():
                        xkch = [XKA[:, k * QT:(k + 1) * QT] for k in range(8)]
                        be = proj_group_nb(WS[u["fk_si"]], op, xkch,
                                           (c_w[u["fk_si"]], u["fk_n"])
                                           if op == 0 else None)
                        bo = proj_group_nb(WS[u["fk_si"]], op + 1, xkch)
                        u["fk_last"] = c_g.n
                        wait("v", c_g, u["fk_last"])
                        wait("v", c_va, c_va.n)
                        for j, b_, scr in ((op, be, STV), (op + 1, bo, RECB)):
                            nv = vop("tensor_single_scalar",
                                     (scr[:, :], b_.ap[:, :], 0.0, ALU.max),
                                     inc=(c_vg if j == op + 1 else None))
                        be.free = bo.free = (c_vg, nv)
                        wait("v", c_vg, nv)
                        for j, b_, scr in ((op, be, STV), (op + 1, bo, RECB)):
                            nv = vop("tensor_tensor",
                                     (XRA[:, j * QT:(j + 1) * QT], scr[:, :],
                                      scr[:, :], ALU.mult),
                                     inc=(c_va if j == op + 1 else None))
                        u["ka_ready"] = nv
                        if op == 6:
                            slot_free[u["fk_si"]] = (c_g, u["fk_last"])
                            if load_wo:
                                u["wo2_si"], u["wo2_n"] = load_weight(WO)
                    return f

                def v_pair(op):
                    def f():
                        kach = [XRA[:, k * QT:(k + 1) * QT] for k in range(8)]
                        if op == 0:
                            wait("t", c_va, u["ka_ready"])
                        be = proj_group_nb(WS[u["fv_si"]], op, kach,
                                           (c_w[u["fv_si"]], u["fv_n"])
                                           if op == 0 else None)
                        bo = proj_group_nb(WS[u["fv_si"]], op + 1, kach)
                        u["fv_last"] = c_g.n
                        wait("v", c_g, u["fv_last"])
                        wait("v", c_ar)
                        wait("v", c_va, c_va.n)
                        for j, b_, scr in ((op, be, STV), (op + 1, bo, RECB)):
                            nv = vop("tensor_tensor",
                                     (scr[:, :], RG[:, j * QT:(j + 1) * QT],
                                      b_.ap[:, :], ALU.mult),
                                     inc=(c_vg if j == op + 1 else None))
                        be.free = bo.free = (c_vg, nv)
                        wait("v", c_vg, nv)
                        for j, scr in ((op, STV), (op + 1, RECB)):
                            nv = vop("tensor_tensor",
                                     (OUTB[:, j * QT:(j + 1) * QT], scr[:, :],
                                      Y1[:, j * 516 + 4: j * 516 + 4 + QT],
                                      ALU.add),
                                     inc=(c_va if j == op + 1 else None))
                        wait("v", c_va, nv)
                        for j in (op, op + 1):
                            nv = vop("tensor_copy",
                                     (RG[:, j * QT:(j + 1) * QT],
                                      OUTB[:, j * QT:(j + 1) * QT]),
                                     inc=(c_va if j == op + 1 else None))
                        wait("v", c_va, nv)
                        for j in (op, op + 1):
                            nv = vop("tensor_tensor",
                                     (XKA[:, j * QT:(j + 1) * QT],
                                      RG[:, j * QT:(j + 1) * QT],
                                      RG[:, j * QT:(j + 1) * QT], ALU.mult),
                                     inc=(c_va if j == op + 1 else None))
                        u["y2cast"] = nv
                        if op == 6:
                            slot_free[u["fv_si"]] = (c_g, u["fv_last"])
                    return f

                def ln2_stats():
                    if gidx[0] % 2 == 1:
                        gidx[0] += 1
                    b0 = next_bank()
                    wait("t", c_va, u["y2cast"])
                    for k in range(8):
                        mm(b0.ap[0:32, :], CSTB[:, 0:32],
                           RG[:, k * QT:(k + 1) * QT], k == 0, k == 7, tp=(0, 0))
                        mm(b0.ap[32:64, :], CSTB[:, 0:32],
                           XKA[:, k * QT:(k + 1) * QT], k == 0, k == 7,
                           tp=(0, 32), inc=(c_g if k == 7 else None))
                    u["ln2_stats"] = c_g.n
                    wait("v", c_g, u["ln2_stats"])
                    vop("tensor_scalar", (SR[0:32, :], b0.ap[0:32, :],
                                          1.0 / 1024.0, None, ALU.mult))
                    ntsv = vop("tensor_scalar", (SR[32:64, :], b0.ap[32:64, :],
                                                 1.0 / 1024.0, None, ALU.mult),
                               inc=c_va)
                    wait("v", c_va, ntsv)
                    vop("tensor_tensor", (SR2[0:32, :], SR[0:32, :],
                                          SR[0:32, :], ALU.mult))
                    nsr = vop("tensor_copy", (SR2[32:64, :], SR[32:64, :]),
                              inc=c_va)
                    b0.free = (c_va, nsr)
                    u["nsr"] = nsr

                ln2s = {}

                def ln2_head():
                    ln2s["st"] = emit_ln_bcast_head(u, None)

                def ln2_pair(op):
                    def f():
                        emit_ln_apply_pair(ln2s["st"], op)
                        if op == 6:
                            u["naps"] = ln2s["st"]["naps"]
                    return f

                for oc in range(8):
                    fl.append(r_one(oc))
                for op in range(0, 8, 2):
                    fl.append(k_pair(op))
                for op in range(0, 8, 2):
                    fl.append(v_pair(op))
                fl.append(ln2_stats)
                fl.append(ln2_head)
                for op in range(0, 8, 2):
                    fl.append(ln2_pair(op))
                return fl

            def emit_ln2_tail(u, seq_, t_, unext):
                naps = u["naps"]
                xb_free[0] = u["ln2_stats"]
                if unext is not None:
                    wait("s", c_va, u["y2cast"])
                    unext["nxf"] = dma(
                        "s",
                        bass.AP(Y1, 4, [[NCH * 516, 128], [516, 8], [1, QT]]),
                        bass.AP(xT, unext["seq"] * D * S + unext["t"] * QT,
                                [[S, 128], [128 * S, 8], [1, QT]]), c_xf)
                for i, op in enumerate(range(0, 8, 2)):
                    wait("a", c_va, naps[i])
                    out_done[0] = dma(
                        "a",
                        bass.AP(outT, seq_ * D * S + op * 128 * S + t_ * QT,
                                [[S, 128], [128 * S, 2], [1, QT]]),
                        bass.AP(OUTB, op * QT,
                                [[NCH * QT, 128], [QT, 2], [1, QT]]),
                        c_out)

            def q_group(t, oc, qev, u):
                xsrc = RG if t == 0 else XRA
                xbch = [xsrc[:, k * QT:(k + 1) * QT] for k in range(8)]
                if oc == 0:
                    wait("t", c_xb, u["nxb"])
                    wait("t", c_wqr, n_wq_res)

                def qevac(b):
                    if oc == 0:
                        wait("a", c_g)
                        nv = act(QTB[:, 0:QT], b.ap[:, :], AF.Copy, inc=c_ar)
                        b.free = (c_ar, nv)
                        qev[oc] = (c_ar, nv)
                    else:
                        wait("v", c_g)
                        nv = vop("tensor_copy",
                                 (QTB[:, oc * QT:(oc + 1) * QT], b.ap[:, :]),
                                 inc=c_vg)
                        b.free = (c_vg, nv)
                        qev[oc] = (c_vg, nv)
                proj_group(WS[2], oc, xbch, qevac)

            def emit_xb(t, u_prev, u):
                gate = u_prev["aba_free"] if u_prev is not None else xb_free[0]
                wait("g", c_g, gate)
                u["nxb"] = dma("g",
                               bass.AP(ABA, 0, [[NCH * QT, 128], [QT, 8], [1, QT]]),
                               bass.AP(xT, seq * D * S + t * QT,
                                       [[S, 128], [128 * S, 8], [1, QT]]), c_xb)

            def emit_attention(t, fillers, qev):
                att = {"den_q": [], "epi": None}
                use4[0] = False
                ao_base = c_ao.n
                pend = []
                step = [0]
                last_inj = [-2]

                def emit_scores_exp(hp, kt):
                    g = gkt[0]
                    sps = SA if (g % 2 == 0) else SB2
                    if kt == 0 and qev.get(hp):
                        wait("t", qev[hp][0], qev[hp][1])
                    if g - 2 in exp_hist:
                        wait("t", c_p, exp_hist[g - 2])
                    qa = QTB[0:64, hp * QT:(hp + 1) * QT]
                    qb = QTB[64:128, hp * QT:(hp + 1) * QT]
                    ka = KTB[0:64, hp * S + kt * 128: hp * S + (kt + 1) * 128]
                    kb = KTB[64:128, hp * S + kt * 128: hp * S + (kt + 1) * 128]
                    mm(sps[:, 0:512], ka, qa, True, True, tp=(0, 0))
                    nsc = mm(sps[:, 512:1024], kb, qb, True, True, tp=(64, 0),
                             inc=c_sc)
                    pbs = PB[:, (g % PBS) * 1024:(g % PBS) * 1024 + 1024]
                    wait("a", c_sc, nsc)
                    if g - PBS >= 0:
                        wait("a", c_kt, g - PBS + 1)
                    npp = act(pbs, sps[:, 0:1024], AF.Exp, scale=SCALE, inc=c_p)
                    exp_hist[g] = npp
                    gkt[0] += 1
                    return g

                def emit_den(hp_, kt_, g_):
                    pbs_ = PB[:, (g_ % PBS) * 1024:(g_ % PBS) * 1024 + 1024]
                    mm(DEN[0:32, :], CSTB[:, 0:32], pbs_[:, 0:512], kt_ == 0,
                       kt_ == NKT - 1, tp=(0, 0))
                    mm(DEN[32:64, :], CSTB[:, 0:32], pbs_[:, 512:1024],
                       kt_ == 0, kt_ == NKT - 1, tp=(0, 32), inc=c_kt)

                def emit_epi():
                    nrc, nrb_res, b = att["epi"]
                    att["epi"] = None
                    wait("t", c_rc, nrc)
                    nrb = mm(b.ap[:, :], CSTF[0:64, 256:384], RCR[0:64, :],
                             True, True, inc=c_rb)
                    assert nrb == nrb_res

                def emit_pv(hp_, kt_, g_):
                    wait("t", c_p, exp_hist[g_])
                    if kt_ == 0:
                        acquire(bPV)
                    pbs_ = PB[:, (g_ % PBS) * 1024:(g_ % PBS) * 1024 + 1024]
                    va = VTB[:, kt_ * D + (2 * hp_) * 64:
                             kt_ * D + (2 * hp_) * 64 + 64]
                    vb = VTB[:, kt_ * D + (2 * hp_ + 1) * 64:
                             kt_ * D + (2 * hp_ + 1) * 64 + 64]
                    mm(PV[0:64, :], va, pbs_[:, 0:512], kt_ == 0, kt_ == NKT - 1,
                       tp=(0, 0))
                    mm(PV[64:128, :], vb, pbs_[:, 512:1024], kt_ == 0,
                       kt_ == NKT - 1, tp=(0, 64))

                def flush_pv():
                    hp, kt, g = pend.pop(0)
                    if att["epi"] is not None:
                        if att["epi_cd"] == 0:
                            emit_epi()
                        else:
                            att["epi_cd"] -= 1
                    if kt < 2:
                        att["den_q"].append((hp, kt, g))
                        return
                    if kt == 2:
                        dq = att["den_q"]
                        if dq:
                            for hp_, kt_, g_ in dq:
                                emit_pv(hp_, kt_, g_)
                            acquire(bDEN)
                            while dq:
                                emit_den(*dq.pop(0))
                    emit_pv(hp, kt, g)
                    emit_den(hp, kt, g)
                    if kt == NKT - 1:
                        wait("a", c_kt, c_kt.n)
                        achain_start()
                        if last_nrb[0]:
                            wait("a", c_rb, last_nrb[0])
                        nln = act(LTA[0:64, 0:512], DEN[0:64, :], AF.Ln,
                                  inc=c_ar)
                        wait("a", c_ar, nln)
                        nrc = act(RCR[0:64, :], LTA[0:64, 0:512], AF.Exp,
                                  scale=-1.0, inc=c_rc)
                        last_achain[0] = (c_rc, nrc)
                        bDEN.free = (c_ar, nln)
                        b = next_bank()
                        nrb_res = c_rb.n + 1
                        last_nrb[0] = nrb_res
                        att["epi"] = (nrc, nrb_res, b)
                        att["epi_cd"] = 1
                        wait("a", c_rb, nrb_res)
                        ncp = act(RSB[:, :], b.ap[:, :], AF.Copy, inc=c_ar)
                        b.free = (c_ar, ncp)
                        wait("v", c_ar, ncp)
                        nao = vop("tensor_tensor",
                                  (ABA[:, hp * QT:(hp + 1) * QT], PV[:, :],
                                   RSB[:, :], ALU.mult), inc=c_ao)
                        bPV.free = (c_ao, nao)

                for hp in range(NHP):
                    for kt in range(NKT):
                        g = emit_scores_exp(hp, kt)
                        pend.append((hp, kt, g))
                        if len(pend) >= 3:
                            flush_pv()
                        step[0] += 1
                        if (fillers and fillers[0][0] <= step[0]
                                and step[0] - last_inj[0] >= 2
                                and (att["epi"] is None
                                     or (len(fillers[0]) > 2 and fillers[0][2]))):
                            use4[0] = False
                            fillers.pop(0)[1]()
                            last_inj[0] = step[0]
                while pend:
                    flush_pv()
                if att["epi"] is not None:
                    emit_epi()
                while fillers:
                    fillers.pop(0)[1]()
                return ao_base + 8

            for t in range(NT):
                u = {"wo_si": None, "t": t, "seq": seq}
                fillers = []
                qev = {}

                def mk_q(oc, _t=t, _u=u, _qev=qev):
                    def f():
                        q_group(_t, oc, _qev, _u)
                    return f

                if prev is None:
                    wait("s", c_va, n_y1_init)
                    u["nxf"] = dma(
                        "s",
                        bass.AP(Y1, 4, [[NCH * 516, 128], [516, 8], [1, QT]]),
                        bass.AP(xT, seq * D * S + t * QT,
                                [[S, 128], [128 * S, 8], [1, QT]]), c_xf)
                if prev is not None and t > 0:
                    emit_wo_pairs(prev)
                u["nxb"] = kv_nxb0[0] if t == 0 else xra_nxb[0]
                use4[0] = True
                q_group(t, 0, qev, u)
                for oc in range(1, 8):
                    fillers.append((2 * oc - 1, mk_q(oc)))

                if prev is not None:
                    def u_loads(p=prev):
                        p["fr_si"], p["fr_n"] = load_weight(FR)
                        p["fk_si"], p["fk_n"] = load_weight(FK)
                    fillers.append((15, u_loads, True))
                    if t > 0:
                        def u_stats(p=prev):
                            emit_ln1_head(p)
                        lnst = {}
                        def u_bhead(p=prev, s=lnst):
                            s["st"] = emit_ln_bcast_head(p, Y1)
                        def mk_ap(op, s=lnst):
                            return lambda: emit_ln_apply_pair(s["st"], op)
                        def mk_mx(op, p=prev):
                            return lambda: emit_mix_pair(p, op)
                        fillers.append((17, u_stats))
                        fillers.append((19, u_bhead))
                        for i, op in enumerate(range(0, 8, 2)):
                            fillers.append((21 + 4 * i, mk_ap(op), True))
                            fillers.append((23 + 4 * i, mk_mx(op), True))
                        steps = [50 + 3 * i for i in range(17)]
                        steps += [100 + 2 * i for i in range(5)]
                    else:
                        steps = [24 + 3 * i for i in range(17)]
                        steps += [75 + 2 * i for i in range(5)]
                    mkfl = make_fillers(prev, load_wo=True)
                    for i, (st, f) in enumerate(zip(steps, mkfl)):
                        fillers.append((st, f, i >= len(mkfl) - 4))

                u["ao_done"] = emit_attention(t, fillers, qev)
                if prev is not None:
                    u["wo_si"], u["wo_n"] = prev["wo2_si"], prev["wo2_n"]
                    emit_ln2_tail(prev, prev["seq"], prev["t"], u)
                else:
                    wo_si0, wo_n0 = load_weight(WO)
                    u["wo_si"], u["wo_n"] = wo_si0, wo_n0
                if t < NT - 1 or seq + 1 < NSEQ:
                    src_off = (seq * D * S + (t + 1) * QT if t < NT - 1
                               else (seq + 1) * D * S + 1 * QT)
                    wait("g", c_xb, c_xb.n)
                    if prev is not None:
                        wait("g", c_g, prev["fv_last"])
                    xra_nxb[0] = dma(
                        "g",
                        bass.AP(XRA, 0, [[NCH * QT, 128], [QT, 8], [1, QT]]),
                        bass.AP(xT, src_off,
                                [[S, 128], [128 * S, 8], [1, QT]]), c_xb)
                prev = u

        emit_wo_pairs(prev)
        last_ycast[0] = prev["yadds"]
        prev["fr_si"], prev["fr_n"] = load_weight(FR)
        prev["fk_si"], prev["fk_n"] = load_weight(FK)
        emit_ln1_head(prev)
        emit_ln_bcast_apply(prev, Y1)
        emit_mix(prev)
        for f in make_fillers(prev, load_wo=False):
            f()
        emit_ln2_tail(prev, prev["seq"], prev["t"], None)

        wait("s", c_out, c_out.n)

        with nc.Block() as block:
            def runner(key):
                def _run(e):
                    for f in plan[key]:
                        f(e)
                return _run
            block.tensor(runner("t"))
            block.scalar(runner("a"))
            block.vector(runner("v"))
            block.sync(runner("s"))
            block.gpsimd(runner("g"))

    return nc


_NC_CACHE = [None]


def kernel(L0_attn_x, L0_attn_kernel, L0_attn_WO, L0_ffn_W_r, L0_ffn_W_k,
           L0_ffn_W_v, L0_ffn_mu_r, L0_ffn_mu_k):
    x = np.asarray(L0_attn_x, dtype=np.float32)
    ker = np.asarray(L0_attn_kernel, dtype=np.float32)
    wo = np.asarray(L0_attn_WO, dtype=np.float32)[0]
    wr = np.asarray(L0_ffn_W_r, dtype=np.float32)
    wk = np.asarray(L0_ffn_W_k, dtype=np.float32)
    wv = np.asarray(L0_ffn_W_v, dtype=np.float32)
    mur = np.asarray(L0_ffn_mu_r, dtype=np.float32)
    muk = np.asarray(L0_ffn_mu_k, dtype=np.float32)

    B = x.shape[0]
    ncore = 8
    wq_l = np.ascontiguousarray(ker[0].transpose(1, 0, 2).reshape(D, D))
    wk_l = np.ascontiguousarray(ker[1].transpose(1, 0, 2).reshape(D, D))
    wv_l = np.ascontiguousarray(ker[2].transpose(1, 0, 2).reshape(D, D))
    wo_l = np.ascontiguousarray(wo.reshape(64, 16, D).transpose(1, 0, 2).reshape(D, D))
    wts = np.stack([wk_l, wv_l, wq_l, wo_l,
                    np.ascontiguousarray(wr.T),
                    np.ascontiguousarray(wk.T),
                    np.ascontiguousarray(wv.T)], axis=0)

    mus = np.zeros((128, 18), np.float32)
    for c in range(8):
        mus[:, c] = mur[c * 128:(c + 1) * 128]
        mus[:, 8 + c] = muk[c * 128:(c + 1) * 128]
    mus[:, 16] = EPS
    mus[:, 17] = 1.0

    cst = np.zeros((128, 416), np.float32)
    cst[:, 0:32] = 1.0
    cst[0, 32:160] = 1.0
    cst[0, 160:288] = -1.0
    cst[32, 160:288] = 1.0
    cst[0, 288:352] = 1.0
    cst[32, 352:416] = 1.0

    xT = np.ascontiguousarray(x.transpose(0, 2, 1))

    if _NC_CACHE[0] is None:
        _NC_CACHE[0] = build_nc()
    nc = _NC_CACHE[0]

    in_maps = [{"xT": np.ascontiguousarray(xT[2 * c:2 * c + 2]),
                "wts": wts, "mus": mus, "cst": cst} for c in range(ncore)]
    res = run_bass_kernel_spmd(nc, in_maps, core_ids=list(range(ncore)))
    out = np.empty((B, S, D), np.float32)
    for c in range(ncore):
        o = res.results[c]["outT"]
        out[2 * c] = o[0].T
        out[2 * c + 1] = o[1].T
    return out
